# version 23 (final)
# baseline (speedup 1.0000x reference)
import numpy as np
import concourse.bass as bass
import concourse.mybir as mybir
from concourse.bass_utils import run_bass_kernel_spmd
from contextlib import ExitStack

F32 = mybir.dt.float32
BF16 = mybir.dt.bfloat16
I32 = mybir.dt.int32
AF = mybir.ActivationFunctionType
ALU = mybir.AluOpType
AX = mybir.AxisListType

NCORE = 8
B = 2
T = 2048
D = 1024
NT = B * T
E = 16
EM = 12
CAP = 4096
CLOC = 640
DFF = 2752
FFN = 4096
NH = 16
HD = 64
NS = 16
NM = 3
ROW_ZERO = EM * CLOC
ROW_CV = ROW_ZERO + 1
ROW_XN = ROW_ZERO + 3
NROWS = ROW_XN + NT
SCH = 256
NSCH = T // SCH
RUNL = 260
EPSI = 1e-6
EPSM = 1e-5

TAPS = {}


class Sem:
    __slots__ = ("name", "tot", "h")

    def __init__(self, name):
        self.name = name
        self.tot = 0
        self.h = None


class Bld:
    ENGS = ("TEN", "VEC", "ACT", "POOL", "SP")

    def __init__(self):
        self.ops = {e: [] for e in self.ENGS}
        self.done = {e: Sem("dn" + e) for e in self.ENGS}
        self.dma = {e: Sem("dm" + e) for e in ("SP", "POOL", "ACT")}
        self.bar = Sem("bar")
        self.nbar = 0

    def all_sems(self):
        return list(self.done.values()) + list(self.dma.values()) + [self.bar]

    @staticmethod
    def _dedup(deps):
        m = {}
        for item in deps:
            if item is None:
                continue
            s, v = item
            if s is None or v <= 0:
                continue
            if id(s) not in m or m[id(s)][1] < v:
                m[id(s)] = (s, v)
        return tuple(m.values())

    def g(self, eng, emitfn, deps=()):
        d = self.done[eng]
        d.tot += 1
        deps = self._dedup(deps)

        def f(e, deps=deps, emitfn=emitfn, d=d):
            for s, v in deps:
                e.wait_ge(s.h, v)
            emitfn(e).then_inc(d.h, 1)
        self.ops[eng].append(f)
        return (d, d.tot)

    def dmas(self, eng, emitfn, n, deps=()):
        ds = self.dma[eng]
        ds.tot += 16 * n
        deps = self._dedup(deps)

        def f(e, deps=deps, emitfn=emitfn, ds=ds):
            for s, v in deps:
                e.wait_ge(s.h, v)
            emitfn(e, ds.h)
        self.ops[eng].append(f)
        return (ds, ds.tot)

    def now(self, eng):
        return (self.done[eng], self.done[eng].tot)

    def dma_now(self, eng):
        return (self.dma[eng], self.dma[eng].tot)

    def barrier(self):
        self.nbar += 1
        target = 5 * self.nbar
        waits = [(s, s.tot) for s in
                 [self.dma[e] for e in ("SP", "POOL", "ACT")]
                 + [self.done[e] for e in self.ENGS] if s.tot > 0]
        bar = self.bar
        for eng in self.ENGS:
            def f(e, waits=tuple(waits), bar=bar, target=target):
                for s, v in waits:
                    e.wait_ge(s.h, v)
                e.nop().then_inc(bar.h, 1)
                e.wait_ge(bar.h, target)
            self.ops[eng].append(f)


class Banks:
    def __init__(self):
        self.t = [None] * 8
        self.war = [None] * 8

    def dep(self, i):
        return [self.war[i]] if self.war[i] is not None else []

    def consumed(self, i, dep):
        self.war[i] = dep


def kernel(**inp):
    x_full = np.ascontiguousarray(inp["L0_attn_x"], dtype=np.float32)
    fc1_t = np.ascontiguousarray(np.transpose(
        np.asarray(inp["L0_ffn_fc1_weight"], np.float32), (0, 2, 1)))
    fc2_t = np.ascontiguousarray(np.transpose(
        np.asarray(inp["L0_ffn_fc2_weight"], np.float32), (0, 2, 1)))
    wr_t = np.ascontiguousarray(
        np.asarray(inp["L0_ffn_router_weight"], np.float32).T)
    common = {
        "attnw": np.asarray(inp["L0_attn_attn_norm_w"], np.float32),
        "ffnw": np.asarray(inp["L0_attn_ffn_norm_w"], np.float32),
        "Wq": np.asarray(inp["L0_attn_Wq"], np.float32)[0],
        "Wk": np.asarray(inp["L0_attn_Wk"], np.float32)[0],
        "Wv": np.asarray(inp["L0_attn_Wv"], np.float32)[0],
        "Wo": np.asarray(inp["L0_attn_Wo"], np.float32)[0],
        "Wg": np.asarray(inp["L0_attn_Wg"], np.float32)[0],
        "Wu": np.asarray(inp["L0_attn_Wu"], np.float32)[0],
        "Wd": np.asarray(inp["L0_attn_Wd"], np.float32)[0],
        "mnw": np.asarray(inp["L0_attn_m_norm_w"], np.float32),
        "Win": np.asarray(inp["L0_attn_Win"], np.float32),
        "convw": np.asarray(inp["L0_attn_conv_w"], np.float32),
        "convb": np.asarray(inp["L0_attn_conv_b"], np.float32),
        "Wx": np.asarray(inp["L0_attn_Wx"], np.float32),
        "Alog": np.asarray(inp["L0_attn_A_log"], np.float32),
        "Dp": np.asarray(inp["L0_attn_Dp"], np.float32),
        "Wout": np.asarray(inp["L0_attn_Wout"], np.float32),
        "nw": np.asarray(inp["L0_ffn_norm_weight"], np.float32),
        "WrT": wr_t,
        "rb": np.asarray(inp["L0_ffn_router_bias"], np.float32).reshape(1, E),
        "fc1T": fc1_t,
        "fc1b": np.asarray(inp["L0_ffn_fc1_bias"], np.float32),
        "fc2T": fc2_t,
        "fc2b": np.asarray(inp["L0_ffn_fc2_bias"], np.float32),
        "cv": np.asarray(inp["L0_ffn_constant_vectors"], np.float32),
    }
    common = {k: np.ascontiguousarray(v) for k, v in common.items()}

    nc = build_program()
    in_maps = []
    for c in range(NCORE):
        m = dict(common)
        m["x_in"] = np.ascontiguousarray(x_full[2 * c:2 * c + 2])
        pf = np.zeros((8, 1), np.float32)
        pf[:c] = 1.0
        m["corepfx"] = pf
        in_maps.append(m)
    res = run_bass_kernel_spmd(nc, in_maps, core_ids=list(range(NCORE)))
    out = np.concatenate(
        [np.asarray(res.results[c]["out"]) for c in range(NCORE)], axis=0)
    if TAPS:
        kernel.last_taps = [dict(res.results[c]) for c in range(NCORE)]
    return np.ascontiguousarray(out.astype(np.float32).reshape(16, T, D))


class G:
    pass


ID_B, TRILX, TRIA, ONES, MSK4 = 0, 128, 256, 384, 512
PA_ATTNW, PA_FFNW, PA_MNW, PA_CONVB, PA_DP = 0, 8, 16, 40, 64
PA_ANEG, PA_IOTA16, PA_IOTA16B, PA_NW = 88, 472, 488, 504
PA_CONVW, PA_RB, PA_CPFX, PA_ONESF = 512, 608, 624, 626
PA_CNTALL, PA_IOTAP, PA_WR = 640, 656, 672
PA_CONSTS = 658
CONST_VALS = (1e-5, 1e-6, -0.5, -1.0, 0.125, 1.0 / 1024.0)


def build_program():
    nc = bass.Bass("TRN2", target_bir_lowering=False)
    g = G()
    g.nc = nc
    A = lambda t: t.ap()

    g.x_in = A(nc.dram_tensor("x_in", [B, T, D], F32, kind="ExternalInput"))
    for nm, shp in [("attnw", [1, D]), ("ffnw", [1, D]), ("Wq", [D, D]),
                    ("Wk", [D, 256]), ("Wv", [D, 256]), ("Wo", [D, D]),
                    ("Wg", [D, DFF]), ("Wu", [D, DFF]), ("Wd", [DFF, D]),
                    ("mnw", [NM, D]), ("Win", [NM, D, 2 * D]),
                    ("convw", [NM, D, 4]), ("convb", [NM, D]),
                    ("Wx", [NM, D, D + 32]), ("Alog", [NM, D, NS]),
                    ("Dp", [NM, D]), ("Wout", [NM, D, D]), ("nw", [D]),
                    ("WrT", [D, E]), ("rb", [1, E]),
                    ("fc1T", [EM, D, FFN]), ("fc1b", [EM, FFN]),
                    ("fc2T", [EM, FFN, D]), ("fc2b", [EM, D]),
                    ("cv", [2, D]), ("corepfx", [8, 1])]:
        setattr(g, nm, A(nc.dram_tensor(nm, shp, F32, kind="ExternalInput")))
    g.out = A(nc.dram_tensor("out", [B, T, D], F32, kind="ExternalOutput"))

    g.xs_dram = A(nc.dram_tensor("xs_dram", [B, D, T], F32, kind="Internal"))
    g.sfin_tm = A(nc.dram_tensor("sfin_tm", [B, T, D], F32, kind="Internal"))
    g.z_dram = A(nc.dram_tensor("z_dram", [B, D, T], BF16, kind="Internal"))
    g.rows_dram = A(nc.dram_tensor("rows_dram", [NROWS, D], BF16,
                                   kind="Internal"))
    g.disp_dram = A(nc.dram_tensor("disp_dram", [EM * CLOC + 1, D], BF16,
                                   kind="Internal"))
    g.bounce = A(nc.dram_tensor("bounce", [128, E], F32, kind="Internal"))
    g.proj_dram = A(nc.dram_tensor("proj_dram", [1024, T], BF16,
                                   kind="Internal"))
    g.cc_in = A(nc.dram_tensor("cc_in", [1, E], F32, kind="Internal"))
    g.cc_out = A(nc.dram_tensor("cc_out", [8, E], F32, kind="Internal"))

    g.taps = {}
    for name, (shape, dt) in TAPS.items():
        g.taps[name] = A(nc.dram_tensor("tap_" + name, list(shape), dt,
                                        kind="ExternalOutput"))

    bld = Bld()
    g.bld = bld
    es = ExitStack()
    with es:
        sb = lambda n, s, d: es.enter_context(nc.sbuf_tensor(n, s, d))
        g.xb = sb("xb", [128, 16384], BF16)
        g.bufA = sb("bufA", [128, 24832], BF16)
        g.bufB = sb("bufB", [128, 16384], BF16)
        g.bufC = sb("bufC", [128, 22528], BF16)
        g.wbuf = sb("wbuf", [128, 8192], BF16)
        g.statr = sb("statr", [128, 2048], F32)
        g.rowsc = sb("rowsc", [128, 1024], F32)
        g.xtmp = sb("xtmp", [128, 1024], F32)
        g.meta = sb("meta", [128, 1280], F32)
        g.rmeta = sb("rmeta", [128, 512], F32)
        g.metai = sb("metai", [128, 256], I32)
        g.consts = sb("consts", [128, 512], BF16)
        g.params = sb("params", [128, 1664], F32)
        g.idf = sb("idf", [128, 128], F32)

        pk = Banks()
        for i in range(6):
            pk.t[i] = es.enter_context(
                nc.psum_tensor("pb%d" % i, [128, 512], F32))
        for i in range(6, 8):
            pk.t[i] = es.enter_context(
                nc.psum_tensor("pb%d" % i, [128, 1024], BF16))
        g.pk = pk

        for s in bld.all_sems():
            s.h = es.enter_context(nc.semaphore(s.name))
        blk = es.enter_context(nc.Block())

        build_all(g)

        def fin(e):
            for s in (bld.dma["SP"], bld.dma["POOL"], bld.dma["ACT"]):
                if s.tot:
                    e.wait_ge(s.h, s.tot)
            for en in Bld.ENGS:
                if en != "SP" and bld.done[en].tot:
                    e.wait_ge(bld.done[en].h, bld.done[en].tot)
        bld.ops["SP"].append(fin)

        @blk.tensor
        def _(eng):
            for f in bld.ops["TEN"]:
                f(eng)

        @blk.vector
        def _(eng):
            for f in bld.ops["VEC"]:
                f(eng)

        @blk.scalar
        def _(eng):
            for f in bld.ops["ACT"]:
                f(eng)

        @blk.gpsimd
        def _(eng):
            for f in bld.ops["POOL"]:
                f(eng)

        @blk.sync
        def _(eng):
            for f in bld.ops["SP"]:
                f(eng)
    return nc


import os as _os
KPHASES = int(_os.environ.get("KPHASES", "99"))


def _final_taps(g):
    if "xsd" in g.taps:
        g.bld.dmas("SP", lambda e, ds: e.dma_start(
            out=g.taps["xsd"], in_=g.xs_dram).then_inc(ds, 16), 1,
            deps=[g.bld.now("VEC"), g.bld.now("ACT"), g.bld.now("TEN"),
                  g.bld.dma_now("SP")])
    if "sfin" in g.taps:
        g.bld.dmas("SP", lambda e, ds: e.dma_start(
            out=g.taps["sfin"], in_=g.sfin_tm).then_inc(ds, 16), 1,
            deps=[g.bld.now("VEC"), g.bld.now("TEN"), g.bld.dma_now("SP")])
    if "rowsall" in g.taps:
        g.bld.dmas("SP", lambda e, ds: e.dma_start(
            out=g.taps["rowsall"], in_=g.rows_dram).then_inc(ds, 16), 1,
            deps=[g.bld.now("VEC"), g.bld.dma_now("SP"),
                  g.bld.dma_now("POOL")])


def build_all(g):
    def cinit(e):
        for i, v in enumerate(CONST_VALS):
            last = e.memset(g.params[:, PA_CONSTS + i:PA_CONSTS + i + 1],
                            float(v))
        return last
    g.d_cinit = g.bld.g("VEC", cinit)
    for i, v in enumerate(CONST_VALS):
        g.nc.const_aps.register(
            F32, float(v), g.params[:, PA_CONSTS + i:PA_CONSTS + i + 1])
    g.bld.g("ACT", lambda e: e.activation(
        g.params[:, PA_CONSTS:PA_CONSTS + 1],
        g.params[:, PA_CONSTS:PA_CONSTS + 1], AF.Copy), deps=[g.d_cinit])
    prep(g)
    if KPHASES < 2:
        return _final_taps(g)
    phaseR(g)
    g.bld.barrier()
    if KPHASES < 3:
        return _final_taps(g)
    inner_attn(g)
    g.bld.barrier()
    if KPHASES < 4:
        return _final_taps(g)
    inner_ffn(g)
    g.bld.barrier()
    if KPHASES < 5:
        return _final_taps(g)
    for li in range(NM):
        inner_mamba(g, li)
        g.bld.barrier()
        if KPHASES < 5 + li + 1:
            return _final_taps(g)
    phaseT(g)
    g.bld.barrier()
    if KPHASES < 9:
        return _final_taps(g)
    moe_experts(g)
    g.bld.barrier()
    if KPHASES < 10:
        return _final_taps(g)
    combine(g)
    _final_taps(g)


def chain(g, eng, fns, deps):
    d = None
    for i, fn in enumerate(fns):
        d = g.bld.g(eng, fn, deps=(list(deps) if i == 0 else [d]))
    return d


def tap_store(g, name, src_ap, deps):
    if name not in g.taps:
        return None
    dst = g.taps[name]

    def f(e, ds, dst=dst, src=src_ap):
        e.dma_start(out=dst, in_=src).then_inc(ds, 16)
    return g.bld.dmas("SP", f, 1, deps=deps)


def bcast_row_bf(g, src_row, dst, bank, deps, nparts=128):
    pk, bld = g.pk, g.bld
    W = src_row.shape[-1]
    pt = pk.t[bank]
    dmm = bld.g("TEN", lambda e, s=src_row, pt=pt, n=nparts, W=W: e.matmul(
        pt[0:n, 0:W], g.consts[0:1, ONES:ONES + n], s, start=True, stop=True),
        deps=list(deps) + pk.dep(bank) + [g.d_masks])
    dev = bld.g("VEC", lambda e, pt=pt, d=dst, n=nparts, W=W: e.tensor_copy(
        d, pt[0:n, 0:W]), deps=[dmm])
    pk.consumed(bank, dev)
    return dev


def bcast_row_f32(g, src_row, dst, bank, deps, nparts=128):
    pk, bld = g.pk, g.bld
    W = src_row.shape[-1]
    pt = pk.t[bank]
    bp = src_row.base_partition()
    ones = g.params[bp:bp + 1, PA_ONESF:PA_ONESF + 1].to_broadcast([1, nparts])
    dmm = bld.g("TEN", lambda e, s=src_row, pt=pt, n=nparts, W=W, o=ones:
                e.matmul(pt[0:n, 0:W], o, s, start=True, stop=True),
                deps=list(deps) + pk.dep(bank) + [g.d_onesf])
    dev = bld.g("VEC", lambda e, pt=pt, d=dst, n=nparts, W=W: e.tensor_copy(
        d, pt[0:n, 0:W]), deps=[dmm])
    pk.consumed(bank, dev)
    return dev


def prep(g):
    bld, C, P = g.bld, g.consts, g.params

    def masks(e):
        e.memset(C[:, ID_B:ID_B + 128], 0.0)
        e.affine_select(out=C[:, ID_B:ID_B + 128], in_=C[:, ID_B:ID_B + 128],
                        compare_op=ALU.not_equal, fill=1.0, base=0,
                        pattern=[[-1, 128]], channel_multiplier=1)
        e.memset(C[:, TRILX:TRILX + 128], 1.0)
        e.affine_select(out=C[:, TRILX:TRILX + 128],
                        in_=C[:, TRILX:TRILX + 128],
                        compare_op=ALU.is_ge, fill=0.0, base=-1,
                        pattern=[[1, 128]], channel_multiplier=-1)
        e.memset(C[:, ONES:ONES + 128], 1.0)
        e.memset(g.idf[:, :], 0.0)
        e.affine_select(out=g.idf[:, :], in_=g.idf[:, :],
                        compare_op=ALU.not_equal, fill=1.0, base=0,
                        pattern=[[-1, 128]], channel_multiplier=1)
        e.iota(P[:, PA_IOTA16:PA_IOTA16 + 16], pattern=[[1, 16]],
               base=0, channel_multiplier=0,
               allow_small_or_imprecise_dtypes=True)
        e.iota(P[:, PA_IOTA16B:PA_IOTA16B + 16], pattern=[[1, 16]],
               base=1000000, channel_multiplier=0,
               allow_small_or_imprecise_dtypes=True)
        return e.iota(P[:, PA_IOTAP:PA_IOTAP + 1], pattern=[[1, 1]],
                      base=0, channel_multiplier=1,
                      allow_small_or_imprecise_dtypes=True)
    g.d_masks = bld.g("POOL", masks)
    g.d_onesf = bld.g("VEC", lambda e: e.memset(
        P[:, PA_ONESF:PA_ONESF + 1], 1.0))

    def amasks(e):
        for i in range(4):
            off = i * 128
            e.memset(g.bufC[:, 16448 + i * 512:16448 + (i + 1) * 512], 1.0)
            last = e.affine_select(
                out=g.bufC[:, 16448 + i * 512:16448 + (i + 1) * 512],
                in_=g.bufC[:, 16448 + i * 512:16448 + (i + 1) * 512],
                compare_op=ALU.is_ge, fill=0.0, base=-off,
                pattern=[[1, 512]], channel_multiplier=-1)
        return last
    g.d_amasks = bld.g("POOL", amasks)

    def tload(src_ap, rows, dst_col, extra=()):
        dl = bld.dmas("SP", lambda e, ds, s=src_ap, r=rows: e.dma_start(
            out=g.xtmp[0:r, 0:128], in_=s).then_inc(ds, 16), 1,
            deps=list(extra))
        bank = 4
        dmm = bld.g("TEN", lambda e, r=rows: e.transpose(
            g.pk.t[bank][:, 0:r], g.xtmp[0:r, 0:128], g.idf[0:r, 0:r]),
            deps=[dl, g.d_masks] + g.pk.dep(bank))
        dev = bld.g("VEC", lambda e, r=rows, c=dst_col: e.tensor_copy(
            P[:, c:c + r], g.pk.t[bank][:, 0:r]), deps=[dmm])
        g.pk.consumed(bank, dev)
        return dev, dmm

    prev = ()
    loads = [(g.attnw.rearrange("o (r p) -> (o r) p", p=128), 8, PA_ATTNW),
             (g.ffnw.rearrange("o (r p) -> (o r) p", p=128), 8, PA_FFNW),
             (g.mnw.rearrange("l (r p) -> (l r) p", p=128), 24, PA_MNW),
             (g.convb.rearrange("l (r p) -> (l r) p", p=128), 24, PA_CONVB),
             (g.Dp.rearrange("l (r p) -> (l r) p", p=128), 24, PA_DP),
             (g.nw.rearrange("(r p) -> r p", p=128), 8, PA_NW)]
    pdeps = []
    for src, rows, col in loads:
        dev, dmm = tload(src, rows, col, extra=prev)
        prev = (dmm,)
        pdeps.append(dev)
    g.d_params = pdeps
    g.d_xtmp_free = prev

    def cwf(e, ds):
        for l in range(NM):
            for r in range(8):
                c = PA_CONVW + (l * 8 + r) * 4
                e.dma_start(out=P[:, c:c + 4],
                            in_=g.convw[l, r * 128:(r + 1) * 128, :]
                            ).then_inc(ds, 16)
    g.d_convw = bld.dmas("SP", cwf, 24)

    def alf(e, ds):
        for l in range(NM):
            for r in range(8):
                c = PA_ANEG + (l * 8 + r) * 16
                e.dma_start(out=P[:, c:c + 16],
                            in_=g.Alog[l, r * 128:(r + 1) * 128, :]
                            ).then_inc(ds, 16)
    dal = bld.dmas("SP", alf, 24)
    dae = bld.g("ACT", lambda e: e.activation(
        P[:, PA_ANEG:PA_ANEG + 384], P[:, PA_ANEG:PA_ANEG + 384], AF.Exp),
        deps=[dal])
    g.d_aneg = bld.g("VEC", lambda e: e.tensor_scalar_mul(
        P[:, PA_ANEG:PA_ANEG + 384], P[:, PA_ANEG:PA_ANEG + 384], -1.0),
        deps=[dae])

    drb = bld.dmas("SP", lambda e, ds: e.dma_start(
        out=g.rowsc[0:1, 0:E], in_=g.rb).then_inc(ds, 16), 1)
    g.d_rb = bcast_row_f32(g, g.rowsc[0:1, 0:E], P[:, PA_RB:PA_RB + E], 5,
                           deps=[drb])
    dz = bld.g("VEC", lambda e: (e.memset(P[:, PA_CPFX:PA_CPFX + 1], 0.0),
                                 e.memset(P[:, PA_CNTALL:PA_CNTALL + E], 0.0)
                                 )[-1])
    g.d_cpfx = bld.dmas("SP", lambda e, ds: e.dma_start(
        out=P[0:8, PA_CPFX:PA_CPFX + 1], in_=g.corepfx).then_inc(ds, 16), 1,
        deps=[dz])
    g.d_cntz = dz

    dzt = bld.g("VEC", lambda e: e.memset(g.bufB[:, 0:1024], 0.0))

    def zf(e, ds):
        for i in range(EM * CLOC // 128):
            e.dma_start(out=g.disp_dram[i * 128:(i + 1) * 128, :],
                        in_=g.bufB[:, 0:1024]).then_inc(ds, 16)
        e.dma_start(out=g.rows_dram[ROW_ZERO:ROW_ZERO + 1, :],
                    in_=g.bufB[0:1, 0:1024]).then_inc(ds, 16)
    g.d_dispzero = bld.dmas("SP", zf, EM * CLOC // 128 + 1, deps=[dzt])

    dcv = bld.dmas("SP", lambda e, ds: e.dma_start(
        out=g.rowsc[0:2, 0:D], in_=g.cv).then_inc(ds, 16), 1, deps=[g.d_rb])
    dcvc = bld.g("VEC", lambda e: e.tensor_copy(
        g.bufB[0:2, 1024:2048], g.rowsc[0:2, 0:D]), deps=[dcv, dzt])
    g.d_cvrows = bld.dmas("SP", lambda e, ds: e.dma_start(
        out=g.rows_dram[ROW_CV:ROW_CV + 2, :],
        in_=g.bufB[0:2, 1024:2048]).then_inc(ds, 16), 1, deps=[dcvc])

    def wrf(e, ds):
        e.dma_start(out=g.params[:, PA_WR:PA_WR + 128].rearrange(
            "p (c n) -> p c n", c=8),
            in_=g.WrT.rearrange("(c p) n -> p c n", p=128)).then_inc(ds, 16)
    dwr = bld.dmas("SP", wrf, 1)
    g.d_wr = bld.g("VEC", lambda e: e.tensor_tensor(
        g.params[:, PA_WR:PA_WR + 128].rearrange("p (c n) -> p c n", c=8),
        g.params[:, PA_WR:PA_WR + 128].rearrange("p (c n) -> p c n", c=8),
        g.params[:, PA_NW:PA_NW + 8].rearrange("p (c o) -> p c o", o=1)
        .to_broadcast([128, 8, 16]), ALU.mult), deps=[dwr, pdeps[5]])

    dnw = bld.dmas("SP", lambda e, ds: e.dma_start(
        out=g.statr[0:1, 0:D], in_=g.nw.rearrange("(o d) -> o d", o=1)
        ).then_inc(ds, 16), 1)
    dn1 = bcast_row_f32(g, g.statr[0:1, 0:512], g.bufA[:, 23552:24064], 5,
                        deps=[dnw])
    dn2 = bcast_row_f32(g, g.statr[0:1, 512:1024], g.bufA[:, 24064:24576], 5,
                        deps=[dnw])
    g.d_nwrow = [dn1, dn2]


def phaseR(g):
    bld, pk, C, P = g.bld, g.pk, g.consts, g.params
    OHF1, OHF2, CNTC, SQS, ROWS = 0, 1024, 2048, 3072, 4096
    prev_tail = [g.d_wr, g.d_cvrows, g.d_nwrow[0], g.d_nwrow[1],
                 g.d_xtmp_free[0] if g.d_xtmp_free else None]
    prev_tail = [d for d in prev_tail if d]
    g.tile_last = {}

    for b in range(B):
        for tt in range(16):
            ti = b * 16 + tt
            dl = bld.dmas("SP", lambda e, ds, b=b, tt=tt: e.dma_start(
                out=g.rowsc[:, :],
                in_=g.x_in[b, tt * 128:(tt + 1) * 128, :]).then_inc(ds, 16),
                1, deps=prev_tail)
            ds1 = bld.g("VEC", lambda e: e.reduce_sum(
                g.rmeta[:, 384:385], g.rowsc[:, :], AX.X), deps=[dl])
            ds2 = bld.g("ACT", lambda e: e.activation(
                g.bufC[:, SQS:SQS + 1024], g.rowsc[:, :], AF.Square,
                accum_out=g.rmeta[:, 385:386]), deps=[dl])

            dv = chain(g, "VEC", [
                lambda e: e.tensor_scalar_mul(
                    g.rmeta[:, 386:387], g.rmeta[:, 384:385], 1.0 / D),
                lambda e: e.tensor_tensor(
                    g.rmeta[:, 387:388], g.rmeta[:, 386:387],
                    g.rmeta[:, 386:387], ALU.mult),
                lambda e: e.scalar_tensor_tensor(
                    g.rmeta[:, 388:389], g.rmeta[:, 385:386], 1.0 / D,
                    g.rmeta[:, 387:388], ALU.mult, ALU.subtract),
            ], deps=[ds1, ds2])
            drsa = bld.g("ACT", lambda e: e.activation(
                g.rmeta[:, 389:390], g.rmeta[:, 388:389], AF.Ln, bias=EPSM),
                deps=[dv])
            drs = bld.g("ACT", lambda e: e.activation(
                g.rmeta[:, 390:391], g.rmeta[:, 389:390], AF.Exp,
                scale=-0.5), deps=[drsa])
            dxn = bld.g("VEC", lambda e: e.scalar_tensor_tensor(
                g.xtmp[:, :], g.rowsc[:, :], g.rmeta[:, 386:387],
                g.rmeta[:, 390:391].to_broadcast([128, D]),
                ALU.subtract, ALU.mult), deps=[drs, dl])
            dq = bld.g("ACT", lambda e: e.activation(
                g.bufC[:, SQS:SQS + 1024], g.xtmp[:, :], AF.Square,
                accum_out=g.rmeta[:, 391:392]), deps=[dxn, ds2])
            drs2a = bld.g("ACT", lambda e: e.activation(
                g.rmeta[:, 392:393], g.rmeta[:, 391:392], AF.Ln, bias=EPSM,
                scale=1.0 / D), deps=[dq])
            drs2 = bld.g("ACT", lambda e: e.activation(
                g.rmeta[:, 393:394], g.rmeta[:, 392:393], AF.Exp,
                scale=-0.5), deps=[drs2a])
            drow = bld.g("VEC", lambda e: e.scalar_tensor_tensor(
                g.bufC[:, ROWS:ROWS + 1024], g.xtmp[:, :],
                g.rmeta[:, 393:394], g.bufA[:, 23552:24576],
                ALU.mult, ALU.mult), deps=[drs2])
            dstrow = bld.dmas("SP", lambda e, ds, b=b, tt=tt: e.dma_start(
                out=g.rows_dram[ROW_XN + b * T + tt * 128:
                                ROW_XN + b * T + (tt + 1) * 128, :],
                in_=g.bufC[:, ROWS:ROWS + 1024]).then_inc(ds, 16), 1,
                deps=[drow])

            evdeps = []
            for j in range(8):
                bank = j % 2
                dtr = bld.g("TEN", lambda e, j=j, bank=bank: e.transpose(
                    pk.t[bank][:, 0:128], g.xtmp[:, j * 128:(j + 1) * 128],
                    g.idf[:, :]),
                    deps=[dxn, ds1, g.d_masks] + pk.dep(bank))
                dev = bld.g("VEC", lambda e, j=j, bank=bank: e.tensor_copy(
                    g.rowsc[:, j * 128:(j + 1) * 128], pk.t[bank][:, 0:128]),
                    deps=[dtr, dq, drow])
                pk.consumed(bank, dev)
                evdeps.append(dev)
            dst1 = bld.dmas("SP", lambda e, ds, b=b, tt=tt: [
                e.dma_start(out=g.xs_dram[b, j * 128:(j + 1) * 128,
                                          tt * 128:(tt + 1) * 128],
                            in_=g.rowsc[:, j * 128:(j + 1) * 128]
                            ).then_inc(ds, 16) for j in range(8)][-1], 8,
                deps=evdeps)

            dmm = None
            for j in range(8):
                dd = (evdeps + [g.d_wr] + pk.dep(2)) if j == 0 else []
                dmm = bld.g("TEN", lambda e, j=j: e.matmul(
                    pk.t[2][:, 0:E], g.rowsc[:, j * 128:(j + 1) * 128],
                    g.rowsc[:, 0:0] if False else
                    _wr_slice(g, j), start=(j == 0), stop=(j == 7)),
                    deps=dd)
            dlg = bld.g("VEC", lambda e, ti=ti: e.scalar_tensor_tensor(
                g.statr[:, ti * 16:(ti + 1) * 16], pk.t[2][:, 0:E],
                g.rmeta[:, 393:394], P[:, PA_RB:PA_RB + E],
                ALU.mult, ALU.add), deps=[dmm, g.d_rb])
            pk.consumed(2, dlg)

            dmx = bld.g("VEC", lambda e, ti=ti: e.reduce_max(
                g.rmeta[:, 394:395], g.statr[:, ti * 16:(ti + 1) * 16], AX.X,
                negate=True), deps=[dlg])
            dex = bld.g("ACT", lambda e, ti=ti: e.activation(
                g.rmeta[:, 400:416], g.statr[:, ti * 16:(ti + 1) * 16],
                AF.Exp, bias=g.rmeta[:, 394:395],
                accum_out=g.rmeta[:, 395:396]), deps=[dmx, drs2])
            dpr = chain(g, "VEC", [
                lambda e: e.reciprocal(g.rmeta[:, 396:397],
                                       g.rmeta[:, 395:396]),
                lambda e: e.tensor_scalar(g.rmeta[:, 400:416],
                                          g.rmeta[:, 400:416],
                                          g.rmeta[:, 396:397], None,
                                          ALU.mult),
            ], deps=[dex])

            pr = g.rmeta[:, 400:416]
            oh1a = g.bufC[:, OHF1 + ti * 16:OHF1 + ti * 16 + 16]
            oh2a = g.bufC[:, OHF2 + ti * 16:OHF2 + ti * 16 + 16]
            dtp = chain(g, "VEC", [
                lambda e, pr=pr: e.reduce_max(g.rmeta[:, 416:417], pr, AX.X),
                lambda e, pr=pr: e.tensor_tensor(
                    g.rmeta[:, 432:448], pr,
                    g.rmeta[:, 416:417].to_broadcast([128, E]), ALU.is_ge),
                lambda e: e.scalar_tensor_tensor(
                    g.rmeta[:, 448:464], g.rmeta[:, 432:448], -1000000.0,
                    P[:, PA_IOTA16B:PA_IOTA16B + E], ALU.mult, ALU.add),
                lambda e: e.tensor_reduce(
                    g.rmeta[:, 417:418], g.rmeta[:, 448:464], AX.X, ALU.min),
                lambda e, oh1a=oh1a: e.tensor_tensor(
                    oh1a, P[:, PA_IOTA16:PA_IOTA16 + E],
                    g.rmeta[:, 417:418].to_broadcast([128, E]), ALU.is_equal),
                lambda e, pr=pr, oh1a=oh1a: e.scalar_tensor_tensor(
                    g.rmeta[:, 448:464], oh1a, -1e30, pr, ALU.mult, ALU.add),
                lambda e: e.reduce_max(
                    g.rmeta[:, 418:419], g.rmeta[:, 448:464], AX.X),
                lambda e: e.tensor_tensor(
                    g.rmeta[:, 464:480], g.rmeta[:, 448:464],
                    g.rmeta[:, 418:419].to_broadcast([128, E]), ALU.is_ge),
                lambda e: e.scalar_tensor_tensor(
                    g.rmeta[:, 480:496], g.rmeta[:, 464:480], -1000000.0,
                    P[:, PA_IOTA16B:PA_IOTA16B + E], ALU.mult, ALU.add),
                lambda e: e.tensor_reduce(
                    g.rmeta[:, 419:420], g.rmeta[:, 480:496], AX.X, ALU.min),
                lambda e, oh2a=oh2a: e.tensor_tensor(
                    oh2a, P[:, PA_IOTA16:PA_IOTA16 + E],
                    g.rmeta[:, 419:420].to_broadcast([128, E]), ALU.is_equal),
                lambda e, ti=ti, oh1a=oh1a, oh2a=oh2a: e.tensor_add(
                    g.bufC[:, CNTC + ti * 16:CNTC + ti * 16 + 16],
                    oh1a, oh2a),
            ], deps=[dpr, g.d_masks])
            dgi = bld.g("VEC", lambda e, ti=ti: e.tensor_copy(
                g.statr[:, 1024 + ti * 16:1024 + ti * 16 + 4],
                g.rmeta[:, 416:420]), deps=[dtp])

            dtl = bld.g("TEN", lambda e, ti=ti: e.matmul(
                pk.t[3][:, 0:E], C[:, TRILX:TRILX + 128],
                g.bufC[:, CNTC + ti * 16:CNTC + ti * 16 + 16],
                start=True, stop=True), deps=[dtp] + pk.dep(3))
            dte = bld.g("VEC", lambda e, ti=ti: e.tensor_copy(
                g.meta[:, ti * 16:(ti + 1) * 16], pk.t[3][:, 0:E]),
                deps=[dtl])
            pk.consumed(3, dte)

            ddeps = [dtp]
            if ti == 0:
                ddeps += pk.dep(5) + [g.d_nwrow[0], g.d_nwrow[1], g.d_rb]
            dtt = bld.g("TEN", lambda e, ti=ti: e.matmul(
                pk.t[5][0:1, ti * 16:ti * 16 + E], C[:, ONES:ONES + 1],
                g.bufC[:, CNTC + ti * 16:CNTC + ti * 16 + 16],
                start=(ti == 0), stop=(ti == 31)), deps=ddeps)
            if ti == 31:
                g.d_tot1 = dtt
            g.tile_last[ti] = [dtp, dte, dgi, dstrow, dst1, dlg]
            prev_tail = [dte, dgi, dstrow, dst1, dmx, dex, dtt]

    g.r_tail = prev_tail
    if int(_os.environ.get("KPHR", "9")) < 2:
        return
    dt1 = bld.g("VEC", lambda e: e.tensor_copy(
        g.rowsc[0:1, 0:512], pk.t[5][0:1, 0:512]), deps=[g.d_tot1])
    pk.consumed(5, dt1)
    dt2 = dt1
    dtot = bld.g("VEC", lambda e: e.tensor_reduce(
        g.rmeta[0:1, 0:16],
        bass.AP(g.rowsc, 0, [[1024, 1], [1, 16], [16, 32]]),
        AX.X, ALU.add), deps=[dt1])
    dcci = bld.dmas("SP", lambda e, ds: e.dma_start(
        out=g.cc_in, in_=g.rmeta[0:1, 0:16]).then_inc(ds, 16), 1,
        deps=[dtot])
    if int(_os.environ.get("KPHR", "9")) < 3:
        return
    dcc = bld.g("POOL", lambda e: e.collective_compute(
        "AllGather", ALU.bypass, replica_groups=[list(range(8))],
        ins=[g.cc_in], outs=[g.cc_out]), deps=[dcci, g.d_dispzero])
    dcca = bld.dmas("SP", lambda e, ds: e.dma_start(
        out=P[0:8, PA_CNTALL:PA_CNTALL + E], in_=g.cc_out).then_inc(ds, 16),
        1, deps=[dcc, g.d_cntz])

    if int(_os.environ.get("KPHR", "9")) < 4:
        return
    phaseR2(g, OHF1, OHF2, CNTC, dt1, dt2, dcca)
    if "metai" in g.taps:
        g.bld.dmas("SP", lambda e, ds: e.dma_start(
            out=g.taps["metai"], in_=g.metai[:, :]).then_inc(ds, 16), 1,
            deps=[g.bld.now("VEC")])
    if "meta" in g.taps:
        g.bld.dmas("SP", lambda e, ds: e.dma_start(
            out=g.taps["meta"], in_=g.meta[:, :]).then_inc(ds, 16), 1,
            deps=[g.bld.now("VEC")])
    if "statr" in g.taps:
        g.bld.dmas("SP", lambda e, ds: e.dma_start(
            out=g.taps["statr"], in_=g.statr[:, :]).then_inc(ds, 16), 1,
            deps=[g.bld.now("VEC")])
    if "dispd" in g.taps:
        g.bld.dmas("SP", lambda e, ds: e.dma_start(
            out=g.taps["dispd"], in_=g.disp_dram).then_inc(ds, 16), 1,
            deps=[g.bld.now("VEC"), g.bld.dma_now("POOL")])
    if "rowsd" in g.taps:
        g.bld.dmas("SP", lambda e, ds: e.dma_start(
            out=g.taps["rowsd"], in_=g.rows_dram[ROW_XN:ROW_XN + NT, :]
            ).then_inc(ds, 16), 1, deps=[g.bld.now("VEC")])


def _wr_slice(g, j):
    return g.params[:, PA_WR + j * 16:PA_WR + j * 16 + 16]


def phaseR2(g, OHF1, OHF2, CNTC, dt1, dt2, dcca):
    bld, pk, C, P = g.bld, g.pk, g.consts, g.params
    db1 = bld.dmas("SP", lambda e, ds: e.dma_start(
        out=g.bounce[0:32, :].rearrange("(a r) n -> a (r n)", a=1),
        in_=g.rowsc[0:1, 0:512]).then_inc(ds, 16), 1,
        deps=[dt1] + g.pk.dep(4) + g.pk.dep(5))
    dz = bld.g("VEC", lambda e: e.memset(g.xtmp[:, 0:64], 0.0),
               deps=list(g.d_xtmp_free) + list(g.r_tail))
    dzb = bld.g("VEC", lambda e: e.memset(g.bufC[:, 3072:3072 + 16], 0.0),
                deps=list(g.r_tail))
    db2 = bld.dmas("POOL", lambda e, ds: e.dma_start(
        out=g.bufC[0:32, 3072:3072 + 16], in_=g.bounce[0:32, :]
        ).then_inc(ds, 16), 1, deps=[db1, dzb])
    dm1 = bld.g("TEN", lambda e: e.matmul(
        pk.t[3][0:32, 0:16], C[:, TRILX:TRILX + 32],
        g.bufC[:, 3072:3072 + 16], start=True, stop=False),
        deps=[db2] + pk.dep(3))
    dm2 = bld.g("TEN", lambda e: e.matmul(
        pk.t[3][0:32, 0:16],
        P[:, PA_CPFX:PA_CPFX + 1].to_broadcast([128, 32]),
        P[:, PA_CNTALL:PA_CNTALL + E], start=False, stop=True), deps=[dcca])
    doffs = bld.g("VEC", lambda e: e.tensor_copy(
        g.xtmp[0:32, 16:32], pk.t[3][0:32, 0:16]), deps=[dm1, dm2, dz])
    pk.consumed(3, doffs)

    KCW = []
    prevv = [doffs]
    for ti in range(32):
        dob = bld.g("TEN", lambda e, ti=ti: e.matmul(
            pk.t[2][:, 0:E],
            g.idf[0:32, ti:ti + 1].to_broadcast([32, 128]),
            g.xtmp[0:32, 16:32], start=True, stop=True),
            deps=[doffs] + pk.dep(2))

        ex = g.rmeta[:, 0:16]
        oh1 = g.bufC[:, OHF1 + ti * 16:OHF1 + ti * 16 + 16]
        oh2 = g.bufC[:, OHF2 + ti * 16:OHF2 + ti * 16 + 16]
        gg = g.statr[:, 1024 + ti * 16:1024 + ti * 16 + 4]
        dkw = chain(g, "VEC", [
            lambda e, ti=ti, ex=ex: e.tensor_add(
                ex, pk.t[2][:, 0:E], g.meta[:, ti * 16:ti * 16 + 16]),
            lambda e, ex=ex, oh1=oh1: e.tensor_tensor(
                g.rmeta[:, 16:32], ex, oh1, ALU.mult),
            lambda e: e.tensor_reduce(
                g.rmeta[:, 64:65], g.rmeta[:, 16:32], AX.X, ALU.add),
            lambda e, ex=ex, oh2=oh2: e.tensor_tensor(
                g.rmeta[:, 32:48], ex, oh2, ALU.mult),
            lambda e: e.tensor_reduce(
                g.rmeta[:, 65:66], g.rmeta[:, 32:48], AX.X, ALU.add),
            lambda e: e.tensor_single_scalar(
                g.rmeta[:, 66:67], g.rmeta[:, 64:65], float(CAP), ALU.is_lt),
            lambda e: e.tensor_single_scalar(
                g.rmeta[:, 67:68], g.rmeta[:, 65:66], float(CAP), ALU.is_lt),
            lambda e, gg=gg: e.tensor_tensor(
                g.rmeta[:, 68:69], gg[:, 0:1], g.rmeta[:, 66:67], ALU.mult),
            lambda e, gg=gg: e.tensor_tensor(
                g.rmeta[:, 69:70], gg[:, 2:3], g.rmeta[:, 67:68], ALU.mult),
            lambda e: e.tensor_add(
                g.rmeta[:, 70:71], g.rmeta[:, 68:69], g.rmeta[:, 69:70]),
            lambda e: e.tensor_single_scalar(
                g.rmeta[:, 71:72], g.rmeta[:, 70:71], 0.0, ALU.is_equal),
            lambda e: e.tensor_add(
                g.rmeta[:, 71:72], g.rmeta[:, 71:72], g.rmeta[:, 70:71]),
            lambda e: e.reciprocal(g.rmeta[:, 72:73], g.rmeta[:, 71:72]),
            lambda e, ti=ti: e.tensor_tensor(
                g.meta[:, 1024 + ti * 2:1024 + ti * 2 + 1],
                g.rmeta[:, 68:69], g.rmeta[:, 72:73], ALU.mult),
            lambda e, ti=ti: e.tensor_tensor(
                g.meta[:, 1024 + ti * 2 + 1:1024 + ti * 2 + 2],
                g.rmeta[:, 69:70], g.rmeta[:, 72:73], ALU.mult),
            lambda e, ti=ti: e.tensor_copy(
                g.statr[:, 1024 + ti * 16 + 4:1024 + ti * 16 + 6],
                g.rmeta[:, 66:68]),
            lambda e, oh1=oh1: e.tensor_scalar(
                g.rmeta[:, 16:32], oh1, g.rmeta[:, 66:67], None, ALU.mult),
            lambda e, ti=ti, oh2=oh2: e.scalar_tensor_tensor(
                g.bufC[:, CNTC + ti * 16:CNTC + ti * 16 + 16], oh2,
                g.rmeta[:, 67:68], g.rmeta[:, 16:32], ALU.mult, ALU.add),
        ], deps=[dob] + prevv)
        pk.consumed(2, dkw)
        ddeps = [dkw]
        if ti == 0:
            ddeps += pk.dep(5)
        dkt = bld.g("TEN", lambda e, ti=ti: e.matmul(
            pk.t[5][0:1, ti * 16:ti * 16 + E], C[:, ONES:ONES + 1],
            g.bufC[:, CNTC + ti * 16:CNTC + ti * 16 + 16],
            start=(ti == 0), stop=(ti == 31)), deps=ddeps)
        if ti == 31:
            g.d_ktot1 = dkt
        dtl = bld.g("TEN", lambda e, ti=ti: e.matmul(
            pk.t[3][:, 0:E], C[:, TRILX:TRILX + 128],
            g.bufC[:, CNTC + ti * 16:CNTC + ti * 16 + 16],
            start=True, stop=True), deps=[dkw] + pk.dep(3))
        dte = bld.g("VEC", lambda e, ti=ti: e.tensor_copy(
            g.meta[:, ti * 16:(ti + 1) * 16], pk.t[3][:, 0:E]), deps=[dtl])
        pk.consumed(3, dte)
        KCW.append(dte)
        prevv = [dkw, dte]

    dk1 = bld.g("VEC", lambda e: e.tensor_copy(
        g.rowsc[0:1, 0:512], pk.t[5][0:1, 0:512]), deps=[g.d_ktot1])
    pk.consumed(5, dk1)
    db1 = bld.dmas("SP", lambda e, ds: e.dma_start(
        out=g.bounce[0:32, :].rearrange("(a r) n -> a (r n)", a=1),
        in_=g.rowsc[0:1, 0:512]).then_inc(ds, 16), 1, deps=[dk1])
    db2 = bld.dmas("POOL", lambda e, ds: e.dma_start(
        out=g.bufC[0:32, 3072:3072 + 16], in_=g.bounce[0:32, :]
        ).then_inc(ds, 16), 1, deps=[db1])
    dko = bld.g("TEN", lambda e: e.matmul(
        pk.t[3][0:32, 0:16], C[:, TRILX:TRILX + 32],
        g.bufC[:, 3072:3072 + 16], start=True, stop=True),
        deps=[db2] + pk.dep(3))
    dkoffs = bld.g("VEC", lambda e: e.tensor_copy(
        g.xtmp[0:32, 16:32], pk.t[3][0:32, 0:16]), deps=[dko])
    pk.consumed(3, dkoffs)

    prevv = [dkoffs]
    g.d_scatter = []
    for b in range(B):
        for tt in range(16):
            ti = b * 16 + tt
            dob = bld.g("TEN", lambda e, ti=ti: e.matmul(
                pk.t[2][:, 0:E],
                g.idf[0:32, ti:ti + 1].to_broadcast([32, 128]),
                g.xtmp[0:32, 16:32], start=True, stop=True),
                deps=[dkoffs] + pk.dep(2))

            kx = g.rmeta[:, 0:16]
            gg = g.statr[:, 1024 + ti * 16:1024 + ti * 16 + 6]
            fns = [lambda e, ti=ti, kx=kx: e.tensor_add(
                kx, pk.t[2][:, 0:E], g.meta[:, ti * 16:ti * 16 + 16])]
            for k in range(2):
                ohc = OHF1 if k == 0 else OHF2
                oh = g.bufC[:, ohc + ti * 16:ohc + ti * 16 + 16]
                ik = gg[:, 1 + 2 * k:2 + 2 * k]
                fns += [
                    lambda e, kx=kx, oh=oh: e.tensor_tensor(
                        g.rmeta[:, 16:32], kx, oh, ALU.mult),
                    lambda e: e.tensor_reduce(
                        g.rmeta[:, 64:65], g.rmeta[:, 16:32], AX.X, ALU.add),
                    lambda e: e.tensor_single_scalar(
                        g.rmeta[:, 65:66], g.rmeta[:, 64:65], float(CLOC),
                        ALU.is_ge),
                    lambda e, ik=ik: e.tensor_scalar(
                        g.rmeta[:, 66:67], ik, float(CLOC),
                        float(-4 * CLOC), ALU.mult, ALU.add),
                    lambda e, ik=ik: e.tensor_single_scalar(
                        g.rmeta[:, 67:68], ik, 4.0, ALU.is_ge),
                    lambda e, ik=ik: e.tensor_single_scalar(
                        g.rmeta[:, 68:69], ik, 0.0, ALU.is_equal),
                    lambda e, ik=ik: e.tensor_single_scalar(
                        g.rmeta[:, 69:70], ik, 1.0, ALU.is_equal),
                    lambda e, ik=ik: e.tensor_single_scalar(
                        g.rmeta[:, 70:71], ik, 2.0, ALU.is_ge),
                    lambda e: e.tensor_sub(
                        g.rmeta[:, 70:71], g.rmeta[:, 70:71],
                        g.rmeta[:, 67:68]),
                    lambda e: e.tensor_add(
                        g.rmeta[:, 71:72], g.rmeta[:, 66:67],
                        g.rmeta[:, 64:65]),
                    lambda e: e.tensor_scalar(
                        g.rmeta[:, 72:73], g.rmeta[:, 67:68], -6e8, 6e8,
                        ALU.mult, ALU.add),
                    lambda e: e.tensor_scalar(
                        g.rmeta[:, 73:74], g.rmeta[:, 65:66], 6e8, None,
                        ALU.mult),
                    lambda e: e.tensor_add(
                        g.rmeta[:, 71:72], g.rmeta[:, 71:72],
                        g.rmeta[:, 72:73]),
                    lambda e: e.tensor_add(
                        g.rmeta[:, 71:72], g.rmeta[:, 71:72],
                        g.rmeta[:, 73:74]),
                    lambda e: e.tensor_scalar_min(
                        g.rmeta[:, 71:72], g.rmeta[:, 71:72],
                        float(EM * CLOC)),
                    lambda e: e.tensor_scalar_max(
                        g.rmeta[:, 71:72], g.rmeta[:, 71:72], 0.0),
                    lambda e, ti=ti, k=k: e.tensor_copy(
                        g.metai[:, 128 + ti * 2 + k:129 + ti * 2 + k],
                        g.rmeta[:, 71:72]),
                    lambda e: e.tensor_scalar_min(
                        g.rmeta[:, 74:75], g.rmeta[:, 64:65],
                        float(CLOC - 1)),
                    lambda e: e.tensor_add(
                        g.rmeta[:, 74:75], g.rmeta[:, 74:75],
                        g.rmeta[:, 66:67]),
                    lambda e: e.tensor_tensor(
                        g.rmeta[:, 74:75], g.rmeta[:, 74:75],
                        g.rmeta[:, 67:68], ALU.mult),
                    lambda e: e.tensor_scalar(
                        g.rmeta[:, 75:76], g.rmeta[:, 68:69],
                        float(ROW_ZERO), None, ALU.mult),
                    lambda e, b=b, tt=tt: e.tensor_scalar(
                        g.rmeta[:, 76:77], P[:, PA_IOTAP:PA_IOTAP + 1], 1.0,
                        float(ROW_XN + b * T + tt * 128), ALU.mult, ALU.add),
                    lambda e: e.tensor_tensor(
                        g.rmeta[:, 76:77], g.rmeta[:, 76:77],
                        g.rmeta[:, 69:70], ALU.mult),
                    lambda e, ik=ik: e.tensor_scalar(
                        g.rmeta[:, 77:78], ik, 1.0, float(ROW_ZERO - 1),
                        ALU.mult, ALU.add),
                    lambda e: e.tensor_tensor(
                        g.rmeta[:, 77:78], g.rmeta[:, 77:78],
                        g.rmeta[:, 70:71], ALU.mult),
                    lambda e: e.tensor_add(
                        g.rmeta[:, 74:75], g.rmeta[:, 74:75],
                        g.rmeta[:, 75:76]),
                    lambda e: e.tensor_add(
                        g.rmeta[:, 74:75], g.rmeta[:, 74:75],
                        g.rmeta[:, 76:77]),
                    lambda e: e.tensor_add(
                        g.rmeta[:, 74:75], g.rmeta[:, 74:75],
                        g.rmeta[:, 77:78]),
                    lambda e: e.tensor_scalar_min(
                        g.rmeta[:, 74:75], g.rmeta[:, 74:75],
                        float(NROWS - 1)),
                    lambda e: e.tensor_scalar_max(
                        g.rmeta[:, 74:75], g.rmeta[:, 74:75], 0.0),
                    lambda e, ti=ti, k=k: e.tensor_copy(
                        g.metai[:, ti * 2 + k:ti * 2 + k + 1],
                        g.rmeta[:, 74:75]),
                ]
            dsl = chain(g, "VEC", fns, deps=[dob] + prevv)
            pk.consumed(2, dsl)
            prevv = [dsl]
            if int(_os.environ.get("KPHR", "9")) < 5:
                continue
            drl = bld.dmas("POOL", lambda e, ds, b=b, tt=tt: e.dma_start(
                out=g.bufC[:, 4096:5120],
                in_=g.rows_dram[ROW_XN + b * T + tt * 128:
                                ROW_XN + b * T + (tt + 1) * 128, :]
                ).then_inc(ds, 16), 1,
                deps=[dsl] + ([g.d_scatter[-1]] if g.d_scatter else []))
            dsc = bld.dmas("POOL", lambda e, ds, ti=ti: [
                e.indirect_dma_start(
                    out=g.disp_dram,
                    out_offset=bass.IndirectOffsetOnAxis(
                        g.metai[:, 128 + ti * 2 + k:129 + ti * 2 + k], 0),
                    in_=g.bufC[:, 4096:5120],
                    in_offset=None).then_inc(ds, 16) for k in range(2)][-1],
                2, deps=[drl])
            g.d_scatter.append(dsc)




def load_xb(g, b, deps):
    return g.bld.dmas("POOL", lambda e, ds, b=b: e.dma_start(
        out=g.xb.rearrange("p (c t) -> p c t", c=8),
        in_=g.xs_dram[b].rearrange("(c p) t -> p c t", p=128)
        ).then_inc(ds, 16), 1, deps=deps)


def norm_scale(g, eps, deps, sq0, sq1, rstdb):
    bld, pk = g.bld, g.pk
    mmd = [None] * 4
    sq_war = {0: [], 1: []}
    for j in range(8):
        par = j % 2
        dst = sq0 if par == 0 else sq1
        dsq = bld.g("ACT", lambda e, j=j, dst=dst: e.activation(
            dst, g.xb[:, j * T:(j + 1) * T], AF.Square),
            deps=list(deps) + sq_war[par])
        cons = []
        for ts in range(4):
            dd = [dsq] + (pk.dep(ts) if j == 0 else [])
            mmd[ts] = bld.g("TEN", lambda e, j=j, ts=ts, dst=dst:
                            e.matmul(pk.t[ts][0:1, 0:512],
                                     g.consts[:, ONES:ONES + 1],
                                     dst[:, ts * 512:(ts + 1) * 512],
                                     start=(j == 0), stop=(j == 7)),
                            deps=dd)
            cons.append(mmd[ts])
        sq_war[par] = cons
    devs = []
    for ts in range(4):
        dev = bld.g("VEC", lambda e, ts=ts: e.tensor_copy(
            g.statr[0:1, ts * 512:(ts + 1) * 512], pk.t[ts][0:1, 0:512]),
            deps=[mmd[ts]])
        pk.consumed(ts, dev)
        devs.append(dev)
    drsa = bld.g("ACT", lambda e, eps=eps: e.activation(
        g.statr[64:65, 0:2048], g.statr[0:1, 0:2048], AF.Ln,
        bias=eps, scale=1.0 / D), deps=devs)
    drs = bld.g("ACT", lambda e: e.activation(
        g.statr[64:65, 0:2048], g.statr[64:65, 0:2048], AF.Exp,
        scale=-0.5), deps=[drsa])
    bdeps = []
    for ts in range(4):
        dbc = bcast_row_f32(g, g.statr[64:65, ts * 512:(ts + 1) * 512],
                            rstdb[:, ts * 512:(ts + 1) * 512], ts,
                            deps=[drs])
        bdeps.append(dbc)
    xsdeps = []
    for j in range(8):
        dxs = bld.g("VEC", lambda e, j=j: e.tensor_tensor(
            g.xb[:, j * T:(j + 1) * T], g.xb[:, j * T:(j + 1) * T],
            rstdb, ALU.mult), deps=bdeps + list(deps))
        xsdeps.append(dxs)
    return xsdeps


def wscale(g, wsb, nchunk, ncols, wcol, deps):
    def f(e, wsb=wsb, nchunk=nchunk, ncols=ncols, wcol=wcol):
        for j in range(nchunk):
            last = e.tensor_scalar(
                wsb[:, j * ncols:(j + 1) * ncols],
                wsb[:, j * ncols:(j + 1) * ncols],
                g.params[:, wcol + j:wcol + j + 1], None, ALU.mult)
        return last
    return g.bld.g("VEC", f, deps=deps)


def res_evac(g, b, mb, ts, bank, deps_load, deps_mm, slot):
    bld = g.bld
    if not hasattr(g, "evac_war"):
        g.evac_war = {0: [], 1: [], 2: [], 3: []}
    if slot < 2:
        xsl = g.xtmp[:, slot * 512:(slot + 1) * 512]
        osl = g.rowsc[:, slot * 512:(slot + 1) * 512]
    else:
        xsl = g.statr[:, (slot - 2) * 512:(slot - 1) * 512]
        osl = g.statr[:, 1024 + (slot - 2) * 512:1024 + (slot - 1) * 512]
    war = g.evac_war[slot]
    dxl = bld.dmas("SP", lambda e, ds, b=b, mb=mb, ts=ts, xsl=xsl:
                   e.dma_start(out=xsl,
                               in_=g.xs_dram[b, mb * 128:(mb + 1) * 128,
                                             ts * 512:(ts + 1) * 512]
                               ).then_inc(ds, 16), 1,
                   deps=list(deps_load) + war)
    dst = bld.g("VEC", lambda e, bank=bank, xsl=xsl, osl=osl:
                e.scalar_tensor_tensor(
                    osl, g.pk.t[bank][:, 0:512], 1.0,
                    xsl, ALU.mult, ALU.add),
                deps=[dxl] + deps_mm + war)
    g.pk.consumed(bank, dst)
    dso = bld.dmas("SP", lambda e, ds, b=b, mb=mb, ts=ts, osl=osl:
                   e.dma_start(out=g.xs_dram[b, mb * 128:(mb + 1) * 128,
                                             ts * 512:(ts + 1) * 512],
                               in_=osl).then_inc(ds, 16), 1, deps=[dst])
    g.evac_war[slot] = [dso, dst]
    return dso, dst


def inner_attn(g):
    bld, pk, C, P = g.bld, g.pk, g.consts, g.params
    KD, VB, VT, AMSK = 0, 8192, 12288, 16448
    SQ0, SQ1 = 8192, 10240
    RSTD = KD
    g.attn_tail = []
    for b in range(B):
        prev = [g.d_amasks] + (g.attn_tail if b else [])
        dxb = load_xb(g, b, deps=prev + getattr(g, "xb_free", []))
        xs = norm_scale(g, EPSI, [dxb], g.bufC[:, SQ0:SQ0 + 2048],
                        g.bufC[:, SQ1:SQ1 + 2048],
                        g.bufC[:, RSTD:RSTD + 2048])
        dwk = bld.dmas("POOL", lambda e, ds: (
            e.dma_start(out=g.wbuf[:, 0:2048].rearrange(
                "p (c m) -> p c m", c=8),
                in_=g.Wk.rearrange("(c p) m -> p c m", p=128)
                ).then_inc(ds, 16),
            e.dma_start(out=g.wbuf[:, 2048:4096].rearrange(
                "p (c m) -> p c m", c=8),
                in_=g.Wv.rearrange("(c p) m -> p c m", p=128)
                ).then_inc(ds, 16))[-1], 2,
            deps=prev + getattr(g, "wbuf_free", []))
        dws1 = wscale(g, g.wbuf[:, 0:2048], 8, 256, PA_ATTNW, [dwk])
        dws2 = wscale(g, g.wbuf[:, 2048:4096], 8, 256, PA_ATTNW, [dwk])
        kv_evs = []
        for (woff, tag) in ((0, "k"), (2048, "v")):
            for m in range(2):
                for ts in range(4):
                    bank = (m * 4 + ts) % 2
                    dmm = None
                    for k in range(8):
                        dd = ([dws1, dws2] + xs + pk.dep(bank)) \
                            if k == 0 else []
                        dmm = bld.g("TEN", lambda e, woff=woff, m=m, ts=ts,
                                    k=k, bank=bank: e.matmul(
                                        pk.t[bank][:, 0:512],
                                        g.wbuf[:, woff + k * 256 + m * 128:
                                               woff + k * 256 + m * 128
                                               + 128],
                                        g.xb[:, k * T + ts * 512:
                                             k * T + ts * 512 + 512],
                                        start=(k == 0), stop=(k == 7)),
                                    deps=dd)
                    if tag == "k":
                        def ev(e, m=m, ts=ts, bank=bank):
                            e.tensor_copy(
                                g.bufC[0:64, KD + (2 * m) * T + ts * 512:
                                       KD + (2 * m) * T + ts * 512 + 512],
                                pk.t[bank][0:64, 0:512])
                            return e.tensor_copy(
                                g.bufC[64:128,
                                       KD + (2 * m + 1) * T + ts * 512:
                                       KD + (2 * m + 1) * T + ts * 512
                                       + 512],
                                pk.t[bank][64:128, 0:512])
                        dev = bld.g("VEC", ev, deps=[dmm] + xs)
                    else:
                        dev = bld.g("VEC", lambda e, m=m, ts=ts, bank=bank:
                                    e.tensor_copy(
                                        g.bufC[:, VB + m * T + ts * 512:
                                               VB + m * T + ts * 512 + 512],
                                        pk.t[bank][:, 0:512]), deps=[dmm])
                    pk.consumed(bank, dev)
                    kv_evs.append(dev)
        def kdupf(e, ds):
            for gg in (0, 2):
                e.dma_start(out=g.bufC[64:128, KD + gg * T:KD + (gg + 1) * T],
                            in_=g.bufC[0:64, KD + gg * T:KD + (gg + 1) * T]
                            ).then_inc(ds, 16)
            for gg in (1, 3):
                e.dma_start(out=g.bufC[0:64, KD + gg * T:KD + (gg + 1) * T],
                            in_=g.bufC[64:128, KD + gg * T:KD + (gg + 1) * T]
                            ).then_inc(ds, 16)
        dkd = bld.dmas("SP", kdupf, 4, deps=kv_evs)
        dvo = bld.g("VEC", lambda e: e.memset(g.bufC[:, VT:VT + 4160], 1.0),
                    deps=kv_evs)
        vt_evs = []
        for m in range(2):
            for tl in range(16):
                bank = 6 + (tl % 2)
                dtr = bld.g("TEN", lambda e, m=m, tl=tl, bank=bank:
                            e.transpose(pk.t[bank][:, 0:128],
                                        g.bufC[:, VB + m * T + tl * 128:
                                               VB + m * T + tl * 128 + 128],
                                        C[:, ID_B:ID_B + 128]),
                            deps=kv_evs + pk.dep(bank) + [g.d_masks])
                dev = bld.g("VEC", lambda e, m=m, tl=tl, bank=bank:
                            e.tensor_copy(
                                g.bufC[:, VT + m * 2080 + tl * 130:
                                       VT + m * 2080 + tl * 130 + 130]
                                .rearrange("p (s h) -> p s h", s=2)
                                [:, :, 0:64],
                                pk.t[bank][:, 0:128]
                                .rearrange("p (s h) -> p s h", s=2)),
                            deps=[dtr, dvo])
                pk.consumed(bank, dev)
                vt_evs.append(dev)

        dwq = bld.dmas("POOL", lambda e, ds: e.dma_start(
            out=g.wbuf[:, 0:8192].rearrange("p (c m) -> p c m", c=8),
            in_=g.Wq.rearrange("(c p) m -> p c m", p=128)).then_inc(ds, 16),
            1, deps=kv_evs)
        dwqs = wscale(g, g.wbuf[:, 0:8192], 8, 1024, PA_ATTNW, [dwq])
        q_evs = []
        for m in range(8):
            for ts in range(4):
                bank = (m * 4 + ts) % 2
                dmm = None
                for k in range(8):
                    dd = ([dwqs] + xs + pk.dep(bank)) if k == 0 else []
                    dmm = bld.g("TEN", lambda e, m=m, ts=ts, k=k, bank=bank:
                                e.matmul(
                                    pk.t[bank][:, 0:512],
                                    g.wbuf[:, k * 1024 + m * 128:
                                           k * 1024 + m * 128 + 128],
                                    g.xb[:, k * T + ts * 512:
                                         k * T + ts * 512 + 512],
                                    start=(k == 0), stop=(k == 7)), deps=dd)
                dev = bld.g("ACT", lambda e, m=m, ts=ts, bank=bank:
                            e.activation(
                                g.bufA[:, m * T + ts * 512:
                                       m * T + ts * 512 + 512],
                                pk.t[bank][:, 0:512], AF.Copy), deps=[dmm])
                pk.consumed(bank, dev)
                q_evs.append(dev)

        PT = 16384
        o_evs = []
        sq_war = {0: [], 64: []}
        g.kn_dep = None
        prev_pv = []
        prev_dlb = []
        for h in range(NH):
            gkv = h // 4
            qb = (h % 2) * 64
            dsq = bld.g("ACT", lambda e, h=h, qb=qb: e.activation(
                g.wbuf[qb:qb + 64, 0:2048],
                g.bufA[qb:qb + 64, (h // 2) * T:(h // 2) * T + T],
                AF.Square), deps=q_evs + sq_war[qb] + [dwqs])
            qn_mm = []
            for ts in range(4):
                dmm = bld.g("TEN", lambda e, ts=ts, qb=qb: e.matmul(
                    pk.t[4][0:1, 0:512],
                    C[qb:qb + 64, ONES:ONES + 1],
                    g.wbuf[qb:qb + 64, ts * 512:ts * 512 + 512],
                    start=True, stop=True), deps=[dsq] + pk.dep(4))
                dev = bld.g("VEC", lambda e, ts=ts: e.reduce_max(
                    g.rmeta[0:1, 100 + ts:101 + ts], pk.t[4][0:1, 0:512],
                    AX.X), deps=[dmm])
                pk.consumed(4, dev)
                qn_mm.append(dev)
            sq_war[qb] = qn_mm
            if h % 4 == 0:
                dsk = bld.g("ACT", lambda e, h=h, qb=qb, gkv=gkv:
                            e.activation(
                                g.wbuf[qb:qb + 64, 2048:4096],
                                g.bufC[qb:qb + 64, KD + gkv * T:
                                       KD + gkv * T + T],
                                AF.Square), deps=[dsq, dkd] + sq_war[qb])
                kn_mm = []
                for ts in range(4):
                    dmm = bld.g("TEN", lambda e, ts=ts, qb=qb: e.matmul(
                        pk.t[5][0:1, 0:512],
                        C[qb:qb + 64, ONES:ONES + 1],
                        g.wbuf[qb:qb + 64, 2048 + ts * 512:
                               2048 + ts * 512 + 512],
                        start=True, stop=True), deps=[dsk] + pk.dep(5))
                    dev = bld.g("VEC", lambda e, ts=ts: e.reduce_max(
                        g.rmeta[0:1, 104 + ts:105 + ts],
                        pk.t[5][0:1, 0:512], AX.X), deps=[dmm])
                    pk.consumed(5, dev)
                    kn_mm.append(dev)
                dkp = chain(g, "VEC", [
                    lambda e: e.tensor_copy(g.rmeta[0:1, 108:109],
                                            g.rmeta[0:1, 104:105]),
                    lambda e: e.tensor_tensor(g.rmeta[0:1, 109:110],
                                              g.rmeta[0:1, 108:109],
                                              g.rmeta[0:1, 105:106],
                                              ALU.max),
                    lambda e: e.tensor_tensor(g.rmeta[0:1, 110:111],
                                              g.rmeta[0:1, 109:110],
                                              g.rmeta[0:1, 106:107],
                                              ALU.max),
                    lambda e: e.tensor_tensor(g.rmeta[0:1, 111:112],
                                              g.rmeta[0:1, 110:111],
                                              g.rmeta[0:1, 107:108],
                                              ALU.max),
                ], deps=kn_mm)
                g.kn_dep = dkp
                sq_war[qb] = sq_war[qb] + kn_mm
            for qc in range(4):
                ntk = 4 * (qc + 1)
                dbb = bld.g("VEC", lambda e, qc=qc: e.tensor_tensor(
                    g.rmeta[0:1, 112:113], g.rmeta[0:1, 100 + qc:101 + qc],
                    g.rmeta[0:1, 108 + qc:109 + qc], ALU.mult),
                    deps=qn_mm + [g.kn_dep])
                dbs0 = bld.g("ACT", lambda e: e.activation(
                    g.rmeta[0:1, 113:114], g.rmeta[0:1, 112:113], AF.Ln),
                    deps=[dbb])
                dbs = bld.g("ACT", lambda e: e.activation(
                    g.rmeta[0:1, 113:114], g.rmeta[0:1, 113:114], AF.Exp,
                    scale=0.5), deps=[dbs0])
                dbn = bld.g("VEC", lambda e: e.tensor_scalar_mul(
                    g.rmeta[0:1, 114:115], g.rmeta[0:1, 113:114], -0.125),
                    deps=[dbs])
                dnb = bcast_row_f32(g, g.rmeta[0:1, 114:115],
                                    g.rmeta[:, 115:116], 3,
                                    deps=[dbn] + prev_pv)
                exp_deps = []
                for tkc in range(ntk):
                    bank = tkc % 3
                    dd = [dnb, dkd] + q_evs + pk.dep(bank) + prev_pv
                    dmm = bld.g("TEN", lambda e, h=h, qc=qc, tkc=tkc,
                                bank=bank, qb=qb, gkv=gkv: e.matmul(
                                    pk.t[bank][:, 0:512],
                                    g.bufC[qb:qb + 64,
                                           KD + gkv * T + tkc * 128:
                                           KD + gkv * T + tkc * 128 + 128],
                                    g.bufA[qb:qb + 64,
                                           (h // 2) * T + qc * 512:
                                           (h // 2) * T + qc * 512 + 512],
                                    start=True, stop=True), deps=dd)
                    dex = bld.g("ACT", lambda e, tkc=tkc, bank=bank:
                                e.activation(
                                    g.bufA[:, PT + tkc * 512:
                                           PT + tkc * 512 + 512],
                                    pk.t[bank][:, 0:512], AF.Exp,
                                    scale=0.125,
                                    bias=g.rmeta[:, 115:116]),
                                deps=[dmm, dnb])
                    pk.consumed(bank, dex)
                    off = tkc * 128 - qc * 512
                    if 0 <= off < 512:
                        dmk = bld.g("VEC", lambda e, tkc=tkc, off=off:
                                    e.tensor_tensor(
                                        g.bufA[:, PT + tkc * 512:
                                               PT + tkc * 512 + 512],
                                        g.bufA[:, PT + tkc * 512:
                                               PT + tkc * 512 + 512],
                                        g.bufC[:, AMSK + (off // 128) * 512:
                                               AMSK + (off // 128) * 512
                                               + 512],
                                        ALU.mult),
                                    deps=[dex, g.d_amasks])
                        exp_deps.append(dmk)
                    else:
                        exp_deps.append(dex)
                dpv = None
                for tkc in range(ntk):
                    dd = [exp_deps[tkc]] + vt_evs
                    if tkc == 0:
                        dd = dd + pk.dep(5)
                    dpv = bld.g("TEN", lambda e, tkc=tkc, gkv=gkv, ntk=ntk:
                                e.matmul(
                                    pk.t[5][0:65, 0:512],
                                    g.bufC[:, VT + (gkv // 2) * 2080
                                           + tkc * 130 + (gkv % 2) * 65:
                                           VT + (gkv // 2) * 2080
                                           + tkc * 130 + (gkv % 2) * 65
                                           + 65],
                                    g.bufA[:, PT + tkc * 512:
                                           PT + tkc * 512 + 512],
                                    start=(tkc == 0), stop=(tkc == ntk - 1)),
                                deps=dd)
                dlr = bld.g("VEC", lambda e: e.reciprocal(
                    g.xtmp[64:65, 512:1024], pk.t[5][64:65, 0:512]),
                    deps=[dpv] + prev_dlb)
                dlb = bld.g("TEN", lambda e: e.matmul(
                    pk.t[4][0:64, 0:512],
                    P[64:65, PA_ONESF:PA_ONESF + 1].to_broadcast([1, 64]),
                    g.xtmp[64:65, 512:1024], start=True, stop=True),
                    deps=[dlr] + pk.dep(4) + [g.d_onesf])
                dle = bld.g("VEC", lambda e: e.tensor_copy(
                    g.xtmp[0:64, 0:512], pk.t[4][0:64, 0:512]), deps=[dlb])
                pk.consumed(4, dle)
                doe = bld.g("VEC", lambda e, h=h, qc=qc, qb=qb:
                            e.tensor_tensor(
                                g.bufB[qb:qb + 64, (h // 2) * T + qc * 512:
                                       (h // 2) * T + qc * 512 + 512],
                                pk.t[5][0:64, 0:512], g.xtmp[0:64, 0:512],
                                ALU.mult), deps=[dle, dpv])
                pk.consumed(5, doe)
                o_evs.append(doe)
                prev_pv = [dpv]
                prev_dlb = [dlb]

        dwo = bld.dmas("POOL", lambda e, ds: e.dma_start(
            out=g.wbuf[:, 0:8192].rearrange("p (c m) -> p c m", c=8),
            in_=g.Wo.rearrange("(c p) m -> p c m", p=128)).then_inc(ds, 16),
            1, deps=o_evs + sq_war[0] + sq_war[64])
        flash_done = [bld.now("VEC"), bld.now("ACT"), bld.now("TEN")]
        tail = []
        slot = 0
        prev_slot = {0: [], 1: [], 2: [], 3: []}
        for m in range(8):
            for ts in range(4):
                bank = (m * 4 + ts) % 2
                dmm = None
                for k in range(8):
                    dd = ([dwo] + o_evs + pk.dep(bank)) if k == 0 else []
                    dmm = bld.g("TEN", lambda e, m=m, ts=ts, k=k, bank=bank:
                                e.matmul(
                                    pk.t[bank][:, 0:512],
                                    g.wbuf[:, k * 1024 + m * 128:
                                           k * 1024 + m * 128 + 128],
                                    g.bufB[:, k * T + ts * 512:
                                           k * T + ts * 512 + 512],
                                    start=(k == 0), stop=(k == 7)), deps=dd)
                dso, dst = res_evac(g, b, m, ts, bank,
                                    prev_slot[slot] + flash_done,
                                    [dmm], slot)
                prev_slot[slot] = [dso, dst]
                slot = (slot + 1) % 4
                tail.append(dso)
        g.attn_tail = tail
        g.wbuf_free = tail
        g.xb_free = tail


def inner_ffn(g):
    bld, pk, C = g.bld, g.pk, g.consts
    SQ0, SQ1, RSTD = 0, 2048, 4096
    prev = g.attn_tail
    for b in range(B):
        dxb = load_xb(g, b, deps=prev)
        xs = norm_scale(g, EPSI, [dxb], g.bufC[:, SQ0:SQ0 + 2048],
                        g.bufC[:, SQ1:SQ1 + 2048],
                        g.bufC[:, RSTD:RSTD + 2048])
        dhz = bld.g("VEC", lambda e: e.memset(
            g.bufA[64:128, 21 * 1024:22 * 1024], 0.0), deps=prev)
        tail = prev
        for tc in range(2):
            ha_deps = []
            wgu_war = {0: [], 1: []}
            scr_war = {0: [], 1: []}
            for fb in range(22):
                fw = 128 if fb < 21 else 64
                wo = (fb % 2) * 2048
                dwg = bld.dmas("POOL", lambda e, ds, fb=fb, fw=fw, wo=wo: (
                    e.dma_start(
                        out=g.wbuf[:, wo:wo + 8 * fw].rearrange(
                            "p (c m) -> p c m", c=8),
                        in_=g.Wg.rearrange("(c p) m -> p c m", p=128)
                        [:, :, fb * 128:fb * 128 + fw]).then_inc(ds, 16),
                    e.dma_start(
                        out=g.wbuf[:, 1024 + wo:1024 + wo + 8 * fw]
                        .rearrange("p (c m) -> p c m", c=8),
                        in_=g.Wu.rearrange("(c p) m -> p c m", p=128)
                        [:, :, fb * 128:fb * 128 + fw]).then_inc(ds, 16)
                    )[-1], 2, deps=wgu_war[fb % 2] + (tail if tc == 0 or
                                                      fb < 2 else []))
                dwsc1 = wscale(g, g.wbuf[:, wo:wo + 8 * fw], 8, fw,
                               PA_FFNW, [dwg])
                dwsc2 = wscale(g, g.wbuf[:, 1024 + wo:1024 + wo + 8 * fw],
                               8, fw, PA_FFNW, [dwsc1, dwg])
                myw = []
                for ts in range(2):
                    bankA = 0 if ts == 0 else 1
                    bankB = 2 if ts == 0 else 3
                    dmmA = dmmB = None
                    for k in range(8):
                        dd = ([dwsc1, dwsc2] + xs + pk.dep(bankA)) \
                            if k == 0 else []
                        dmmA = bld.g("TEN", lambda e, fw=fw, wo=wo, tc=tc,
                                     ts=ts, k=k, bankA=bankA: e.matmul(
                                         pk.t[bankA][0:fw, 0:512],
                                         g.wbuf[:, wo + k * fw:
                                                wo + k * fw + fw],
                                         g.xb[:, k * T + tc * 1024
                                              + ts * 512:
                                              k * T + tc * 1024 + ts * 512
                                              + 512],
                                         start=(k == 0), stop=(k == 7)),
                                     deps=dd)
                    for k in range(8):
                        dd = ([dwsc1, dwsc2] + xs + pk.dep(bankB)) \
                            if k == 0 else []
                        dmmB = bld.g("TEN", lambda e, fw=fw, wo=wo, tc=tc,
                                     ts=ts, k=k, bankB=bankB: e.matmul(
                                         pk.t[bankB][0:fw, 0:512],
                                         g.wbuf[:, 1024 + wo + k * fw:
                                                1024 + wo + k * fw + fw],
                                         g.xb[:, k * T + tc * 1024
                                              + ts * 512:
                                              k * T + tc * 1024 + ts * 512
                                              + 512],
                                         start=(k == 0), stop=(k == 7)),
                                     deps=dd)
                    dsl = bld.g("ACT", lambda e, fw=fw, ts=ts, bankA=bankA:
                                e.activation(
                                    g.bufC[0:fw, 6144 + ts * 512:
                                           6144 + ts * 512 + 512],
                                    pk.t[bankA][0:fw, 0:512], AF.Silu),
                                deps=[dmmA] + scr_war[ts])
                    pk.consumed(bankA, dsl)
                    dh = bld.g("VEC", lambda e, fb=fb, fw=fw, ts=ts,
                               bankB=bankB: e.tensor_tensor(
                                   g.bufA[0:fw, fb * 1024 + ts * 512:
                                          fb * 1024 + ts * 512 + 512],
                                   pk.t[bankB][0:fw, 0:512],
                                   g.bufC[0:fw, 6144 + ts * 512:
                                          6144 + ts * 512 + 512], ALU.mult),
                               deps=[dsl, dmmB, dhz])
                    pk.consumed(bankB, dh)
                    ha_deps.append(dh)
                    scr_war[ts] = [dh]
                    myw += [dsl, dh]
                wgu_war[fb % 2] = myw
            tail = []
            slot = 0
            prev_slot = {0: [], 1: [], 2: [], 3: []}
            wd_war = []
            for mq in range(4):
                dwd = bld.dmas("POOL", lambda e, ds, mq=mq: (
                    e.dma_start(
                        out=g.wbuf[:, 0:5376].rearrange(
                            "p (c m) -> p c m", c=21),
                        in_=g.Wd[0:2688, :].rearrange(
                            "(c p) m -> p c m", p=128)
                        [:, :, mq * 256:mq * 256 + 256]).then_inc(ds, 16),
                    e.dma_start(
                        out=g.wbuf[0:64, 5376:5632],
                        in_=g.Wd[2688:2752, mq * 256:mq * 256 + 256]
                        ).then_inc(ds, 16))[-1], 2,
                    deps=ha_deps + wd_war + wgu_war[0] + wgu_war[1])
                dwdz = bld.g("VEC", lambda e: e.memset(
                    g.wbuf[64:128, 21 * 256:22 * 256], 0.0), deps=[dwd])
                for mi in range(2):
                    m = mq * 2 + mi
                    for ts in range(2):
                        bank = 4 + ((mi * 2 + ts) % 2)
                        dmm = None
                        for k in range(22):
                            dd = ([dwdz] + ha_deps + pk.dep(bank)) \
                                if k == 0 else []
                            dmm = bld.g("TEN", lambda e, mi=mi, ts=ts, k=k,
                                        bank=bank: e.matmul(
                                            pk.t[bank][:, 0:512],
                                            g.wbuf[:, k * 256 + mi * 128:
                                                   k * 256 + mi * 128
                                                   + 128],
                                            g.bufA[:, k * 1024 + ts * 512:
                                                   k * 1024 + ts * 512
                                                   + 512],
                                            start=(k == 0), stop=(k == 21)),
                                        deps=dd)
                            wd_war.append(dmm)
                        tsg = tc * 2 + ts
                        dso, dst = res_evac(g, b, m, tsg, bank,
                                            prev_slot[slot], [dmm], slot)
                        prev_slot[slot] = [dso, dst]
                        slot = (slot + 1) % 4
                        tail.append(dso)
        prev = tail
    g.ffn_tail = prev


def _wx_load(g, li, half, deps):
    if half == 0:
        off, w = 0, 512
    else:
        off, w = 512, 544
    return g.bld.dmas("POOL", lambda e, ds, li=li, off=off, w=w: e.dma_start(
        out=g.wbuf[:, 0:8 * w].rearrange("p (c m) -> p c m", c=8),
        in_=g.Wx[li].rearrange("(c p) m -> p c m", p=128)[:, :, off:off + w]
        ).then_inc(ds, 16), 1, deps=deps)


RL = 256
NRUN = 16 * RL


def inner_mamba(g, li):
    bld, pk, C, P = g.bld, g.pk, g.consts, g.params
    SQ0, SQ1, RSTD = 0, 2048, 4096
    B0, B1, C0, C1 = 0, NRUN, 2 * NRUN, 3 * NRUN
    BCR = 4 * NRUN
    DSRC = BCR + 2048
    WVS = DSRC + 1024
    CARR = WVS + 512
    TM16 = CARR + 128
    ZST = TM16 + 32
    A0, U0, H0 = 0, NRUN, 2 * NRUN
    PAR1 = 3 * NRUN
    prev = g.ffn_tail if li == 0 else g.mamba_tail
    for b in range(B):
        dxb = load_xb(g, b, deps=prev)
        xs = norm_scale(g, EPSI, [dxb], g.bufC[:, SQ0:SQ0 + 2048],
                        g.bufC[:, SQ1:SQ1 + 2048],
                        g.bufC[:, RSTD:RSTD + 2048])
        tailz = []
        xz1_evs = []
        win_evs = []
        for half in range(2):
            dwi = bld.dmas("POOL", lambda e, ds, li=li, half=half:
                           e.dma_start(
                               out=g.wbuf[:, 0:8192].rearrange(
                                   "p (c m) -> p c m", c=8),
                               in_=g.Win[li].rearrange(
                                   "(c p) m -> p c m", p=128)
                               [:, :, half * 1024:half * 1024 + 1024]
                               ).then_inc(ds, 16), 1,
                           deps=(prev + getattr(g, "wbuf_free", []))
                           if half == 0 else win_evs)
            dwis = wscale(g, g.wbuf[:, 0:8192], 8, 1024,
                          PA_MNW + li * 8, [dwi])
            zslot_war = {0: [], 1: []}
            for m in range(8):
                for k in range(8):
                    for ts in range(4):
                        bank = ts
                        dd = []
                        if k == 0:
                            dd = [dwis] + xs + pk.dep(bank)
                        dmm = bld.g("TEN", lambda e, m=m, ts=ts, k=k,
                                    bank=bank: e.matmul(
                                        pk.t[bank][:, 0:512],
                                        g.wbuf[:, k * 1024 + m * 128:
                                               k * 1024 + m * 128 + 128],
                                        g.xb[:, k * T + ts * 512:
                                             k * T + ts * 512 + 512],
                                        start=(k == 0), stop=(k == 7)),
                                    deps=dd)
                        if k == 7:
                            win_evs.append(dmm)
                            if half == 0:
                                dev = bld.g("ACT", lambda e, m=m, ts=ts,
                                            bank=bank: e.activation(
                                                g.bufB[:, m * T + ts * 512:
                                                       m * T + ts * 512
                                                       + 512],
                                                pk.t[bank][:, 0:512],
                                                AF.Copy), deps=[dmm])
                                pk.consumed(bank, dev)
                                xz1_evs.append(dev)
                            else:
                                zs = (m * 4 + ts) % 2
                                dev = bld.g("ACT", lambda e, zs=zs,
                                            bank=bank: e.activation(
                                                g.bufC[:, ZST + zs * 512:
                                                       ZST + zs * 512
                                                       + 512],
                                                pk.t[bank][:, 0:512],
                                                AF.Copy),
                                            deps=[dmm] + zslot_war[zs])
                                pk.consumed(bank, dev)
                                dzs = bld.dmas(
                                    "SP", lambda e, ds, b=b, m=m, ts=ts,
                                    zs=zs: e.dma_start(
                                        out=g.z_dram[
                                            b, m * 128:(m + 1) * 128,
                                            ts * 512:(ts + 1) * 512],
                                        in_=g.bufC[:, ZST + zs * 512:
                                                   ZST + zs * 512 + 512]
                                        ).then_inc(ds, 16), 1, deps=[dev])
                                zslot_war[zs] = [dzs]
                                tailz.append(dzs)
        xc_evs = []
        for jd in range(8):
            def conv(e, jd=jd, li=li):
                cwc = PA_CONVW + (li * 8 + jd) * 4
                e.tensor_scalar(g.xb[:, jd * T:(jd + 1) * T],
                                g.bufB[:, jd * T:(jd + 1) * T],
                                P[:, cwc + 3:cwc + 4], None, ALU.mult)
                last = None
                for sh in (1, 2, 3):
                    last = e.scalar_tensor_tensor(
                        g.xb[:, jd * T + sh:(jd + 1) * T],
                        g.bufB[:, jd * T:(jd + 1) * T - sh],
                        P[:, cwc + 3 - sh:cwc + 4 - sh],
                        g.xb[:, jd * T + sh:(jd + 1) * T],
                        ALU.mult, ALU.add)
                return last
            dcv = bld.g("VEC", conv, deps=xz1_evs + win_evs + [g.d_convw])
            dsl = bld.g("ACT", lambda e, jd=jd, li=li: e.activation(
                g.xb[:, jd * T:(jd + 1) * T], g.xb[:, jd * T:(jd + 1) * T],
                AF.Silu, bias=P[:, PA_CONVB + li * 8 + jd:
                                PA_CONVB + li * 8 + jd + 1]), deps=[dcv])
            xc_evs.append(dsl)
        proj_evs = []
        proj_mms = []
        pstage_war = {0: list(tailz), 1: list(tailz)}
        statr_war = {0: [], 1: []}
        for half in range(2):
            dwx = _wx_load(g, li, half,
                           deps=(xz1_evs + win_evs) if half == 0
                           else proj_mms)
            mlist = range(0, 4) if half == 0 else range(4, 9)
            w = 512 if half == 0 else 544
            for m in mlist:
                mo = (m - (0 if half == 0 else 4)) * 128
                mw = 128 if m < 8 else 32
                for k in range(8):
                    for ts in range(4):
                        bank = ts
                        dd = []
                        if k == 0:
                            dd = [dwx] + xc_evs + pk.dep(bank)
                        dmm = bld.g("TEN", lambda e, mo=mo, mw=mw, ts=ts,
                                    k=k, bank=bank, w=w: e.matmul(
                                        pk.t[bank][0:mw, 0:512],
                                        g.wbuf[:, k * w + mo:
                                               k * w + mo + mw],
                                        g.xb[:, k * T + ts * 512:
                                             k * T + ts * 512 + 512],
                                        start=(k == 0), stop=(k == 7)),
                                    deps=dd)
                        if k < 7:
                            continue
                        proj_mms.append(dmm)
                        if m < 8:
                            ps = ts % 2
                            sp0 = ps * 512
                            da1 = bld.g("ACT", lambda e, sp0=sp0, bank=bank:
                                        e.activation(
                                            g.statr[:, sp0:sp0 + 512],
                                            pk.t[bank][:, 0:512], AF.Abs),
                                        deps=[dmm] + statr_war[ps])
                            da2 = bld.g("ACT", lambda e, sp0=sp0:
                                        e.activation(
                                            g.statr[:, sp0:sp0 + 512],
                                            g.statr[:, sp0:sp0 + 512],
                                            AF.Exp, scale=-1.0), deps=[da1])
                            da3 = bld.g("ACT", lambda e, sp0=sp0:
                                        e.activation(
                                            g.statr[:, sp0:sp0 + 512],
                                            g.statr[:, sp0:sp0 + 512],
                                            AF.Ln, bias=1.0), deps=[da2])
                            dev = bld.g("VEC", lambda e, ps=ps, sp0=sp0,
                                        bank=bank: e.scalar_tensor_tensor(
                                            g.bufC[:, ZST + ps * 512:
                                                   ZST + ps * 512 + 512],
                                            pk.t[bank][:, 0:512], 0.0,
                                            g.statr[:, sp0:sp0 + 512],
                                            ALU.max, ALU.add),
                                        deps=[da3] + pstage_war[ps])
                            pk.consumed(bank, dev)
                            statr_war[ps] = [dev]
                            dst = bld.dmas(
                                "SP", lambda e, ds, m=m, ts=ts, ps=ps:
                                e.dma_start(
                                    out=g.proj_dram[m * 128:(m + 1) * 128,
                                                    ts * 512:(ts + 1)
                                                    * 512],
                                    in_=g.bufC[:, ZST + ps * 512:
                                               ZST + ps * 512 + 512]
                                    ).then_inc(ds, 16), 1, deps=[dev])
                            pstage_war[ps] = [dst]
                            proj_evs.append(dst)
                        else:
                            dev = bld.g("ACT", lambda e, ts=ts, bank=bank:
                                        e.activation(
                                            g.bufC[0:32,
                                                   BCR + ts * 512:
                                                   BCR + ts * 512 + 512],
                                            pk.t[bank][0:32, 0:512],
                                            AF.Copy), deps=[dmm])
                            pk.consumed(bank, dev)
                            proj_evs.append(dev)
        def scaninit(e):
            return e.memset(g.bufC[:, CARR:CARR + 128], 0.0)
        dsi = bld.g("VEC", scaninit, deps=xs + tailz + xz1_evs
                    + [d for d in pstage_war[0] + pstage_war[1]])

        NIT = NSCH * 8
        free_a = {0: [dsi], 1: [dsi]}
        free_u = {0: [dsi], 1: [dsi]}
        free_h = {0: [dsi], 1: [dsi]}
        wv_free = {0: [], 1: []}
        tm_war = {0: [], 1: []}
        dsrc_free = {0: [], 1: [], 2: [], 3: []}
        bc_evs = {0: [], 1: []}
        bc_readers = {0: [], 1: []}
        dsrc_ld = {}
        y_evs = []

        def emit_dsrc(i):
            sc, jd = divmod(i, 8)
            sl = i % 4
            dsrc_ld[i] = bld.dmas(
                "SP", lambda e, ds, jd=jd, sc=sc, sl=sl: e.dma_start(
                    out=g.bufC[:, DSRC + sl * 256:DSRC + sl * 256 + 256],
                    in_=g.proj_dram[jd * 128:(jd + 1) * 128,
                                    sc * 256:sc * 256 + 256]
                    ).then_inc(ds, 16), 1,
                deps=proj_evs + dsrc_free[sl])

        bc_rd_snap = {0: [], 1: []}
        bc_accum = {0: [], 1: []}

        def emit_bcast_part(sc, part):
            sl = sc % 2
            if part == 0:
                bc_rd_snap[sl] = bc_readers[sl]
                bc_readers[sl] = []
                bc_accum[sl] = []
            for (rbase, base, bank_) in ((0, (B0, B1)[sl], 0),
                                         (16, (C0, C1)[sl], 1)):
                dmms = []
                for k in range(2):
                    n = 2 * part + k
                    dd = proj_evs + [g.d_masks, dsi] + bc_rd_snap[sl] \
                        + pk.dep(bank_)
                    dmms.append(bld.g("TEN", lambda e, srow=rbase + n,
                                      sc=sc, k=k, bank_=bank_: e.matmul(
                                          pk.t[bank_][:, k * 256:
                                                      k * 256 + 256],
                                          C[0:32, ID_B + srow:
                                            ID_B + srow + 1]
                                          .to_broadcast([32, 128]),
                                          g.bufC[0:32, BCR + sc * 256:
                                                 BCR + sc * 256 + 256],
                                          start=True, stop=True), deps=dd))
                dev = bld.g("ACT", lambda e, part=part, base=base,
                            bank_=bank_: e.activation(
                                g.bufC[:, base + 2 * part * RL:
                                       base + 2 * part * RL + 512],
                                pk.t[bank_][:, 0:512], AF.Copy),
                            deps=dmms)
                pk.consumed(bank_, dev)
                bc_accum[sl].append(dev)
            if part == 7:
                bc_evs[sl] = list(bc_accum[sl])

        for _j in range(min(4, NIT)):
            emit_dsrc(_j)
        for _p in range(8):
            emit_bcast_part(0, _p)

        st1 = {}
        def front(i):
            sc, jd = divmod(i, 8)
            sl = i % 2
            par = sl * PAR1
            if sc + 1 < NSCH:
                emit_bcast_part(sc + 1, jd)
            if i + 4 < NIT:
                emit_dsrc(i + 4)
            sl4 = i % 4
            dlt = g.bufC[:, DSRC + sl4 * 256:DSRC + sl4 * 256 + 256]
            wv = g.bufC[:, WVS + sl * 256:WVS + sl * 256 + 256]
            tm = g.bufC[:, TM16 + sl * 16:TM16 + sl * 16 + 16]
            dw = bld.g("VEC", lambda e, jd=jd, sc=sc, wv=wv, dlt=dlt:
                       e.tensor_tensor(
                           wv, dlt,
                           g.xb[:, jd * T + sc * 256:jd * T + sc * 256
                                + 256], ALU.mult),
                       deps=[dsrc_ld[i]] + xc_evs + wv_free[sl])
            du = bld.g("VEC", lambda e, par=par, sc=sc, wv=wv:
                       e.tensor_tensor(
                           g.bufA[:, par + U0:par + U0 + NRUN]
                           .rearrange("p (n c) -> p n c", n=16),
                           wv.rearrange("p (o c) -> p o c", o=1)
                           .to_broadcast([128, 16, 256]),
                           g.bufC[:, (B0, B1)[sc % 2]:
                                  (B0, B1)[sc % 2] + NRUN]
                           .rearrange("p (n c) -> p n c", n=16),
                           ALU.mult),
                       deps=[dw] + bc_evs[sc % 2] + free_u[sl])
            dua = du
            bc_readers[sc % 2].append(du)
            dex = None
            exd = [dsrc_ld[i]] + [g.d_aneg] + free_a[sl]
            for n in range(NS):
                dex = bld.g("ACT", lambda e, li=li, jd=jd, n=n, par=par,
                            dlt=dlt: e.activation(
                                g.bufA[:, par + A0 + n * RL:
                                       par + A0 + n * RL + RL],
                                dlt, AF.Exp,
                                scale=P[:, PA_ANEG + (li * 8 + jd) * 16
                                        + n:PA_ANEG + (li * 8 + jd) * 16
                                        + n + 1]),
                            deps=exd if n == 0 else [])
            st1[i] = (sl, par, dex, [du, dua], dw)
            wv_free[sl] = [du, dua]
            dsrc_free[sl4] = [dw, dex]

        def carry_inject(i):
            sc, jd = divmod(i, 8)
            sl = i % 2
            par = sl * PAR1
            dex, du = st1[i][2], st1[i][3]
            tm = g.bufC[:, TM16 + sl * 16:TM16 + sl * 16 + 16]
            dc1 = bld.g("VEC", lambda e, par=par, jd=jd, tm=tm:
                        e.tensor_tensor(
                            tm,
                            bass.AP(g.bufA, par + A0,
                                    [[24832, 128], [RL, 16]]),
                            g.bufC[:, CARR + jd * 16:CARR + jd * 16 + 16],
                            ALU.mult),
                        deps=[dex] + du + carry_w[jd] + tm_war[sl])
            dc2 = bld.g("VEC", lambda e, par=par, tm=tm: e.tensor_tensor(
                bass.AP(g.bufA, par + U0, [[24832, 128], [RL, 16]]),
                bass.AP(g.bufA, par + U0, [[24832, 128], [RL, 16]]),
                tm, ALU.add), deps=[dc1] + du)
            dc3 = bld.g("VEC", lambda e, par=par: e.memset(
                bass.AP(g.bufA, par + A0, [[24832, 128], [RL, 16]]), 0.0),
                deps=[dc1])
            tm_war[sl] = [dc2]
            st2[i] = (dc2, dc3)

        def back_scan(i):
            sl, par = st1[i][0], st1[i][1]
            dc2, dc3 = st2[i]
            dscan = bld.g("VEC", lambda e, par=par: e.tensor_tensor_scan(
                g.bufA[:, par + H0:par + H0 + NRUN],
                g.bufA[:, par + A0:par + A0 + NRUN],
                g.bufA[:, par + U0:par + U0 + NRUN], 0.0,
                ALU.mult, ALU.add),
                deps=[dc2, dc3] + free_h[sl])
            st2[i] = dscan

        def back(i):
            sc, jd = divmod(i, 8)
            sl, par = st1[i][0], st1[i][1]
            dscan = st2[i]
            dco = bld.g("POOL", lambda e, jd=jd, par=par: e.tensor_copy(
                g.bufC[:, CARR + jd * 16:CARR + jd * 16 + 16],
                bass.AP(g.bufA, par + H0 + RL - 1,
                        [[24832, 128], [RL, 16]])),
                deps=[dscan])
            carry_w[jd] = [dco]
            dhc = bld.g("VEC", lambda e, par=par, sc=sc: e.tensor_tensor(
                g.bufA[:, par + U0:par + U0 + NRUN],
                g.bufA[:, par + H0:par + H0 + NRUN],
                g.bufC[:, (C0, C1)[sc % 2]:(C0, C1)[sc % 2] + NRUN],
                ALU.mult), deps=[dscan] + bc_evs[sc % 2])
            bc_readers[sc % 2].append(dhc)
            free_h[sl] = [dhc, dco]
            dl = dhc
            w = NRUN
            for _lv in range(4):
                w //= 2
                dl = bld.g("VEC", lambda e, par=par, w=w: e.tensor_add(
                    g.bufA[:, par + U0:par + U0 + w],
                    g.bufA[:, par + U0:par + U0 + w],
                    g.bufA[:, par + U0 + w:par + U0 + 2 * w]), deps=[dl])
            dy = bld.g("VEC", lambda e, jd=jd, sc=sc, li=li, par=par:
                       e.scalar_tensor_tensor(
                           g.bufB[:, jd * T + sc * 256:jd * T + sc * 256
                                  + 256],
                           g.xb[:, jd * T + sc * 256:jd * T + sc * 256
                                + 256],
                           P[:, PA_DP + li * 8 + jd:
                             PA_DP + li * 8 + jd + 1],
                           g.bufA[:, par + U0:par + U0 + RL],
                           ALU.mult, ALU.add), deps=[dl, g.d_params[4]])
            y_evs.append(dy)
            free_a[sl] = [dscan]
            free_u[sl] = [dy]
            return dl, dy, dhc

        carry_w = {jd: [dsi] for jd in range(8)}
        st2 = {}
        front(0)
        carry_inject(0)
        for i in range(NIT):
            if i + 1 < NIT:
                front(i + 1)
            back_scan(i)
            dl4, dy, dhc = back(i)
            if i + 1 < NIT:
                carry_inject(i + 1)
        tree_war = [y_evs[-1]]
        dwo = bld.dmas("POOL", lambda e, ds, li=li: e.dma_start(
            out=g.wbuf[:, 0:8192].rearrange("p (c m) -> p c m", c=8),
            in_=g.Wout[li].rearrange("(c p) m -> p c m", p=128)
            ).then_inc(ds, 16), 1, deps=proj_mms)
        yz_evs = []
        for jd in range(8):
            dzl = bld.dmas("SP", lambda e, ds, b=b, jd=jd: e.dma_start(
                out=g.bufC[:, (jd % 4) * 2048:(jd % 4) * 2048 + 2048],
                in_=g.z_dram[b, jd * 128:(jd + 1) * 128, :]
                ).then_inc(ds, 16), 1,
                deps=y_evs + [dl4, dy, dhc]
                + ([] if jd < 4 else [yz_evs[jd - 4]]))
            dzsl = bld.g("ACT", lambda e, jd=jd: e.activation(
                g.bufC[:, (jd % 4) * 2048:(jd % 4) * 2048 + 2048],
                g.bufC[:, (jd % 4) * 2048:(jd % 4) * 2048 + 2048],
                AF.Silu), deps=[dzl])
            dyz = bld.g("VEC", lambda e, jd=jd: e.tensor_tensor(
                g.bufB[:, jd * T:(jd + 1) * T],
                g.bufB[:, jd * T:(jd + 1) * T],
                g.bufC[:, (jd % 4) * 2048:(jd % 4) * 2048 + 2048],
                ALU.mult), deps=[dzsl] + y_evs)
            yz_evs.append(dyz)
        tail = []
        slot = 0
        for m in range(8):
            for k in range(8):
                for ts in range(4):
                    bank = ts
                    dd = []
                    if k == 0:
                        dd = [dwo] + yz_evs + pk.dep(bank)
                    dmm = bld.g("TEN", lambda e, m=m, ts=ts, k=k, bank=bank:
                                e.matmul(
                                    pk.t[bank][:, 0:512],
                                    g.wbuf[:, k * 1024 + m * 128:
                                           k * 1024 + m * 128 + 128],
                                    g.bufB[:, k * T + ts * 512:
                                           k * T + ts * 512 + 512],
                                    start=(k == 0), stop=(k == 7)),
                                deps=dd)
                    if k == 7:
                        dso, dst = res_evac(g, b, m, ts, bank, [], [dmm],
                                            slot)
                        slot = (slot + 1) % 4
                        tail.append(dso)
        prev = tail
        g.wbuf_free = tail
    g.mamba_tail = prev





def phaseT(g):
    bld, pk = g.bld, g.pk
    prev = g.mamba_tail
    g.sfin_deps = []
    for b in range(B):
        for tt in range(16):
            dl = bld.dmas("SP", lambda e, ds, b=b, tt=tt: e.dma_start(
                out=g.xtmp.rearrange("p (c t) -> p c t", c=8),
                in_=g.xs_dram[b].rearrange("(c p) t -> p c t", p=128)
                [:, :, tt * 128:(tt + 1) * 128]).then_inc(ds, 16), 1,
                deps=prev)
            evd = []
            for j in range(8):
                bank = j % 2
                dtr = bld.g("TEN", lambda e, j=j, bank=bank: e.transpose(
                    pk.t[bank][:, 0:128], g.xtmp[:, j * 128:(j + 1) * 128],
                    g.idf[:, :]), deps=[dl] + pk.dep(bank))
                dev = bld.g("VEC", lambda e, j=j, bank=bank: e.tensor_copy(
                    g.rowsc[:, j * 128:(j + 1) * 128],
                    pk.t[bank][:, 0:128]), deps=[dtr] + prev)
                pk.consumed(bank, dev)
                evd.append(dev)
            dst = bld.dmas("SP", lambda e, ds, b=b, tt=tt: e.dma_start(
                out=g.sfin_tm[b, tt * 128:(tt + 1) * 128, :],
                in_=g.rowsc[:, :]).then_inc(ds, 16), 1, deps=evd)
            prev = [dst] + evd
            g.sfin_deps.append(dst)


def moe_experts(g):
    bld, pk, C, P = g.bld, g.pk, g.consts, g.params
    DSPF, EOF_, EOTM, DSTM = 0, 5120, 10240, 15360
    H1 = 0
    prev = g.sfin_deps + g.d_scatter
    fc1_war, fc2_war = [], []
    bias_war = []
    for ex in range(EM):
        dbl = bld.dmas("SP", lambda e, ds, ex=ex: (
            e.dma_start(out=g.rowsc[0:32, 0:128],
                        in_=g.fc1b[ex].rearrange("(r p) -> r p", p=128)
                        ).then_inc(ds, 16),
            e.dma_start(out=g.rowsc[32:40, 0:128],
                        in_=g.fc2b[ex].rearrange("(r p) -> r p", p=128)
                        ).then_inc(ds, 16))[-1], 2,
            deps=prev + bias_war)
        dbt = bld.g("TEN", lambda e: e.transpose(
            pk.t[4][:, 0:64], g.rowsc[0:64, 0:128], g.idf[0:64, 0:64]),
            deps=[dbl] + pk.dep(4))
        dbe = bld.g("VEC", lambda e: e.tensor_copy(
            g.xtmp[:, 0:64], pk.t[4][:, 0:64]), deps=[dbt])
        pk.consumed(4, dbe)
        bias_war = [dbt, dbe]

        ddl = bld.dmas("SP", lambda e, ds, ex=ex: [
            e.dma_start(out=g.bufA[:, DSTM + s * 1024:DSTM + (s + 1) * 1024],
                        in_=g.disp_dram[ex * CLOC + s * 128:
                                        ex * CLOC + (s + 1) * 128, :]
                        ).then_inc(ds, 16) for s in range(5)][-1], 5,
            deps=prev + g.d_scatter)
        df_evs = []
        for s in range(5):
            for j in range(8):
                bank = 6 + ((s * 8 + j) % 2)
                dtr = bld.g("TEN", lambda e, s=s, j=j, bank=bank:
                            e.transpose(
                                pk.t[bank][:, 0:128],
                                g.bufA[:, DSTM + s * 1024 + j * 128:
                                       DSTM + s * 1024 + j * 128 + 128],
                                C[:, ID_B:ID_B + 128]),
                            deps=[ddl, g.d_masks] + pk.dep(bank))
                dev = bld.g("VEC", lambda e, s=s, j=j, bank=bank:
                            e.tensor_copy(
                                g.bufA[:, DSPF + j * 640 + s * 128:
                                       DSPF + j * 640 + s * 128 + 128],
                                pk.t[bank][:, 0:128]), deps=[dtr])
                pk.consumed(bank, dev)
                df_evs.append(dev)

        h1_evs = []
        for half in range(2):
            dw1 = bld.dmas("POOL", lambda e, ds, ex=ex, half=half:
                           e.dma_start(
                               out=g.xb.rearrange("p (c m) -> p c m", c=8),
                               in_=g.fc1T[ex].rearrange(
                                   "(c p) m -> p c m", p=128)
                               [:, :, half * 2048:half * 2048 + 2048]
                               ).then_inc(ds, 16), 1,
                           deps=(prev + fc1_war) if (ex == 0 and half == 0)
                           else fc1_war)
            myw = []
            for fb in range(16):
                fg = half * 16 + fb
                for ss in range(2):
                    bank = (fb * 2 + ss) % 2
                    dmm = None
                    for k in range(8):
                        dd = ([dw1] + df_evs + pk.dep(bank)) if k == 0 \
                            else []
                        dmm = bld.g("TEN", lambda e, fb=fb, ss=ss, k=k,
                                    bank=bank: e.matmul(
                                        pk.t[bank][:, 0:320],
                                        g.xb[:, k * 2048 + fb * 128:
                                             k * 2048 + fb * 128 + 128],
                                        g.bufA[:, DSPF + k * 640 + ss * 320:
                                               DSPF + k * 640 + ss * 320
                                               + 320],
                                        start=(k == 0), stop=(k == 7)),
                                    deps=dd)
                    myw.append(dmm)
                    dge = bld.g("ACT", lambda e, fg=fg, ss=ss, bank=bank:
                                e.activation(
                                    g.bufC[:, H1 + fg * 640 + ss * 320:
                                           H1 + fg * 640 + ss * 320 + 320],
                                    pk.t[bank][:, 0:320], AF.Gelu,
                                    bias=g.xtmp[:, fg:fg + 1]),
                                deps=[dmm, dbe])
                    pk.consumed(bank, dge)
                    h1_evs.append(dge)
            fc1_war = myw
        eo_evs = []
        for mh in range(2):
            dw2 = bld.dmas("POOL", lambda e, ds, ex=ex, mh=mh: e.dma_start(
                out=g.bufB.rearrange("p (c m) -> p c m", c=32),
                in_=g.fc2T[ex].rearrange("(c p) m -> p c m", p=128)
                [:, :, mh * 512:mh * 512 + 512]).then_inc(ds, 16), 1,
                deps=fc2_war if (ex or mh) else prev)
            myw = []
            for mi in range(4):
                m = mh * 4 + mi
                for ss in range(2):
                    bank = 2 + ((mi * 2 + ss) % 2)
                    dmm = None
                    for k in range(32):
                        dd = ([dw2] + h1_evs + pk.dep(bank)) if k == 0 \
                            else []
                        dmm = bld.g("TEN", lambda e, mi=mi, ss=ss, k=k,
                                    bank=bank: e.matmul(
                                        pk.t[bank][:, 0:320],
                                        g.bufB[:, k * 512 + mi * 128:
                                               k * 512 + mi * 128 + 128],
                                        g.bufC[:, H1 + k * 640 + ss * 320:
                                               H1 + k * 640 + ss * 320
                                               + 320],
                                        start=(k == 0), stop=(k == 31)),
                                    deps=dd)
                    myw.append(dmm)
                    dev = bld.g("VEC", lambda e, ex=ex, m=m, ss=ss,
                                bank=bank: e.tensor_scalar(
                                    g.bufA[:, EOF_ + m * 640 + ss * 320:
                                           EOF_ + m * 640 + ss * 320 + 320],
                                    pk.t[bank][:, 0:320],
                                    g.xtmp[:, 32 + m:33 + m], None,
                                    ALU.add), deps=[dmm, dbe])
                    pk.consumed(bank, dev)
                    eo_evs.append(dev)
            fc2_war = myw
        st_evs = []
        for s in range(5):
            tev = []
            for j in range(8):
                bank = 6 + ((s * 8 + j) % 2)
                dtr = bld.g("TEN", lambda e, s=s, j=j, bank=bank:
                            e.transpose(
                                pk.t[bank][:, 0:128],
                                g.bufA[:, EOF_ + j * 640 + s * 128:
                                       EOF_ + j * 640 + s * 128 + 128],
                                C[:, ID_B:ID_B + 128]),
                            deps=eo_evs + pk.dep(bank))
                dev = bld.g("VEC", lambda e, s=s, j=j, bank=bank:
                            e.tensor_copy(
                                g.bufA[:, EOTM + s * 1024 + j * 128:
                                       EOTM + s * 1024 + j * 128 + 128],
                                pk.t[bank][:, 0:128]), deps=[dtr])
                pk.consumed(bank, dev)
                tev.append(dev)
            dst = bld.dmas("SP", lambda e, ds, ex=ex, s=s: e.dma_start(
                out=g.rows_dram[ex * CLOC + s * 128:
                                ex * CLOC + (s + 1) * 128, :],
                in_=g.bufA[:, EOTM + s * 1024:EOTM + (s + 1) * 1024]
                ).then_inc(ds, 16), 1, deps=tev)
            st_evs.append(dst)
        prev = st_evs
    g.moe_tail = prev


def combine(g):
    bld = g.bld
    prev = g.moe_tail
    g1w, g2w = [], []
    for b in range(B):
        for tt in range(16):
            ti = b * 16 + tt
            dlx = bld.dmas("SP", lambda e, ds, b=b, tt=tt: (
                e.dma_start(out=g.rowsc[:, :],
                            in_=g.x_in[b, tt * 128:(tt + 1) * 128, :]
                            ).then_inc(ds, 16),
                e.dma_start(out=g.xtmp[:, :],
                            in_=g.sfin_tm[b, tt * 128:(tt + 1) * 128, :]
                            ).then_inc(ds, 16))[-1], 2,
                deps=prev + g1w)
            dg = bld.dmas("POOL", lambda e, ds, ti=ti: (
                e.indirect_dma_start(
                    out=g.bufA[:, 0:1024], out_offset=None,
                    in_=g.rows_dram,
                    in_offset=bass.IndirectOffsetOnAxis(
                        g.metai[:, ti * 2:ti * 2 + 1], 0)
                    ).then_inc(ds, 16),
                e.indirect_dma_start(
                    out=g.bufA[:, 1024:2048], out_offset=None,
                    in_=g.rows_dram,
                    in_offset=bass.IndirectOffsetOnAxis(
                        g.metai[:, ti * 2 + 1:ti * 2 + 2], 0)
                    ).then_inc(ds, 16))[-1], 2,
                deps=prev + g2w)
            dc = bld.g("VEC", lambda e, ti=ti: (
                e.tensor_add(g.rowsc[:, :], g.rowsc[:, :], g.xtmp[:, :]),
                e.scalar_tensor_tensor(
                    g.rowsc[:, :], g.bufA[:, 0:1024],
                    g.meta[:, 1024 + ti * 2:1024 + ti * 2 + 1],
                    g.rowsc[:, :], ALU.mult, ALU.add),
                e.scalar_tensor_tensor(
                    g.rowsc[:, :], g.bufA[:, 1024:2048],
                    g.meta[:, 1024 + ti * 2 + 1:1024 + ti * 2 + 2],
                    g.rowsc[:, :], ALU.mult, ALU.add))[-1],
                deps=[dlx, dg])
            dso = bld.dmas("SP", lambda e, ds, b=b, tt=tt: e.dma_start(
                out=g.out[b, tt * 128:(tt + 1) * 128, :],
                in_=g.rowsc[:, :]).then_inc(ds, 16), 1, deps=[dc])
            g1w = [dc, dso]
            g2w = [dc]
            prev = []


# version 24
# speedup vs baseline: 1.1999x; 1.1999x over previous
import numpy as np
import concourse.bass as bass
import concourse.mybir as mybir
from concourse.bass_utils import run_bass_kernel_spmd
from contextlib import ExitStack

F32 = mybir.dt.float32
BF16 = mybir.dt.bfloat16
I32 = mybir.dt.int32
AF = mybir.ActivationFunctionType
ALU = mybir.AluOpType
AX = mybir.AxisListType

NCORE = 8
B = 2
T = 2048
D = 1024
NT = B * T
E = 16
EM = 12
CAP = 4096
CLOC = 640
DFF = 2752
FFN = 4096
NH = 16
HD = 64
NS = 16
NM = 3
ROW_ZERO = EM * CLOC
ROW_CV = ROW_ZERO + 1
ROW_XN = ROW_ZERO + 3
NROWS = ROW_XN + NT
SCH = 256
NSCH = T // SCH
RUNL = 260
EPSI = 1e-6
EPSM = 1e-5

TAPS = {}


class Sem:
    __slots__ = ("name", "tot", "h")

    def __init__(self, name):
        self.name = name
        self.tot = 0
        self.h = None


class Bld:
    ENGS = ("TEN", "VEC", "ACT", "POOL", "SP")

    def __init__(self):
        self.ops = {e: [] for e in self.ENGS}
        self.done = {e: Sem("dn" + e) for e in self.ENGS}
        self.dma = {e: Sem("dm" + e) for e in ("SP", "POOL", "ACT")}
        self.bar = Sem("bar")
        self.nbar = 0

    def all_sems(self):
        return list(self.done.values()) + list(self.dma.values()) + [self.bar]

    @staticmethod
    def _dedup(deps):
        m = {}
        for item in deps:
            if item is None:
                continue
            s, v = item
            if s is None or v <= 0:
                continue
            if id(s) not in m or m[id(s)][1] < v:
                m[id(s)] = (s, v)
        return tuple(m.values())

    def g(self, eng, emitfn, deps=()):
        d = self.done[eng]
        d.tot += 1
        deps = self._dedup(deps)

        def f(e, deps=deps, emitfn=emitfn, d=d):
            for s, v in deps:
                e.wait_ge(s.h, v)
            emitfn(e).then_inc(d.h, 1)
        self.ops[eng].append(f)
        return (d, d.tot)

    def dmas(self, eng, emitfn, n, deps=()):
        ds = self.dma[eng]
        ds.tot += 16 * n
        deps = self._dedup(deps)

        def f(e, deps=deps, emitfn=emitfn, ds=ds):
            for s, v in deps:
                e.wait_ge(s.h, v)
            emitfn(e, ds.h)
        self.ops[eng].append(f)
        return (ds, ds.tot)

    def now(self, eng):
        return (self.done[eng], self.done[eng].tot)

    def dma_now(self, eng):
        return (self.dma[eng], self.dma[eng].tot)

    def barrier(self):
        self.nbar += 1
        target = 5 * self.nbar
        waits = [(s, s.tot) for s in
                 [self.dma[e] for e in ("SP", "POOL", "ACT")]
                 + [self.done[e] for e in self.ENGS] if s.tot > 0]
        bar = self.bar
        for eng in self.ENGS:
            def f(e, waits=tuple(waits), bar=bar, target=target):
                for s, v in waits:
                    e.wait_ge(s.h, v)
                e.nop().then_inc(bar.h, 1)
                e.wait_ge(bar.h, target)
            self.ops[eng].append(f)


class Banks:
    def __init__(self):
        self.t = [None] * 8
        self.war = [None] * 8

    def dep(self, i):
        return [self.war[i]] if self.war[i] is not None else []

    def consumed(self, i, dep):
        self.war[i] = dep


def kernel(**inp):
    x_full = np.ascontiguousarray(inp["L0_attn_x"], dtype=np.float32)
    fc1_t = np.ascontiguousarray(np.transpose(
        np.asarray(inp["L0_ffn_fc1_weight"], np.float32), (0, 2, 1)))
    fc2_t = np.ascontiguousarray(np.transpose(
        np.asarray(inp["L0_ffn_fc2_weight"], np.float32), (0, 2, 1)))
    wr_t = np.ascontiguousarray(
        np.asarray(inp["L0_ffn_router_weight"], np.float32).T)
    common = {
        "attnw": np.asarray(inp["L0_attn_attn_norm_w"], np.float32),
        "ffnw": np.asarray(inp["L0_attn_ffn_norm_w"], np.float32),
        "Wq": np.asarray(inp["L0_attn_Wq"], np.float32)[0],
        "Wk": np.asarray(inp["L0_attn_Wk"], np.float32)[0],
        "Wv": np.asarray(inp["L0_attn_Wv"], np.float32)[0],
        "Wo": np.asarray(inp["L0_attn_Wo"], np.float32)[0],
        "Wg": np.asarray(inp["L0_attn_Wg"], np.float32)[0],
        "Wu": np.asarray(inp["L0_attn_Wu"], np.float32)[0],
        "Wd": np.asarray(inp["L0_attn_Wd"], np.float32)[0],
        "mnw": np.asarray(inp["L0_attn_m_norm_w"], np.float32),
        "Win": np.asarray(inp["L0_attn_Win"], np.float32),
        "convw": np.asarray(inp["L0_attn_conv_w"], np.float32),
        "convb": np.asarray(inp["L0_attn_conv_b"], np.float32),
        "Wx": np.asarray(inp["L0_attn_Wx"], np.float32),
        "Alog": np.asarray(inp["L0_attn_A_log"], np.float32),
        "Dp": np.asarray(inp["L0_attn_Dp"], np.float32),
        "Wout": np.asarray(inp["L0_attn_Wout"], np.float32),
        "nw": np.asarray(inp["L0_ffn_norm_weight"], np.float32),
        "WrT": wr_t,
        "rb": np.asarray(inp["L0_ffn_router_bias"], np.float32).reshape(1, E),
        "fc1T": fc1_t,
        "fc1b": np.asarray(inp["L0_ffn_fc1_bias"], np.float32),
        "fc2T": fc2_t,
        "fc2b": np.asarray(inp["L0_ffn_fc2_bias"], np.float32),
        "cv": np.asarray(inp["L0_ffn_constant_vectors"], np.float32),
    }
    common = {k: np.ascontiguousarray(v) for k, v in common.items()}

    nc = build_program()
    in_maps = []
    for c in range(NCORE):
        m = dict(common)
        m["x_in"] = np.ascontiguousarray(x_full[2 * c:2 * c + 2])
        pf = np.zeros((8, 1), np.float32)
        pf[:c] = 1.0
        m["corepfx"] = pf
        in_maps.append(m)
    res = run_bass_kernel_spmd(nc, in_maps, core_ids=list(range(NCORE)))
    out = np.concatenate(
        [np.asarray(res.results[c]["out"]) for c in range(NCORE)], axis=0)
    if TAPS:
        kernel.last_taps = [dict(res.results[c]) for c in range(NCORE)]
    return np.ascontiguousarray(out.astype(np.float32).reshape(16, T, D))


class G:
    pass


ID_B, TRILX, TRIA, ONES, MSK4 = 0, 128, 256, 384, 512
PA_ATTNW, PA_FFNW, PA_MNW, PA_CONVB, PA_DP = 0, 8, 16, 40, 64
PA_ANEG, PA_IOTA16, PA_IOTA16B, PA_NW = 88, 472, 488, 504
PA_CONVW, PA_RB, PA_CPFX, PA_ONESF = 512, 608, 624, 626
PA_CNTALL, PA_IOTAP, PA_WR = 640, 656, 672
PA_CONSTS = 658
CONST_VALS = (1e-5, 1e-6, -0.5, -1.0, 0.125, 1.0 / 1024.0)


def build_program():
    nc = bass.Bass("TRN2", target_bir_lowering=False)
    g = G()
    g.nc = nc
    A = lambda t: t.ap()

    g.x_in = A(nc.dram_tensor("x_in", [B, T, D], F32, kind="ExternalInput"))
    for nm, shp in [("attnw", [1, D]), ("ffnw", [1, D]), ("Wq", [D, D]),
                    ("Wk", [D, 256]), ("Wv", [D, 256]), ("Wo", [D, D]),
                    ("Wg", [D, DFF]), ("Wu", [D, DFF]), ("Wd", [DFF, D]),
                    ("mnw", [NM, D]), ("Win", [NM, D, 2 * D]),
                    ("convw", [NM, D, 4]), ("convb", [NM, D]),
                    ("Wx", [NM, D, D + 32]), ("Alog", [NM, D, NS]),
                    ("Dp", [NM, D]), ("Wout", [NM, D, D]), ("nw", [D]),
                    ("WrT", [D, E]), ("rb", [1, E]),
                    ("fc1T", [EM, D, FFN]), ("fc1b", [EM, FFN]),
                    ("fc2T", [EM, FFN, D]), ("fc2b", [EM, D]),
                    ("cv", [2, D]), ("corepfx", [8, 1])]:
        setattr(g, nm, A(nc.dram_tensor(nm, shp, F32, kind="ExternalInput")))
    g.out = A(nc.dram_tensor("out", [B, T, D], F32, kind="ExternalOutput"))

    g.xs_dram = A(nc.dram_tensor("xs_dram", [B, D, T], F32, kind="Internal"))
    g.sfin_tm = A(nc.dram_tensor("sfin_tm", [B, T, D], F32, kind="Internal"))
    g.z_dram = A(nc.dram_tensor("z_dram", [B, D, T], BF16, kind="Internal"))
    g.rows_dram = A(nc.dram_tensor("rows_dram", [NROWS, D], BF16,
                                   kind="Internal"))
    g.disp_dram = A(nc.dram_tensor("disp_dram", [EM * CLOC + 1, D], BF16,
                                   kind="Internal"))
    g.bounce = A(nc.dram_tensor("bounce", [128, E], F32, kind="Internal"))
    g.proj_dram = A(nc.dram_tensor("proj_dram", [1024, T], BF16,
                                   kind="Internal"))
    g.cc_in = A(nc.dram_tensor("cc_in", [1, E], F32, kind="Internal"))
    g.cc_out = A(nc.dram_tensor("cc_out", [8, E], F32, kind="Internal"))

    g.taps = {}
    for name, (shape, dt) in TAPS.items():
        g.taps[name] = A(nc.dram_tensor("tap_" + name, list(shape), dt,
                                        kind="ExternalOutput"))

    bld = Bld()
    g.bld = bld
    es = ExitStack()
    with es:
        sb = lambda n, s, d: es.enter_context(nc.sbuf_tensor(n, s, d))
        g.xb = sb("xb", [128, 16384], BF16)
        g.bufA = sb("bufA", [128, 24832], BF16)
        g.bufB = sb("bufB", [128, 16384], BF16)
        g.bufC = sb("bufC", [128, 22528], BF16)
        g.wbuf = sb("wbuf", [128, 8192], BF16)
        g.statr = sb("statr", [128, 2048], F32)
        g.rowsc = sb("rowsc", [128, 1024], F32)
        g.xtmp = sb("xtmp", [128, 1024], F32)
        g.meta = sb("meta", [128, 1280], F32)
        g.rmeta = sb("rmeta", [128, 512], F32)
        g.metai = sb("metai", [128, 256], I32)
        g.consts = sb("consts", [128, 512], BF16)
        g.params = sb("params", [128, 1664], F32)
        g.idf = sb("idf", [128, 128], F32)

        pk = Banks()
        for i in range(6):
            pk.t[i] = es.enter_context(
                nc.psum_tensor("pb%d" % i, [128, 512], F32))
        for i in range(6, 8):
            pk.t[i] = es.enter_context(
                nc.psum_tensor("pb%d" % i, [128, 1024], BF16))
        g.pk = pk

        for s in bld.all_sems():
            s.h = es.enter_context(nc.semaphore(s.name))
        blk = es.enter_context(nc.Block())

        build_all(g)

        def fin(e):
            for s in (bld.dma["SP"], bld.dma["POOL"], bld.dma["ACT"]):
                if s.tot:
                    e.wait_ge(s.h, s.tot)
            for en in Bld.ENGS:
                if en != "SP" and bld.done[en].tot:
                    e.wait_ge(bld.done[en].h, bld.done[en].tot)
        bld.ops["SP"].append(fin)

        @blk.tensor
        def _(eng):
            for f in bld.ops["TEN"]:
                f(eng)

        @blk.vector
        def _(eng):
            for f in bld.ops["VEC"]:
                f(eng)

        @blk.scalar
        def _(eng):
            for f in bld.ops["ACT"]:
                f(eng)

        @blk.gpsimd
        def _(eng):
            for f in bld.ops["POOL"]:
                f(eng)

        @blk.sync
        def _(eng):
            for f in bld.ops["SP"]:
                f(eng)
    return nc


import os as _os
KPHASES = int(_os.environ.get("KPHASES", "99"))


def _final_taps(g):
    if "xsd" in g.taps:
        g.bld.dmas("SP", lambda e, ds: e.dma_start(
            out=g.taps["xsd"], in_=g.xs_dram).then_inc(ds, 16), 1,
            deps=[g.bld.now("VEC"), g.bld.now("ACT"), g.bld.now("TEN"),
                  g.bld.dma_now("SP")])
    if "sfin" in g.taps:
        g.bld.dmas("SP", lambda e, ds: e.dma_start(
            out=g.taps["sfin"], in_=g.sfin_tm).then_inc(ds, 16), 1,
            deps=[g.bld.now("VEC"), g.bld.now("TEN"), g.bld.dma_now("SP")])
    if "rowsall" in g.taps:
        g.bld.dmas("SP", lambda e, ds: e.dma_start(
            out=g.taps["rowsall"], in_=g.rows_dram).then_inc(ds, 16), 1,
            deps=[g.bld.now("VEC"), g.bld.dma_now("SP"),
                  g.bld.dma_now("POOL")])


def build_all(g):
    def cinit(e):
        for i, v in enumerate(CONST_VALS):
            last = e.memset(g.params[:, PA_CONSTS + i:PA_CONSTS + i + 1],
                            float(v))
        return last
    g.d_cinit = g.bld.g("VEC", cinit)
    for i, v in enumerate(CONST_VALS):
        g.nc.const_aps.register(
            F32, float(v), g.params[:, PA_CONSTS + i:PA_CONSTS + i + 1])
    g.bld.g("ACT", lambda e: e.activation(
        g.params[:, PA_CONSTS:PA_CONSTS + 1],
        g.params[:, PA_CONSTS:PA_CONSTS + 1], AF.Copy), deps=[g.d_cinit])
    prep(g)
    if KPHASES < 2:
        return _final_taps(g)
    phaseR(g)
    g.bld.barrier()
    if KPHASES < 3:
        return _final_taps(g)
    inner_attn(g)
    g.bld.barrier()
    if KPHASES < 4:
        return _final_taps(g)
    inner_ffn(g)
    g.bld.barrier()
    if KPHASES < 5:
        return _final_taps(g)
    for li in range(NM):
        inner_mamba(g, li)
        g.bld.barrier()
        if KPHASES < 5 + li + 1:
            return _final_taps(g)
    phaseT(g)
    g.bld.barrier()
    if KPHASES < 9:
        return _final_taps(g)
    moe_experts(g)
    g.bld.barrier()
    if KPHASES < 10:
        return _final_taps(g)
    combine(g)
    _final_taps(g)


def chain(g, eng, fns, deps):
    d = None
    for i, fn in enumerate(fns):
        d = g.bld.g(eng, fn, deps=(list(deps) if i == 0 else [d]))
    return d


def tap_store(g, name, src_ap, deps):
    if name not in g.taps:
        return None
    dst = g.taps[name]

    def f(e, ds, dst=dst, src=src_ap):
        e.dma_start(out=dst, in_=src).then_inc(ds, 16)
    return g.bld.dmas("SP", f, 1, deps=deps)


def bcast_row_bf(g, src_row, dst, bank, deps, nparts=128):
    pk, bld = g.pk, g.bld
    W = src_row.shape[-1]
    pt = pk.t[bank]
    dmm = bld.g("TEN", lambda e, s=src_row, pt=pt, n=nparts, W=W: e.matmul(
        pt[0:n, 0:W], g.consts[0:1, ONES:ONES + n], s, start=True, stop=True),
        deps=list(deps) + pk.dep(bank) + [g.d_masks])
    dev = bld.g("VEC", lambda e, pt=pt, d=dst, n=nparts, W=W: e.tensor_copy(
        d, pt[0:n, 0:W]), deps=[dmm])
    pk.consumed(bank, dev)
    return dev


def bcast_row_f32(g, src_row, dst, bank, deps, nparts=128):
    pk, bld = g.pk, g.bld
    W = src_row.shape[-1]
    pt = pk.t[bank]
    bp = src_row.base_partition()
    ones = g.params[bp:bp + 1, PA_ONESF:PA_ONESF + 1].to_broadcast([1, nparts])
    dmm = bld.g("TEN", lambda e, s=src_row, pt=pt, n=nparts, W=W, o=ones:
                e.matmul(pt[0:n, 0:W], o, s, start=True, stop=True),
                deps=list(deps) + pk.dep(bank) + [g.d_onesf])
    dev = bld.g("VEC", lambda e, pt=pt, d=dst, n=nparts, W=W: e.tensor_copy(
        d, pt[0:n, 0:W]), deps=[dmm])
    pk.consumed(bank, dev)
    return dev


def prep(g):
    bld, C, P = g.bld, g.consts, g.params

    def masks(e):
        e.memset(C[:, ID_B:ID_B + 128], 0.0)
        e.affine_select(out=C[:, ID_B:ID_B + 128], in_=C[:, ID_B:ID_B + 128],
                        compare_op=ALU.not_equal, fill=1.0, base=0,
                        pattern=[[-1, 128]], channel_multiplier=1)
        e.memset(C[:, TRILX:TRILX + 128], 1.0)
        e.affine_select(out=C[:, TRILX:TRILX + 128],
                        in_=C[:, TRILX:TRILX + 128],
                        compare_op=ALU.is_ge, fill=0.0, base=-1,
                        pattern=[[1, 128]], channel_multiplier=-1)
        e.memset(C[:, ONES:ONES + 128], 1.0)
        e.memset(g.idf[:, :], 0.0)
        e.affine_select(out=g.idf[:, :], in_=g.idf[:, :],
                        compare_op=ALU.not_equal, fill=1.0, base=0,
                        pattern=[[-1, 128]], channel_multiplier=1)
        e.iota(P[:, PA_IOTA16:PA_IOTA16 + 16], pattern=[[1, 16]],
               base=0, channel_multiplier=0,
               allow_small_or_imprecise_dtypes=True)
        e.iota(P[:, PA_IOTA16B:PA_IOTA16B + 16], pattern=[[1, 16]],
               base=1000000, channel_multiplier=0,
               allow_small_or_imprecise_dtypes=True)
        return e.iota(P[:, PA_IOTAP:PA_IOTAP + 1], pattern=[[1, 1]],
                      base=0, channel_multiplier=1,
                      allow_small_or_imprecise_dtypes=True)
    g.d_masks = bld.g("POOL", masks)
    g.d_onesf = bld.g("VEC", lambda e: e.memset(
        P[:, PA_ONESF:PA_ONESF + 1], 1.0))

    def amasks(e):
        for i in range(4):
            off = i * 128
            e.memset(g.bufC[:, 16448 + i * 512:16448 + (i + 1) * 512], 1.0)
            last = e.affine_select(
                out=g.bufC[:, 16448 + i * 512:16448 + (i + 1) * 512],
                in_=g.bufC[:, 16448 + i * 512:16448 + (i + 1) * 512],
                compare_op=ALU.is_ge, fill=0.0, base=-off,
                pattern=[[1, 512]], channel_multiplier=-1)
        return last
    g.d_amasks = bld.g("POOL", amasks)

    def tload(src_ap, rows, dst_col, extra=()):
        dl = bld.dmas("SP", lambda e, ds, s=src_ap, r=rows: e.dma_start(
            out=g.xtmp[0:r, 0:128], in_=s).then_inc(ds, 16), 1,
            deps=list(extra))
        bank = 4
        dmm = bld.g("TEN", lambda e, r=rows: e.transpose(
            g.pk.t[bank][:, 0:r], g.xtmp[0:r, 0:128], g.idf[0:r, 0:r]),
            deps=[dl, g.d_masks] + g.pk.dep(bank))
        dev = bld.g("VEC", lambda e, r=rows, c=dst_col: e.tensor_copy(
            P[:, c:c + r], g.pk.t[bank][:, 0:r]), deps=[dmm])
        g.pk.consumed(bank, dev)
        return dev, dmm

    prev = ()
    loads = [(g.attnw.rearrange("o (r p) -> (o r) p", p=128), 8, PA_ATTNW),
             (g.ffnw.rearrange("o (r p) -> (o r) p", p=128), 8, PA_FFNW),
             (g.mnw.rearrange("l (r p) -> (l r) p", p=128), 24, PA_MNW),
             (g.convb.rearrange("l (r p) -> (l r) p", p=128), 24, PA_CONVB),
             (g.Dp.rearrange("l (r p) -> (l r) p", p=128), 24, PA_DP),
             (g.nw.rearrange("(r p) -> r p", p=128), 8, PA_NW)]
    pdeps = []
    for src, rows, col in loads:
        dev, dmm = tload(src, rows, col, extra=prev)
        prev = (dmm,)
        pdeps.append(dev)
    g.d_params = pdeps
    g.d_xtmp_free = prev

    def cwf(e, ds):
        for l in range(NM):
            for r in range(8):
                c = PA_CONVW + (l * 8 + r) * 4
                e.dma_start(out=P[:, c:c + 4],
                            in_=g.convw[l, r * 128:(r + 1) * 128, :]
                            ).then_inc(ds, 16)
    g.d_convw = bld.dmas("SP", cwf, 24)

    def alf(e, ds):
        for l in range(NM):
            for r in range(8):
                c = PA_ANEG + (l * 8 + r) * 16
                e.dma_start(out=P[:, c:c + 16],
                            in_=g.Alog[l, r * 128:(r + 1) * 128, :]
                            ).then_inc(ds, 16)
    dal = bld.dmas("SP", alf, 24)
    dae = bld.g("ACT", lambda e: e.activation(
        P[:, PA_ANEG:PA_ANEG + 384], P[:, PA_ANEG:PA_ANEG + 384], AF.Exp),
        deps=[dal])
    g.d_aneg = bld.g("VEC", lambda e: e.tensor_scalar_mul(
        P[:, PA_ANEG:PA_ANEG + 384], P[:, PA_ANEG:PA_ANEG + 384], -1.0),
        deps=[dae])

    drb = bld.dmas("SP", lambda e, ds: e.dma_start(
        out=g.rowsc[0:1, 0:E], in_=g.rb).then_inc(ds, 16), 1)
    g.d_rb = bcast_row_f32(g, g.rowsc[0:1, 0:E], P[:, PA_RB:PA_RB + E], 5,
                           deps=[drb])
    dz = bld.g("VEC", lambda e: (e.memset(P[:, PA_CPFX:PA_CPFX + 1], 0.0),
                                 e.memset(P[:, PA_CNTALL:PA_CNTALL + E], 0.0)
                                 )[-1])
    g.d_cpfx = bld.dmas("SP", lambda e, ds: e.dma_start(
        out=P[0:8, PA_CPFX:PA_CPFX + 1], in_=g.corepfx).then_inc(ds, 16), 1,
        deps=[dz])
    g.d_cntz = dz

    dzt = bld.g("VEC", lambda e: e.memset(g.bufB[:, 0:1024], 0.0))

    def zf(e, ds):
        for i in range(EM * CLOC // 128):
            e.dma_start(out=g.disp_dram[i * 128:(i + 1) * 128, :],
                        in_=g.bufB[:, 0:1024]).then_inc(ds, 16)
        e.dma_start(out=g.rows_dram[ROW_ZERO:ROW_ZERO + 1, :],
                    in_=g.bufB[0:1, 0:1024]).then_inc(ds, 16)
    g.d_dispzero = bld.dmas("SP", zf, EM * CLOC // 128 + 1, deps=[dzt])

    dcv = bld.dmas("SP", lambda e, ds: e.dma_start(
        out=g.rowsc[0:2, 0:D], in_=g.cv).then_inc(ds, 16), 1, deps=[g.d_rb])
    dcvc = bld.g("VEC", lambda e: e.tensor_copy(
        g.bufB[0:2, 1024:2048], g.rowsc[0:2, 0:D]), deps=[dcv, dzt])
    g.d_cvrows = bld.dmas("SP", lambda e, ds: e.dma_start(
        out=g.rows_dram[ROW_CV:ROW_CV + 2, :],
        in_=g.bufB[0:2, 1024:2048]).then_inc(ds, 16), 1, deps=[dcvc])

    def wrf(e, ds):
        e.dma_start(out=g.params[:, PA_WR:PA_WR + 128].rearrange(
            "p (c n) -> p c n", c=8),
            in_=g.WrT.rearrange("(c p) n -> p c n", p=128)).then_inc(ds, 16)
    dwr = bld.dmas("SP", wrf, 1)
    g.d_wr = bld.g("VEC", lambda e: e.tensor_tensor(
        g.params[:, PA_WR:PA_WR + 128].rearrange("p (c n) -> p c n", c=8),
        g.params[:, PA_WR:PA_WR + 128].rearrange("p (c n) -> p c n", c=8),
        g.params[:, PA_NW:PA_NW + 8].rearrange("p (c o) -> p c o", o=1)
        .to_broadcast([128, 8, 16]), ALU.mult), deps=[dwr, pdeps[5]])

    dnw = bld.dmas("SP", lambda e, ds: e.dma_start(
        out=g.statr[0:1, 0:D], in_=g.nw.rearrange("(o d) -> o d", o=1)
        ).then_inc(ds, 16), 1)
    dn1 = bcast_row_f32(g, g.statr[0:1, 0:512], g.bufA[:, 23552:24064], 5,
                        deps=[dnw])
    dn2 = bcast_row_f32(g, g.statr[0:1, 512:1024], g.bufA[:, 24064:24576], 5,
                        deps=[dnw])
    g.d_nwrow = [dn1, dn2]


def phaseR(g):
    bld, pk, C, P = g.bld, g.pk, g.consts, g.params
    OHF1, OHF2, CNTC, SQS, ROWS = 0, 1024, 2048, 3072, 4096
    prev_tail = [g.d_wr, g.d_cvrows, g.d_nwrow[0], g.d_nwrow[1],
                 g.d_xtmp_free[0] if g.d_xtmp_free else None]
    prev_tail = [d for d in prev_tail if d]
    g.tile_last = {}

    for b in range(B):
        for tt in range(16):
            ti = b * 16 + tt
            dl = bld.dmas("SP", lambda e, ds, b=b, tt=tt: e.dma_start(
                out=g.rowsc[:, :],
                in_=g.x_in[b, tt * 128:(tt + 1) * 128, :]).then_inc(ds, 16),
                1, deps=prev_tail)
            ds1 = bld.g("VEC", lambda e: e.reduce_sum(
                g.rmeta[:, 384:385], g.rowsc[:, :], AX.X), deps=[dl])
            ds2 = bld.g("ACT", lambda e: e.activation(
                g.bufC[:, SQS:SQS + 1024], g.rowsc[:, :], AF.Square,
                accum_out=g.rmeta[:, 385:386]), deps=[dl])

            dv = chain(g, "VEC", [
                lambda e: e.tensor_scalar_mul(
                    g.rmeta[:, 386:387], g.rmeta[:, 384:385], 1.0 / D),
                lambda e: e.tensor_tensor(
                    g.rmeta[:, 387:388], g.rmeta[:, 386:387],
                    g.rmeta[:, 386:387], ALU.mult),
                lambda e: e.scalar_tensor_tensor(
                    g.rmeta[:, 388:389], g.rmeta[:, 385:386], 1.0 / D,
                    g.rmeta[:, 387:388], ALU.mult, ALU.subtract),
            ], deps=[ds1, ds2])
            drsa = bld.g("ACT", lambda e: e.activation(
                g.rmeta[:, 389:390], g.rmeta[:, 388:389], AF.Ln, bias=EPSM),
                deps=[dv])
            drs = bld.g("ACT", lambda e: e.activation(
                g.rmeta[:, 390:391], g.rmeta[:, 389:390], AF.Exp,
                scale=-0.5), deps=[drsa])
            dxn = bld.g("VEC", lambda e: e.scalar_tensor_tensor(
                g.xtmp[:, :], g.rowsc[:, :], g.rmeta[:, 386:387],
                g.rmeta[:, 390:391].to_broadcast([128, D]),
                ALU.subtract, ALU.mult), deps=[drs, dl])
            dq = bld.g("ACT", lambda e: e.activation(
                g.bufC[:, SQS:SQS + 1024], g.xtmp[:, :], AF.Square,
                accum_out=g.rmeta[:, 391:392]), deps=[dxn, ds2])
            drs2a = bld.g("ACT", lambda e: e.activation(
                g.rmeta[:, 392:393], g.rmeta[:, 391:392], AF.Ln, bias=EPSM,
                scale=1.0 / D), deps=[dq])
            drs2 = bld.g("ACT", lambda e: e.activation(
                g.rmeta[:, 393:394], g.rmeta[:, 392:393], AF.Exp,
                scale=-0.5), deps=[drs2a])
            drow = bld.g("VEC", lambda e: e.scalar_tensor_tensor(
                g.bufC[:, ROWS:ROWS + 1024], g.xtmp[:, :],
                g.rmeta[:, 393:394], g.bufA[:, 23552:24576],
                ALU.mult, ALU.mult), deps=[drs2])
            dstrow = bld.dmas("SP", lambda e, ds, b=b, tt=tt: e.dma_start(
                out=g.rows_dram[ROW_XN + b * T + tt * 128:
                                ROW_XN + b * T + (tt + 1) * 128, :],
                in_=g.bufC[:, ROWS:ROWS + 1024]).then_inc(ds, 16), 1,
                deps=[drow])

            evdeps = []
            for j in range(8):
                bank = j % 2
                dtr = bld.g("TEN", lambda e, j=j, bank=bank: e.transpose(
                    pk.t[bank][:, 0:128], g.xtmp[:, j * 128:(j + 1) * 128],
                    g.idf[:, :]),
                    deps=[dxn, ds1, g.d_masks] + pk.dep(bank))
                dev = bld.g("VEC", lambda e, j=j, bank=bank: e.tensor_copy(
                    g.rowsc[:, j * 128:(j + 1) * 128], pk.t[bank][:, 0:128]),
                    deps=[dtr, dq, drow])
                pk.consumed(bank, dev)
                evdeps.append(dev)
            dst1 = bld.dmas("SP", lambda e, ds, b=b, tt=tt: [
                e.dma_start(out=g.xs_dram[b, j * 128:(j + 1) * 128,
                                          tt * 128:(tt + 1) * 128],
                            in_=g.rowsc[:, j * 128:(j + 1) * 128]
                            ).then_inc(ds, 16) for j in range(8)][-1], 8,
                deps=evdeps)

            dmm = None
            for j in range(8):
                dd = (evdeps + [g.d_wr] + pk.dep(2)) if j == 0 else []
                dmm = bld.g("TEN", lambda e, j=j: e.matmul(
                    pk.t[2][:, 0:E], g.rowsc[:, j * 128:(j + 1) * 128],
                    g.rowsc[:, 0:0] if False else
                    _wr_slice(g, j), start=(j == 0), stop=(j == 7)),
                    deps=dd)
            dlg = bld.g("VEC", lambda e, ti=ti: e.scalar_tensor_tensor(
                g.statr[:, ti * 16:(ti + 1) * 16], pk.t[2][:, 0:E],
                g.rmeta[:, 393:394], P[:, PA_RB:PA_RB + E],
                ALU.mult, ALU.add), deps=[dmm, g.d_rb])
            pk.consumed(2, dlg)

            dmx = bld.g("VEC", lambda e, ti=ti: e.reduce_max(
                g.rmeta[:, 394:395], g.statr[:, ti * 16:(ti + 1) * 16], AX.X,
                negate=True), deps=[dlg])
            dex = bld.g("ACT", lambda e, ti=ti: e.activation(
                g.rmeta[:, 400:416], g.statr[:, ti * 16:(ti + 1) * 16],
                AF.Exp, bias=g.rmeta[:, 394:395],
                accum_out=g.rmeta[:, 395:396]), deps=[dmx, drs2])
            dpr = chain(g, "VEC", [
                lambda e: e.reciprocal(g.rmeta[:, 396:397],
                                       g.rmeta[:, 395:396]),
                lambda e: e.tensor_scalar(g.rmeta[:, 400:416],
                                          g.rmeta[:, 400:416],
                                          g.rmeta[:, 396:397], None,
                                          ALU.mult),
            ], deps=[dex])

            pr = g.rmeta[:, 400:416]
            oh1a = g.bufC[:, OHF1 + ti * 16:OHF1 + ti * 16 + 16]
            oh2a = g.bufC[:, OHF2 + ti * 16:OHF2 + ti * 16 + 16]
            dtp = chain(g, "VEC", [
                lambda e, pr=pr: e.reduce_max(g.rmeta[:, 416:417], pr, AX.X),
                lambda e, pr=pr: e.tensor_tensor(
                    g.rmeta[:, 432:448], pr,
                    g.rmeta[:, 416:417].to_broadcast([128, E]), ALU.is_ge),
                lambda e: e.scalar_tensor_tensor(
                    g.rmeta[:, 448:464], g.rmeta[:, 432:448], -1000000.0,
                    P[:, PA_IOTA16B:PA_IOTA16B + E], ALU.mult, ALU.add),
                lambda e: e.tensor_reduce(
                    g.rmeta[:, 417:418], g.rmeta[:, 448:464], AX.X, ALU.min),
                lambda e, oh1a=oh1a: e.tensor_tensor(
                    oh1a, P[:, PA_IOTA16:PA_IOTA16 + E],
                    g.rmeta[:, 417:418].to_broadcast([128, E]), ALU.is_equal),
                lambda e, pr=pr, oh1a=oh1a: e.scalar_tensor_tensor(
                    g.rmeta[:, 448:464], oh1a, -1e30, pr, ALU.mult, ALU.add),
                lambda e: e.reduce_max(
                    g.rmeta[:, 418:419], g.rmeta[:, 448:464], AX.X),
                lambda e: e.tensor_tensor(
                    g.rmeta[:, 464:480], g.rmeta[:, 448:464],
                    g.rmeta[:, 418:419].to_broadcast([128, E]), ALU.is_ge),
                lambda e: e.scalar_tensor_tensor(
                    g.rmeta[:, 480:496], g.rmeta[:, 464:480], -1000000.0,
                    P[:, PA_IOTA16B:PA_IOTA16B + E], ALU.mult, ALU.add),
                lambda e: e.tensor_reduce(
                    g.rmeta[:, 419:420], g.rmeta[:, 480:496], AX.X, ALU.min),
                lambda e, oh2a=oh2a: e.tensor_tensor(
                    oh2a, P[:, PA_IOTA16:PA_IOTA16 + E],
                    g.rmeta[:, 419:420].to_broadcast([128, E]), ALU.is_equal),
                lambda e, ti=ti, oh1a=oh1a, oh2a=oh2a: e.tensor_add(
                    g.bufC[:, CNTC + ti * 16:CNTC + ti * 16 + 16],
                    oh1a, oh2a),
            ], deps=[dpr, g.d_masks])
            dgi = bld.g("VEC", lambda e, ti=ti: e.tensor_copy(
                g.statr[:, 1024 + ti * 16:1024 + ti * 16 + 4],
                g.rmeta[:, 416:420]), deps=[dtp])

            dtl = bld.g("TEN", lambda e, ti=ti: e.matmul(
                pk.t[3][:, 0:E], C[:, TRILX:TRILX + 128],
                g.bufC[:, CNTC + ti * 16:CNTC + ti * 16 + 16],
                start=True, stop=True), deps=[dtp] + pk.dep(3))
            dte = bld.g("VEC", lambda e, ti=ti: e.tensor_copy(
                g.meta[:, ti * 16:(ti + 1) * 16], pk.t[3][:, 0:E]),
                deps=[dtl])
            pk.consumed(3, dte)

            ddeps = [dtp]
            if ti == 0:
                ddeps += pk.dep(5) + [g.d_nwrow[0], g.d_nwrow[1], g.d_rb]
            dtt = bld.g("TEN", lambda e, ti=ti: e.matmul(
                pk.t[5][0:1, ti * 16:ti * 16 + E], C[:, ONES:ONES + 1],
                g.bufC[:, CNTC + ti * 16:CNTC + ti * 16 + 16],
                start=(ti == 0), stop=(ti == 31)), deps=ddeps)
            if ti == 31:
                g.d_tot1 = dtt
            g.tile_last[ti] = [dtp, dte, dgi, dstrow, dst1, dlg]
            prev_tail = [dte, dgi, dstrow, dst1, dmx, dex, dtt]

    g.r_tail = prev_tail
    if int(_os.environ.get("KPHR", "9")) < 2:
        return
    dt1 = bld.g("VEC", lambda e: e.tensor_copy(
        g.rowsc[0:1, 0:512], pk.t[5][0:1, 0:512]), deps=[g.d_tot1])
    pk.consumed(5, dt1)
    dt2 = dt1
    dtot = bld.g("VEC", lambda e: e.tensor_reduce(
        g.rmeta[0:1, 0:16],
        bass.AP(g.rowsc, 0, [[1024, 1], [1, 16], [16, 32]]),
        AX.X, ALU.add), deps=[dt1])
    dcci = bld.dmas("SP", lambda e, ds: e.dma_start(
        out=g.cc_in, in_=g.rmeta[0:1, 0:16]).then_inc(ds, 16), 1,
        deps=[dtot])
    if int(_os.environ.get("KPHR", "9")) < 3:
        return
    dcc = bld.g("POOL", lambda e: e.collective_compute(
        "AllGather", ALU.bypass, replica_groups=[list(range(8))],
        ins=[g.cc_in], outs=[g.cc_out]), deps=[dcci, g.d_dispzero])
    dcca = bld.dmas("SP", lambda e, ds: e.dma_start(
        out=P[0:8, PA_CNTALL:PA_CNTALL + E], in_=g.cc_out).then_inc(ds, 16),
        1, deps=[dcc, g.d_cntz])

    if int(_os.environ.get("KPHR", "9")) < 4:
        return
    phaseR2(g, OHF1, OHF2, CNTC, dt1, dt2, dcca)
    if "metai" in g.taps:
        g.bld.dmas("SP", lambda e, ds: e.dma_start(
            out=g.taps["metai"], in_=g.metai[:, :]).then_inc(ds, 16), 1,
            deps=[g.bld.now("VEC")])
    if "meta" in g.taps:
        g.bld.dmas("SP", lambda e, ds: e.dma_start(
            out=g.taps["meta"], in_=g.meta[:, :]).then_inc(ds, 16), 1,
            deps=[g.bld.now("VEC")])
    if "statr" in g.taps:
        g.bld.dmas("SP", lambda e, ds: e.dma_start(
            out=g.taps["statr"], in_=g.statr[:, :]).then_inc(ds, 16), 1,
            deps=[g.bld.now("VEC")])
    if "dispd" in g.taps:
        g.bld.dmas("SP", lambda e, ds: e.dma_start(
            out=g.taps["dispd"], in_=g.disp_dram).then_inc(ds, 16), 1,
            deps=[g.bld.now("VEC"), g.bld.dma_now("POOL")])
    if "rowsd" in g.taps:
        g.bld.dmas("SP", lambda e, ds: e.dma_start(
            out=g.taps["rowsd"], in_=g.rows_dram[ROW_XN:ROW_XN + NT, :]
            ).then_inc(ds, 16), 1, deps=[g.bld.now("VEC")])


def _wr_slice(g, j):
    return g.params[:, PA_WR + j * 16:PA_WR + j * 16 + 16]


def phaseR2(g, OHF1, OHF2, CNTC, dt1, dt2, dcca):
    bld, pk, C, P = g.bld, g.pk, g.consts, g.params
    db1 = bld.dmas("SP", lambda e, ds: e.dma_start(
        out=g.bounce[0:32, :].rearrange("(a r) n -> a (r n)", a=1),
        in_=g.rowsc[0:1, 0:512]).then_inc(ds, 16), 1,
        deps=[dt1] + g.pk.dep(4) + g.pk.dep(5))
    dz = bld.g("VEC", lambda e: e.memset(g.xtmp[:, 0:64], 0.0),
               deps=list(g.d_xtmp_free) + list(g.r_tail))
    dzb = bld.g("VEC", lambda e: e.memset(g.bufC[:, 3072:3072 + 16], 0.0),
                deps=list(g.r_tail))
    db2 = bld.dmas("POOL", lambda e, ds: e.dma_start(
        out=g.bufC[0:32, 3072:3072 + 16], in_=g.bounce[0:32, :]
        ).then_inc(ds, 16), 1, deps=[db1, dzb])
    dm1 = bld.g("TEN", lambda e: e.matmul(
        pk.t[3][0:32, 0:16], C[:, TRILX:TRILX + 32],
        g.bufC[:, 3072:3072 + 16], start=True, stop=False),
        deps=[db2] + pk.dep(3))
    dm2 = bld.g("TEN", lambda e: e.matmul(
        pk.t[3][0:32, 0:16],
        P[:, PA_CPFX:PA_CPFX + 1].to_broadcast([128, 32]),
        P[:, PA_CNTALL:PA_CNTALL + E], start=False, stop=True), deps=[dcca])
    doffs = bld.g("VEC", lambda e: e.tensor_copy(
        g.xtmp[0:32, 16:32], pk.t[3][0:32, 0:16]), deps=[dm1, dm2, dz])
    pk.consumed(3, doffs)

    KCW = []
    prevv = [doffs]
    for ti in range(32):
        dob = bld.g("TEN", lambda e, ti=ti: e.matmul(
            pk.t[2][:, 0:E],
            g.idf[0:32, ti:ti + 1].to_broadcast([32, 128]),
            g.xtmp[0:32, 16:32], start=True, stop=True),
            deps=[doffs] + pk.dep(2))

        ex = g.rmeta[:, 0:16]
        oh1 = g.bufC[:, OHF1 + ti * 16:OHF1 + ti * 16 + 16]
        oh2 = g.bufC[:, OHF2 + ti * 16:OHF2 + ti * 16 + 16]
        gg = g.statr[:, 1024 + ti * 16:1024 + ti * 16 + 4]
        dkw = chain(g, "VEC", [
            lambda e, ti=ti, ex=ex: e.tensor_add(
                ex, pk.t[2][:, 0:E], g.meta[:, ti * 16:ti * 16 + 16]),
            lambda e, ex=ex, oh1=oh1: e.tensor_tensor(
                g.rmeta[:, 16:32], ex, oh1, ALU.mult),
            lambda e: e.tensor_reduce(
                g.rmeta[:, 64:65], g.rmeta[:, 16:32], AX.X, ALU.add),
            lambda e, ex=ex, oh2=oh2: e.tensor_tensor(
                g.rmeta[:, 32:48], ex, oh2, ALU.mult),
            lambda e: e.tensor_reduce(
                g.rmeta[:, 65:66], g.rmeta[:, 32:48], AX.X, ALU.add),
            lambda e: e.tensor_single_scalar(
                g.rmeta[:, 66:67], g.rmeta[:, 64:65], float(CAP), ALU.is_lt),
            lambda e: e.tensor_single_scalar(
                g.rmeta[:, 67:68], g.rmeta[:, 65:66], float(CAP), ALU.is_lt),
            lambda e, gg=gg: e.tensor_tensor(
                g.rmeta[:, 68:69], gg[:, 0:1], g.rmeta[:, 66:67], ALU.mult),
            lambda e, gg=gg: e.tensor_tensor(
                g.rmeta[:, 69:70], gg[:, 2:3], g.rmeta[:, 67:68], ALU.mult),
            lambda e: e.tensor_add(
                g.rmeta[:, 70:71], g.rmeta[:, 68:69], g.rmeta[:, 69:70]),
            lambda e: e.tensor_single_scalar(
                g.rmeta[:, 71:72], g.rmeta[:, 70:71], 0.0, ALU.is_equal),
            lambda e: e.tensor_add(
                g.rmeta[:, 71:72], g.rmeta[:, 71:72], g.rmeta[:, 70:71]),
            lambda e: e.reciprocal(g.rmeta[:, 72:73], g.rmeta[:, 71:72]),
            lambda e, ti=ti: e.tensor_tensor(
                g.meta[:, 1024 + ti * 2:1024 + ti * 2 + 1],
                g.rmeta[:, 68:69], g.rmeta[:, 72:73], ALU.mult),
            lambda e, ti=ti: e.tensor_tensor(
                g.meta[:, 1024 + ti * 2 + 1:1024 + ti * 2 + 2],
                g.rmeta[:, 69:70], g.rmeta[:, 72:73], ALU.mult),
            lambda e, ti=ti: e.tensor_copy(
                g.statr[:, 1024 + ti * 16 + 4:1024 + ti * 16 + 6],
                g.rmeta[:, 66:68]),
            lambda e, oh1=oh1: e.tensor_scalar(
                g.rmeta[:, 16:32], oh1, g.rmeta[:, 66:67], None, ALU.mult),
            lambda e, ti=ti, oh2=oh2: e.scalar_tensor_tensor(
                g.bufC[:, CNTC + ti * 16:CNTC + ti * 16 + 16], oh2,
                g.rmeta[:, 67:68], g.rmeta[:, 16:32], ALU.mult, ALU.add),
        ], deps=[dob] + prevv)
        pk.consumed(2, dkw)
        ddeps = [dkw]
        if ti == 0:
            ddeps += pk.dep(5)
        dkt = bld.g("TEN", lambda e, ti=ti: e.matmul(
            pk.t[5][0:1, ti * 16:ti * 16 + E], C[:, ONES:ONES + 1],
            g.bufC[:, CNTC + ti * 16:CNTC + ti * 16 + 16],
            start=(ti == 0), stop=(ti == 31)), deps=ddeps)
        if ti == 31:
            g.d_ktot1 = dkt
        dtl = bld.g("TEN", lambda e, ti=ti: e.matmul(
            pk.t[3][:, 0:E], C[:, TRILX:TRILX + 128],
            g.bufC[:, CNTC + ti * 16:CNTC + ti * 16 + 16],
            start=True, stop=True), deps=[dkw] + pk.dep(3))
        dte = bld.g("VEC", lambda e, ti=ti: e.tensor_copy(
            g.meta[:, ti * 16:(ti + 1) * 16], pk.t[3][:, 0:E]), deps=[dtl])
        pk.consumed(3, dte)
        KCW.append(dte)
        prevv = [dkw, dte]

    dk1 = bld.g("VEC", lambda e: e.tensor_copy(
        g.rowsc[0:1, 0:512], pk.t[5][0:1, 0:512]), deps=[g.d_ktot1])
    pk.consumed(5, dk1)
    db1 = bld.dmas("SP", lambda e, ds: e.dma_start(
        out=g.bounce[0:32, :].rearrange("(a r) n -> a (r n)", a=1),
        in_=g.rowsc[0:1, 0:512]).then_inc(ds, 16), 1, deps=[dk1])
    db2 = bld.dmas("POOL", lambda e, ds: e.dma_start(
        out=g.bufC[0:32, 3072:3072 + 16], in_=g.bounce[0:32, :]
        ).then_inc(ds, 16), 1, deps=[db1])
    dko = bld.g("TEN", lambda e: e.matmul(
        pk.t[3][0:32, 0:16], C[:, TRILX:TRILX + 32],
        g.bufC[:, 3072:3072 + 16], start=True, stop=True),
        deps=[db2] + pk.dep(3))
    dkoffs = bld.g("VEC", lambda e: e.tensor_copy(
        g.xtmp[0:32, 16:32], pk.t[3][0:32, 0:16]), deps=[dko])
    pk.consumed(3, dkoffs)

    prevv = [dkoffs]
    g.d_scatter = []
    for b in range(B):
        for tt in range(16):
            ti = b * 16 + tt
            dob = bld.g("TEN", lambda e, ti=ti: e.matmul(
                pk.t[2][:, 0:E],
                g.idf[0:32, ti:ti + 1].to_broadcast([32, 128]),
                g.xtmp[0:32, 16:32], start=True, stop=True),
                deps=[dkoffs] + pk.dep(2))

            kx = g.rmeta[:, 0:16]
            gg = g.statr[:, 1024 + ti * 16:1024 + ti * 16 + 6]
            fns = [lambda e, ti=ti, kx=kx: e.tensor_add(
                kx, pk.t[2][:, 0:E], g.meta[:, ti * 16:ti * 16 + 16])]
            for k in range(2):
                ohc = OHF1 if k == 0 else OHF2
                oh = g.bufC[:, ohc + ti * 16:ohc + ti * 16 + 16]
                ik = gg[:, 1 + 2 * k:2 + 2 * k]
                fns += [
                    lambda e, kx=kx, oh=oh: e.tensor_tensor(
                        g.rmeta[:, 16:32], kx, oh, ALU.mult),
                    lambda e: e.tensor_reduce(
                        g.rmeta[:, 64:65], g.rmeta[:, 16:32], AX.X, ALU.add),
                    lambda e: e.tensor_single_scalar(
                        g.rmeta[:, 65:66], g.rmeta[:, 64:65], float(CLOC),
                        ALU.is_ge),
                    lambda e, ik=ik: e.tensor_scalar(
                        g.rmeta[:, 66:67], ik, float(CLOC),
                        float(-4 * CLOC), ALU.mult, ALU.add),
                    lambda e, ik=ik: e.tensor_single_scalar(
                        g.rmeta[:, 67:68], ik, 4.0, ALU.is_ge),
                    lambda e, ik=ik: e.tensor_single_scalar(
                        g.rmeta[:, 68:69], ik, 0.0, ALU.is_equal),
                    lambda e, ik=ik: e.tensor_single_scalar(
                        g.rmeta[:, 69:70], ik, 1.0, ALU.is_equal),
                    lambda e, ik=ik: e.tensor_single_scalar(
                        g.rmeta[:, 70:71], ik, 2.0, ALU.is_ge),
                    lambda e: e.tensor_sub(
                        g.rmeta[:, 70:71], g.rmeta[:, 70:71],
                        g.rmeta[:, 67:68]),
                    lambda e: e.tensor_add(
                        g.rmeta[:, 71:72], g.rmeta[:, 66:67],
                        g.rmeta[:, 64:65]),
                    lambda e: e.tensor_scalar(
                        g.rmeta[:, 72:73], g.rmeta[:, 67:68], -6e8, 6e8,
                        ALU.mult, ALU.add),
                    lambda e: e.tensor_scalar(
                        g.rmeta[:, 73:74], g.rmeta[:, 65:66], 6e8, None,
                        ALU.mult),
                    lambda e: e.tensor_add(
                        g.rmeta[:, 71:72], g.rmeta[:, 71:72],
                        g.rmeta[:, 72:73]),
                    lambda e: e.tensor_add(
                        g.rmeta[:, 71:72], g.rmeta[:, 71:72],
                        g.rmeta[:, 73:74]),
                    lambda e: e.tensor_scalar_min(
                        g.rmeta[:, 71:72], g.rmeta[:, 71:72],
                        float(EM * CLOC)),
                    lambda e: e.tensor_scalar_max(
                        g.rmeta[:, 71:72], g.rmeta[:, 71:72], 0.0),
                    lambda e, ti=ti, k=k: e.tensor_copy(
                        g.metai[:, 128 + ti * 2 + k:129 + ti * 2 + k],
                        g.rmeta[:, 71:72]),
                    lambda e: e.tensor_scalar_min(
                        g.rmeta[:, 74:75], g.rmeta[:, 64:65],
                        float(CLOC - 1)),
                    lambda e: e.tensor_add(
                        g.rmeta[:, 74:75], g.rmeta[:, 74:75],
                        g.rmeta[:, 66:67]),
                    lambda e: e.tensor_tensor(
                        g.rmeta[:, 74:75], g.rmeta[:, 74:75],
                        g.rmeta[:, 67:68], ALU.mult),
                    lambda e: e.tensor_scalar(
                        g.rmeta[:, 75:76], g.rmeta[:, 68:69],
                        float(ROW_ZERO), None, ALU.mult),
                    lambda e, b=b, tt=tt: e.tensor_scalar(
                        g.rmeta[:, 76:77], P[:, PA_IOTAP:PA_IOTAP + 1], 1.0,
                        float(ROW_XN + b * T + tt * 128), ALU.mult, ALU.add),
                    lambda e: e.tensor_tensor(
                        g.rmeta[:, 76:77], g.rmeta[:, 76:77],
                        g.rmeta[:, 69:70], ALU.mult),
                    lambda e, ik=ik: e.tensor_scalar(
                        g.rmeta[:, 77:78], ik, 1.0, float(ROW_ZERO - 1),
                        ALU.mult, ALU.add),
                    lambda e: e.tensor_tensor(
                        g.rmeta[:, 77:78], g.rmeta[:, 77:78],
                        g.rmeta[:, 70:71], ALU.mult),
                    lambda e: e.tensor_add(
                        g.rmeta[:, 74:75], g.rmeta[:, 74:75],
                        g.rmeta[:, 75:76]),
                    lambda e: e.tensor_add(
                        g.rmeta[:, 74:75], g.rmeta[:, 74:75],
                        g.rmeta[:, 76:77]),
                    lambda e: e.tensor_add(
                        g.rmeta[:, 74:75], g.rmeta[:, 74:75],
                        g.rmeta[:, 77:78]),
                    lambda e: e.tensor_scalar_min(
                        g.rmeta[:, 74:75], g.rmeta[:, 74:75],
                        float(NROWS - 1)),
                    lambda e: e.tensor_scalar_max(
                        g.rmeta[:, 74:75], g.rmeta[:, 74:75], 0.0),
                    lambda e, ti=ti, k=k: e.tensor_copy(
                        g.metai[:, ti * 2 + k:ti * 2 + k + 1],
                        g.rmeta[:, 74:75]),
                ]
            dsl = chain(g, "VEC", fns, deps=[dob] + prevv)
            pk.consumed(2, dsl)
            prevv = [dsl]
            if int(_os.environ.get("KPHR", "9")) < 5:
                continue
            drl = bld.dmas("POOL", lambda e, ds, b=b, tt=tt: e.dma_start(
                out=g.bufC[:, 4096:5120],
                in_=g.rows_dram[ROW_XN + b * T + tt * 128:
                                ROW_XN + b * T + (tt + 1) * 128, :]
                ).then_inc(ds, 16), 1,
                deps=[dsl] + ([g.d_scatter[-1]] if g.d_scatter else []))
            dsc = bld.dmas("POOL", lambda e, ds, ti=ti: [
                e.indirect_dma_start(
                    out=g.disp_dram,
                    out_offset=bass.IndirectOffsetOnAxis(
                        g.metai[:, 128 + ti * 2 + k:129 + ti * 2 + k], 0),
                    in_=g.bufC[:, 4096:5120],
                    in_offset=None).then_inc(ds, 16) for k in range(2)][-1],
                2, deps=[drl])
            g.d_scatter.append(dsc)




def load_xb(g, b, deps):
    return g.bld.dmas("POOL", lambda e, ds, b=b: e.dma_start(
        out=g.xb.rearrange("p (c t) -> p c t", c=8),
        in_=g.xs_dram[b].rearrange("(c p) t -> p c t", p=128)
        ).then_inc(ds, 16), 1, deps=deps)


def norm_scale(g, eps, deps, sq0, sq1, rstdb):
    bld, pk = g.bld, g.pk
    mmd = [None] * 4
    sq_war = {0: [], 1: []}
    for j in range(8):
        par = j % 2
        dst = sq0 if par == 0 else sq1
        dsq = bld.g("ACT", lambda e, j=j, dst=dst: e.activation(
            dst, g.xb[:, j * T:(j + 1) * T], AF.Square),
            deps=list(deps) + sq_war[par])
        cons = []
        for ts in range(4):
            dd = [dsq] + (pk.dep(ts) if j == 0 else [])
            mmd[ts] = bld.g("TEN", lambda e, j=j, ts=ts, dst=dst:
                            e.matmul(pk.t[ts][0:1, 0:512],
                                     g.consts[:, ONES:ONES + 1],
                                     dst[:, ts * 512:(ts + 1) * 512],
                                     start=(j == 0), stop=(j == 7)),
                            deps=dd)
            cons.append(mmd[ts])
        sq_war[par] = cons
    devs = []
    for ts in range(4):
        dev = bld.g("VEC", lambda e, ts=ts: e.tensor_copy(
            g.statr[0:1, ts * 512:(ts + 1) * 512], pk.t[ts][0:1, 0:512]),
            deps=[mmd[ts]])
        pk.consumed(ts, dev)
        devs.append(dev)
    drsa = bld.g("ACT", lambda e, eps=eps: e.activation(
        g.statr[64:65, 0:2048], g.statr[0:1, 0:2048], AF.Ln,
        bias=eps, scale=1.0 / D), deps=devs)
    drs = bld.g("ACT", lambda e: e.activation(
        g.statr[64:65, 0:2048], g.statr[64:65, 0:2048], AF.Exp,
        scale=-0.5), deps=[drsa])
    bdeps = []
    for ts in range(4):
        dbc = bcast_row_f32(g, g.statr[64:65, ts * 512:(ts + 1) * 512],
                            rstdb[:, ts * 512:(ts + 1) * 512], ts,
                            deps=[drs])
        bdeps.append(dbc)
    xsdeps = []
    for j in range(8):
        dxs = bld.g("VEC", lambda e, j=j: e.tensor_tensor(
            g.xb[:, j * T:(j + 1) * T], g.xb[:, j * T:(j + 1) * T],
            rstdb, ALU.mult), deps=bdeps + list(deps))
        xsdeps.append(dxs)
    return xsdeps


def wscale(g, wsb, nchunk, ncols, wcol, deps):
    def f(e, wsb=wsb, nchunk=nchunk, ncols=ncols, wcol=wcol):
        for j in range(nchunk):
            last = e.tensor_scalar(
                wsb[:, j * ncols:(j + 1) * ncols],
                wsb[:, j * ncols:(j + 1) * ncols],
                g.params[:, wcol + j:wcol + j + 1], None, ALU.mult)
        return last
    return g.bld.g("VEC", f, deps=deps)


def res_evac(g, b, mb, ts, bank, deps_load, deps_mm, slot):
    bld = g.bld
    if not hasattr(g, "evac_war"):
        g.evac_war = {0: [], 1: [], 2: [], 3: []}
    if slot < 2:
        xsl = g.xtmp[:, slot * 512:(slot + 1) * 512]
        osl = g.rowsc[:, slot * 512:(slot + 1) * 512]
    else:
        xsl = g.statr[:, (slot - 2) * 512:(slot - 1) * 512]
        osl = g.statr[:, 1024 + (slot - 2) * 512:1024 + (slot - 1) * 512]
    war = g.evac_war[slot]
    dxl = bld.dmas("ACT", lambda e, ds, b=b, mb=mb, ts=ts, xsl=xsl:
                   e.dma_start(out=xsl,
                               in_=g.xs_dram[b, mb * 128:(mb + 1) * 128,
                                             ts * 512:(ts + 1) * 512]
                               ).then_inc(ds, 16), 1,
                   deps=list(deps_load) + war)
    dst = bld.g("VEC", lambda e, bank=bank, xsl=xsl, osl=osl:
                e.scalar_tensor_tensor(
                    osl, g.pk.t[bank][:, 0:512], 1.0,
                    xsl, ALU.mult, ALU.add),
                deps=[dxl] + deps_mm + war)
    g.pk.consumed(bank, dst)
    dso = bld.dmas("SP", lambda e, ds, b=b, mb=mb, ts=ts, osl=osl:
                   e.dma_start(out=g.xs_dram[b, mb * 128:(mb + 1) * 128,
                                             ts * 512:(ts + 1) * 512],
                               in_=osl).then_inc(ds, 16), 1, deps=[dst])
    g.evac_war[slot] = [dso, dst]
    return dso, dst


def inner_attn(g):
    bld, pk, C, P = g.bld, g.pk, g.consts, g.params
    KD, VB, VT, AMSK = 0, 8192, 12288, 16448
    SQ0, SQ1 = 8192, 10240
    RSTD = KD
    g.attn_tail = []
    for b in range(B):
        prev = [g.d_amasks] + (g.attn_tail if b else [])
        dxb = load_xb(g, b, deps=prev + getattr(g, "xb_free", []))
        xs = norm_scale(g, EPSI, [dxb], g.bufC[:, SQ0:SQ0 + 2048],
                        g.bufC[:, SQ1:SQ1 + 2048],
                        g.bufC[:, RSTD:RSTD + 2048])
        dwk = bld.dmas("POOL", lambda e, ds: (
            e.dma_start(out=g.wbuf[:, 0:2048].rearrange(
                "p (c m) -> p c m", c=8),
                in_=g.Wk.rearrange("(c p) m -> p c m", p=128)
                ).then_inc(ds, 16),
            e.dma_start(out=g.wbuf[:, 2048:4096].rearrange(
                "p (c m) -> p c m", c=8),
                in_=g.Wv.rearrange("(c p) m -> p c m", p=128)
                ).then_inc(ds, 16))[-1], 2,
            deps=prev + getattr(g, "wbuf_free", []))
        dws1 = wscale(g, g.wbuf[:, 0:2048], 8, 256, PA_ATTNW, [dwk])
        dws2 = wscale(g, g.wbuf[:, 2048:4096], 8, 256, PA_ATTNW, [dwk])
        kv_evs = []
        for (woff, tag) in ((0, "k"), (2048, "v")):
            for m in range(2):
                for ts in range(4):
                    bank = (m * 4 + ts) % 2
                    dmm = None
                    for k in range(8):
                        dd = ([dws1, dws2] + xs + pk.dep(bank)) \
                            if k == 0 else []
                        dmm = bld.g("TEN", lambda e, woff=woff, m=m, ts=ts,
                                    k=k, bank=bank: e.matmul(
                                        pk.t[bank][:, 0:512],
                                        g.wbuf[:, woff + k * 256 + m * 128:
                                               woff + k * 256 + m * 128
                                               + 128],
                                        g.xb[:, k * T + ts * 512:
                                             k * T + ts * 512 + 512],
                                        start=(k == 0), stop=(k == 7)),
                                    deps=dd)
                    if tag == "k":
                        def ev(e, m=m, ts=ts, bank=bank):
                            e.tensor_copy(
                                g.bufC[0:64, KD + (2 * m) * T + ts * 512:
                                       KD + (2 * m) * T + ts * 512 + 512],
                                pk.t[bank][0:64, 0:512])
                            return e.tensor_copy(
                                g.bufC[64:128,
                                       KD + (2 * m + 1) * T + ts * 512:
                                       KD + (2 * m + 1) * T + ts * 512
                                       + 512],
                                pk.t[bank][64:128, 0:512])
                        dev = bld.g("VEC", ev, deps=[dmm] + xs)
                    else:
                        dev = bld.g("VEC", lambda e, m=m, ts=ts, bank=bank:
                                    e.tensor_copy(
                                        g.bufC[:, VB + m * T + ts * 512:
                                               VB + m * T + ts * 512 + 512],
                                        pk.t[bank][:, 0:512]), deps=[dmm])
                    pk.consumed(bank, dev)
                    kv_evs.append(dev)
        def kdupf(e, ds):
            for gg in (0, 2):
                e.dma_start(out=g.bufC[64:128, KD + gg * T:KD + (gg + 1) * T],
                            in_=g.bufC[0:64, KD + gg * T:KD + (gg + 1) * T]
                            ).then_inc(ds, 16)
            for gg in (1, 3):
                e.dma_start(out=g.bufC[0:64, KD + gg * T:KD + (gg + 1) * T],
                            in_=g.bufC[64:128, KD + gg * T:KD + (gg + 1) * T]
                            ).then_inc(ds, 16)
        dkd = bld.dmas("SP", kdupf, 4, deps=kv_evs)
        dvo = bld.g("VEC", lambda e: e.memset(g.bufC[:, VT:VT + 4160], 1.0),
                    deps=kv_evs)
        vt_evs = []
        for m in range(2):
            for tl in range(16):
                bank = 6 + (tl % 2)
                dtr = bld.g("TEN", lambda e, m=m, tl=tl, bank=bank:
                            e.transpose(pk.t[bank][:, 0:128],
                                        g.bufC[:, VB + m * T + tl * 128:
                                               VB + m * T + tl * 128 + 128],
                                        C[:, ID_B:ID_B + 128]),
                            deps=kv_evs + pk.dep(bank) + [g.d_masks])
                dev = bld.g("VEC", lambda e, m=m, tl=tl, bank=bank:
                            e.tensor_copy(
                                g.bufC[:, VT + m * 2080 + tl * 130:
                                       VT + m * 2080 + tl * 130 + 130]
                                .rearrange("p (s h) -> p s h", s=2)
                                [:, :, 0:64],
                                pk.t[bank][:, 0:128]
                                .rearrange("p (s h) -> p s h", s=2)),
                            deps=[dtr, dvo])
                pk.consumed(bank, dev)
                vt_evs.append(dev)

        dwq = bld.dmas("POOL", lambda e, ds: e.dma_start(
            out=g.wbuf[:, 0:8192].rearrange("p (c m) -> p c m", c=8),
            in_=g.Wq.rearrange("(c p) m -> p c m", p=128)).then_inc(ds, 16),
            1, deps=kv_evs)
        dwqs = wscale(g, g.wbuf[:, 0:8192], 8, 1024, PA_ATTNW, [dwq])
        q_evs = []
        for m in range(8):
            for ts in range(4):
                bank = (m * 4 + ts) % 2
                dmm = None
                for k in range(8):
                    dd = ([dwqs] + xs + pk.dep(bank)) if k == 0 else []
                    dmm = bld.g("TEN", lambda e, m=m, ts=ts, k=k, bank=bank:
                                e.matmul(
                                    pk.t[bank][:, 0:512],
                                    g.wbuf[:, k * 1024 + m * 128:
                                           k * 1024 + m * 128 + 128],
                                    g.xb[:, k * T + ts * 512:
                                         k * T + ts * 512 + 512],
                                    start=(k == 0), stop=(k == 7)), deps=dd)
                dev = bld.g("ACT", lambda e, m=m, ts=ts, bank=bank:
                            e.activation(
                                g.bufA[:, m * T + ts * 512:
                                       m * T + ts * 512 + 512],
                                pk.t[bank][:, 0:512], AF.Copy), deps=[dmm])
                pk.consumed(bank, dev)
                q_evs.append(dev)

        PT = 16384
        o_evs = []
        sq_war = {0: [], 64: []}
        g.kn_dep = None
        prev_pv = []
        prev_dlb = []
        for h in range(NH):
            gkv = h // 4
            qb = (h % 2) * 64
            dsq = bld.g("ACT", lambda e, h=h, qb=qb: e.activation(
                g.wbuf[qb:qb + 64, 0:2048],
                g.bufA[qb:qb + 64, (h // 2) * T:(h // 2) * T + T],
                AF.Square), deps=q_evs + sq_war[qb] + [dwqs])
            qn_mm = []
            for ts in range(4):
                dmm = bld.g("TEN", lambda e, ts=ts, qb=qb: e.matmul(
                    pk.t[4][0:1, 0:512],
                    C[qb:qb + 64, ONES:ONES + 1],
                    g.wbuf[qb:qb + 64, ts * 512:ts * 512 + 512],
                    start=True, stop=True), deps=[dsq] + pk.dep(4))
                dev = bld.g("VEC", lambda e, ts=ts: e.reduce_max(
                    g.rmeta[0:1, 100 + ts:101 + ts], pk.t[4][0:1, 0:512],
                    AX.X), deps=[dmm])
                pk.consumed(4, dev)
                qn_mm.append(dev)
            sq_war[qb] = qn_mm
            if h % 4 == 0:
                dsk = bld.g("ACT", lambda e, h=h, qb=qb, gkv=gkv:
                            e.activation(
                                g.wbuf[qb:qb + 64, 2048:4096],
                                g.bufC[qb:qb + 64, KD + gkv * T:
                                       KD + gkv * T + T],
                                AF.Square), deps=[dsq, dkd] + sq_war[qb])
                kn_mm = []
                for ts in range(4):
                    dmm = bld.g("TEN", lambda e, ts=ts, qb=qb: e.matmul(
                        pk.t[5][0:1, 0:512],
                        C[qb:qb + 64, ONES:ONES + 1],
                        g.wbuf[qb:qb + 64, 2048 + ts * 512:
                               2048 + ts * 512 + 512],
                        start=True, stop=True), deps=[dsk] + pk.dep(5))
                    dev = bld.g("VEC", lambda e, ts=ts: e.reduce_max(
                        g.rmeta[0:1, 104 + ts:105 + ts],
                        pk.t[5][0:1, 0:512], AX.X), deps=[dmm])
                    pk.consumed(5, dev)
                    kn_mm.append(dev)
                dkp = chain(g, "VEC", [
                    lambda e: e.tensor_copy(g.rmeta[0:1, 108:109],
                                            g.rmeta[0:1, 104:105]),
                    lambda e: e.tensor_tensor(g.rmeta[0:1, 109:110],
                                              g.rmeta[0:1, 108:109],
                                              g.rmeta[0:1, 105:106],
                                              ALU.max),
                    lambda e: e.tensor_tensor(g.rmeta[0:1, 110:111],
                                              g.rmeta[0:1, 109:110],
                                              g.rmeta[0:1, 106:107],
                                              ALU.max),
                    lambda e: e.tensor_tensor(g.rmeta[0:1, 111:112],
                                              g.rmeta[0:1, 110:111],
                                              g.rmeta[0:1, 107:108],
                                              ALU.max),
                ], deps=kn_mm)
                g.kn_dep = dkp
                sq_war[qb] = sq_war[qb] + kn_mm
            for qc in range(4):
                ntk = 4 * (qc + 1)
                dbb = bld.g("VEC", lambda e, qc=qc: e.tensor_tensor(
                    g.rmeta[0:1, 112:113], g.rmeta[0:1, 100 + qc:101 + qc],
                    g.rmeta[0:1, 108 + qc:109 + qc], ALU.mult),
                    deps=qn_mm + [g.kn_dep])
                dbs0 = bld.g("ACT", lambda e: e.activation(
                    g.rmeta[0:1, 113:114], g.rmeta[0:1, 112:113], AF.Ln),
                    deps=[dbb])
                dbs = bld.g("ACT", lambda e: e.activation(
                    g.rmeta[0:1, 113:114], g.rmeta[0:1, 113:114], AF.Exp,
                    scale=0.5), deps=[dbs0])
                dbn = bld.g("VEC", lambda e: e.tensor_scalar_mul(
                    g.rmeta[0:1, 114:115], g.rmeta[0:1, 113:114], -0.125),
                    deps=[dbs])
                dnb = bcast_row_f32(g, g.rmeta[0:1, 114:115],
                                    g.rmeta[:, 115:116], 3,
                                    deps=[dbn] + prev_pv)
                exp_deps = []
                for tkc in range(ntk):
                    bank = tkc % 3
                    dd = [dnb, dkd] + q_evs + pk.dep(bank) + prev_pv
                    dmm = bld.g("TEN", lambda e, h=h, qc=qc, tkc=tkc,
                                bank=bank, qb=qb, gkv=gkv: e.matmul(
                                    pk.t[bank][:, 0:512],
                                    g.bufC[qb:qb + 64,
                                           KD + gkv * T + tkc * 128:
                                           KD + gkv * T + tkc * 128 + 128],
                                    g.bufA[qb:qb + 64,
                                           (h // 2) * T + qc * 512:
                                           (h // 2) * T + qc * 512 + 512],
                                    start=True, stop=True), deps=dd)
                    dex = bld.g("ACT", lambda e, tkc=tkc, bank=bank:
                                e.activation(
                                    g.bufA[:, PT + tkc * 512:
                                           PT + tkc * 512 + 512],
                                    pk.t[bank][:, 0:512], AF.Exp,
                                    scale=0.125,
                                    bias=g.rmeta[:, 115:116]),
                                deps=[dmm, dnb])
                    pk.consumed(bank, dex)
                    off = tkc * 128 - qc * 512
                    if 0 <= off < 512:
                        dmk = bld.g("VEC", lambda e, tkc=tkc, off=off:
                                    e.tensor_tensor(
                                        g.bufA[:, PT + tkc * 512:
                                               PT + tkc * 512 + 512],
                                        g.bufA[:, PT + tkc * 512:
                                               PT + tkc * 512 + 512],
                                        g.bufC[:, AMSK + (off // 128) * 512:
                                               AMSK + (off // 128) * 512
                                               + 512],
                                        ALU.mult),
                                    deps=[dex, g.d_amasks])
                        exp_deps.append(dmk)
                    else:
                        exp_deps.append(dex)
                dpv = None
                for tkc in range(ntk):
                    dd = [exp_deps[tkc]] + vt_evs
                    if tkc == 0:
                        dd = dd + pk.dep(5)
                    dpv = bld.g("TEN", lambda e, tkc=tkc, gkv=gkv, ntk=ntk:
                                e.matmul(
                                    pk.t[5][0:65, 0:512],
                                    g.bufC[:, VT + (gkv // 2) * 2080
                                           + tkc * 130 + (gkv % 2) * 65:
                                           VT + (gkv // 2) * 2080
                                           + tkc * 130 + (gkv % 2) * 65
                                           + 65],
                                    g.bufA[:, PT + tkc * 512:
                                           PT + tkc * 512 + 512],
                                    start=(tkc == 0), stop=(tkc == ntk - 1)),
                                deps=dd)
                dlr = bld.g("VEC", lambda e: e.reciprocal(
                    g.xtmp[64:65, 512:1024], pk.t[5][64:65, 0:512]),
                    deps=[dpv] + prev_dlb)
                dlb = bld.g("TEN", lambda e: e.matmul(
                    pk.t[4][0:64, 0:512],
                    P[64:65, PA_ONESF:PA_ONESF + 1].to_broadcast([1, 64]),
                    g.xtmp[64:65, 512:1024], start=True, stop=True),
                    deps=[dlr] + pk.dep(4) + [g.d_onesf])
                dle = bld.g("VEC", lambda e: e.tensor_copy(
                    g.xtmp[0:64, 0:512], pk.t[4][0:64, 0:512]), deps=[dlb])
                pk.consumed(4, dle)
                doe = bld.g("VEC", lambda e, h=h, qc=qc, qb=qb:
                            e.tensor_tensor(
                                g.bufB[qb:qb + 64, (h // 2) * T + qc * 512:
                                       (h // 2) * T + qc * 512 + 512],
                                pk.t[5][0:64, 0:512], g.xtmp[0:64, 0:512],
                                ALU.mult), deps=[dle, dpv])
                pk.consumed(5, doe)
                o_evs.append(doe)
                prev_pv = [dpv]
                prev_dlb = [dlb]

        dwo = bld.dmas("POOL", lambda e, ds: e.dma_start(
            out=g.wbuf[:, 0:8192].rearrange("p (c m) -> p c m", c=8),
            in_=g.Wo.rearrange("(c p) m -> p c m", p=128)).then_inc(ds, 16),
            1, deps=o_evs + sq_war[0] + sq_war[64])
        flash_done = [bld.now("VEC"), bld.now("ACT"), bld.now("TEN")]
        tail = []
        slot = 0
        prev_slot = {0: [], 1: [], 2: [], 3: []}
        for m in range(8):
            for ts in range(4):
                bank = (m * 4 + ts) % 2
                dmm = None
                for k in range(8):
                    dd = ([dwo] + o_evs + pk.dep(bank)) if k == 0 else []
                    dmm = bld.g("TEN", lambda e, m=m, ts=ts, k=k, bank=bank:
                                e.matmul(
                                    pk.t[bank][:, 0:512],
                                    g.wbuf[:, k * 1024 + m * 128:
                                           k * 1024 + m * 128 + 128],
                                    g.bufB[:, k * T + ts * 512:
                                           k * T + ts * 512 + 512],
                                    start=(k == 0), stop=(k == 7)), deps=dd)
                dso, dst = res_evac(g, b, m, ts, bank,
                                    prev_slot[slot] + flash_done,
                                    [dmm], slot)
                prev_slot[slot] = [dso, dst]
                slot = (slot + 1) % 4
                tail.append(dso)
        g.attn_tail = tail
        g.wbuf_free = tail
        g.xb_free = tail


def inner_ffn(g):
    bld, pk, C = g.bld, g.pk, g.consts
    SQ0, SQ1, RSTD = 0, 2048, 4096
    prev = g.attn_tail
    for b in range(B):
        dxb = load_xb(g, b, deps=prev)
        xs = norm_scale(g, EPSI, [dxb], g.bufC[:, SQ0:SQ0 + 2048],
                        g.bufC[:, SQ1:SQ1 + 2048],
                        g.bufC[:, RSTD:RSTD + 2048])
        dhz = bld.g("VEC", lambda e: e.memset(
            g.bufA[64:128, 21 * 1024:22 * 1024], 0.0), deps=prev)
        tail = prev
        for tc in range(2):
            ha_deps = []
            wgu_war = {0: [], 1: []}
            scr_war = {0: [], 1: []}
            for fb in range(22):
                fw = 128 if fb < 21 else 64
                wo = (fb % 2) * 2048
                dwg = bld.dmas("POOL", lambda e, ds, fb=fb, fw=fw, wo=wo: (
                    e.dma_start(
                        out=g.wbuf[:, wo:wo + 8 * fw].rearrange(
                            "p (c m) -> p c m", c=8),
                        in_=g.Wg.rearrange("(c p) m -> p c m", p=128)
                        [:, :, fb * 128:fb * 128 + fw]).then_inc(ds, 16),
                    e.dma_start(
                        out=g.wbuf[:, 1024 + wo:1024 + wo + 8 * fw]
                        .rearrange("p (c m) -> p c m", c=8),
                        in_=g.Wu.rearrange("(c p) m -> p c m", p=128)
                        [:, :, fb * 128:fb * 128 + fw]).then_inc(ds, 16)
                    )[-1], 2, deps=wgu_war[fb % 2] + (tail if tc == 0 or
                                                      fb < 2 else []))
                dwsc1 = wscale(g, g.wbuf[:, wo:wo + 8 * fw], 8, fw,
                               PA_FFNW, [dwg])
                dwsc2 = wscale(g, g.wbuf[:, 1024 + wo:1024 + wo + 8 * fw],
                               8, fw, PA_FFNW, [dwsc1, dwg])
                myw = []
                for ts in range(2):
                    bankA = 0 if ts == 0 else 1
                    bankB = 2 if ts == 0 else 3
                    dmmA = dmmB = None
                    for k in range(8):
                        dd = ([dwsc1, dwsc2] + xs + pk.dep(bankA)) \
                            if k == 0 else []
                        dmmA = bld.g("TEN", lambda e, fw=fw, wo=wo, tc=tc,
                                     ts=ts, k=k, bankA=bankA: e.matmul(
                                         pk.t[bankA][0:fw, 0:512],
                                         g.wbuf[:, wo + k * fw:
                                                wo + k * fw + fw],
                                         g.xb[:, k * T + tc * 1024
                                              + ts * 512:
                                              k * T + tc * 1024 + ts * 512
                                              + 512],
                                         start=(k == 0), stop=(k == 7)),
                                     deps=dd)
                    for k in range(8):
                        dd = ([dwsc1, dwsc2] + xs + pk.dep(bankB)) \
                            if k == 0 else []
                        dmmB = bld.g("TEN", lambda e, fw=fw, wo=wo, tc=tc,
                                     ts=ts, k=k, bankB=bankB: e.matmul(
                                         pk.t[bankB][0:fw, 0:512],
                                         g.wbuf[:, 1024 + wo + k * fw:
                                                1024 + wo + k * fw + fw],
                                         g.xb[:, k * T + tc * 1024
                                              + ts * 512:
                                              k * T + tc * 1024 + ts * 512
                                              + 512],
                                         start=(k == 0), stop=(k == 7)),
                                     deps=dd)
                    dsl = bld.g("ACT", lambda e, fw=fw, ts=ts, bankA=bankA:
                                e.activation(
                                    g.bufC[0:fw, 6144 + ts * 512:
                                           6144 + ts * 512 + 512],
                                    pk.t[bankA][0:fw, 0:512], AF.Silu),
                                deps=[dmmA] + scr_war[ts])
                    pk.consumed(bankA, dsl)
                    dh = bld.g("VEC", lambda e, fb=fb, fw=fw, ts=ts,
                               bankB=bankB: e.tensor_tensor(
                                   g.bufA[0:fw, fb * 1024 + ts * 512:
                                          fb * 1024 + ts * 512 + 512],
                                   pk.t[bankB][0:fw, 0:512],
                                   g.bufC[0:fw, 6144 + ts * 512:
                                          6144 + ts * 512 + 512], ALU.mult),
                               deps=[dsl, dmmB, dhz])
                    pk.consumed(bankB, dh)
                    ha_deps.append(dh)
                    scr_war[ts] = [dh]
                    myw += [dsl, dh]
                wgu_war[fb % 2] = myw
            tail = []
            slot = 0
            prev_slot = {0: [], 1: [], 2: [], 3: []}
            wd_war = []
            for mq in range(4):
                dwd = bld.dmas("POOL", lambda e, ds, mq=mq: (
                    e.dma_start(
                        out=g.wbuf[:, 0:5376].rearrange(
                            "p (c m) -> p c m", c=21),
                        in_=g.Wd[0:2688, :].rearrange(
                            "(c p) m -> p c m", p=128)
                        [:, :, mq * 256:mq * 256 + 256]).then_inc(ds, 16),
                    e.dma_start(
                        out=g.wbuf[0:64, 5376:5632],
                        in_=g.Wd[2688:2752, mq * 256:mq * 256 + 256]
                        ).then_inc(ds, 16))[-1], 2,
                    deps=ha_deps + wd_war + wgu_war[0] + wgu_war[1])
                dwdz = bld.g("VEC", lambda e: e.memset(
                    g.wbuf[64:128, 21 * 256:22 * 256], 0.0), deps=[dwd])
                for mi in range(2):
                    m = mq * 2 + mi
                    for ts in range(2):
                        bank = 4 + ((mi * 2 + ts) % 2)
                        dmm = None
                        for k in range(22):
                            dd = ([dwdz] + ha_deps + pk.dep(bank)) \
                                if k == 0 else []
                            dmm = bld.g("TEN", lambda e, mi=mi, ts=ts, k=k,
                                        bank=bank: e.matmul(
                                            pk.t[bank][:, 0:512],
                                            g.wbuf[:, k * 256 + mi * 128:
                                                   k * 256 + mi * 128
                                                   + 128],
                                            g.bufA[:, k * 1024 + ts * 512:
                                                   k * 1024 + ts * 512
                                                   + 512],
                                            start=(k == 0), stop=(k == 21)),
                                        deps=dd)
                            wd_war.append(dmm)
                        tsg = tc * 2 + ts
                        dso, dst = res_evac(g, b, m, tsg, bank,
                                            prev_slot[slot], [dmm], slot)
                        prev_slot[slot] = [dso, dst]
                        slot = (slot + 1) % 4
                        tail.append(dso)
        prev = tail
    g.ffn_tail = prev


def _wx_load(g, li, half, deps):
    if half == 0:
        off, w = 0, 512
    else:
        off, w = 512, 544
    return g.bld.dmas("POOL", lambda e, ds, li=li, off=off, w=w: e.dma_start(
        out=g.wbuf[:, 0:8 * w].rearrange("p (c m) -> p c m", c=8),
        in_=g.Wx[li].rearrange("(c p) m -> p c m", p=128)[:, :, off:off + w]
        ).then_inc(ds, 16), 1, deps=deps)


RL = 256
NRUN = 16 * RL


def inner_mamba(g, li):
    bld, pk, C, P = g.bld, g.pk, g.consts, g.params
    SQ0, SQ1, RSTD = 0, 2048, 4096
    B0, B1, C0, C1 = 0, NRUN, 2 * NRUN, 3 * NRUN
    BCR = 4 * NRUN
    DSRC = BCR + 2048
    WVS = DSRC + 1024
    CARR = WVS + 512
    TM16 = CARR + 128
    ZST = TM16 + 32
    A0, U0, H0 = 0, NRUN, 2 * NRUN
    PAR1 = 3 * NRUN
    prev = g.ffn_tail if li == 0 else g.mamba_tail
    for b in range(B):
        dxb = load_xb(g, b, deps=prev)
        xs = norm_scale(g, EPSI, [dxb], g.bufC[:, SQ0:SQ0 + 2048],
                        g.bufC[:, SQ1:SQ1 + 2048],
                        g.bufC[:, RSTD:RSTD + 2048])
        tailz = []
        xz1_evs = []
        win_evs = []
        for half in range(2):
            dwi = bld.dmas("POOL", lambda e, ds, li=li, half=half:
                           e.dma_start(
                               out=g.wbuf[:, 0:8192].rearrange(
                                   "p (c m) -> p c m", c=8),
                               in_=g.Win[li].rearrange(
                                   "(c p) m -> p c m", p=128)
                               [:, :, half * 1024:half * 1024 + 1024]
                               ).then_inc(ds, 16), 1,
                           deps=(prev + getattr(g, "wbuf_free", []))
                           if half == 0 else win_evs)
            dwis = wscale(g, g.wbuf[:, 0:8192], 8, 1024,
                          PA_MNW + li * 8, [dwi])
            zslot_war = {0: [], 1: []}
            for m in range(8):
                for k in range(8):
                    for ts in range(4):
                        bank = ts
                        dd = []
                        if k == 0:
                            dd = [dwis] + xs + pk.dep(bank)
                        dmm = bld.g("TEN", lambda e, m=m, ts=ts, k=k,
                                    bank=bank: e.matmul(
                                        pk.t[bank][:, 0:512],
                                        g.wbuf[:, k * 1024 + m * 128:
                                               k * 1024 + m * 128 + 128],
                                        g.xb[:, k * T + ts * 512:
                                             k * T + ts * 512 + 512],
                                        start=(k == 0), stop=(k == 7)),
                                    deps=dd)
                        if k == 7:
                            win_evs.append(dmm)
                            if half == 0:
                                dev = bld.g("ACT", lambda e, m=m, ts=ts,
                                            bank=bank: e.activation(
                                                g.bufB[:, m * T + ts * 512:
                                                       m * T + ts * 512
                                                       + 512],
                                                pk.t[bank][:, 0:512],
                                                AF.Copy), deps=[dmm])
                                pk.consumed(bank, dev)
                                xz1_evs.append(dev)
                            else:
                                zs = (m * 4 + ts) % 2
                                dev = bld.g("ACT", lambda e, zs=zs,
                                            bank=bank: e.activation(
                                                g.bufC[:, ZST + zs * 512:
                                                       ZST + zs * 512
                                                       + 512],
                                                pk.t[bank][:, 0:512],
                                                AF.Copy),
                                            deps=[dmm] + zslot_war[zs])
                                pk.consumed(bank, dev)
                                dzs = bld.dmas(
                                    "SP", lambda e, ds, b=b, m=m, ts=ts,
                                    zs=zs: e.dma_start(
                                        out=g.z_dram[
                                            b, m * 128:(m + 1) * 128,
                                            ts * 512:(ts + 1) * 512],
                                        in_=g.bufC[:, ZST + zs * 512:
                                                   ZST + zs * 512 + 512]
                                        ).then_inc(ds, 16), 1, deps=[dev])
                                zslot_war[zs] = [dzs]
                                tailz.append(dzs)
        xc_evs = []
        for jd in range(8):
            def conv(e, jd=jd, li=li):
                cwc = PA_CONVW + (li * 8 + jd) * 4
                e.tensor_scalar(g.xb[:, jd * T:(jd + 1) * T],
                                g.bufB[:, jd * T:(jd + 1) * T],
                                P[:, cwc + 3:cwc + 4], None, ALU.mult)
                last = None
                for sh in (1, 2, 3):
                    last = e.scalar_tensor_tensor(
                        g.xb[:, jd * T + sh:(jd + 1) * T],
                        g.bufB[:, jd * T:(jd + 1) * T - sh],
                        P[:, cwc + 3 - sh:cwc + 4 - sh],
                        g.xb[:, jd * T + sh:(jd + 1) * T],
                        ALU.mult, ALU.add)
                return last
            dcv = bld.g("VEC", conv, deps=xz1_evs + win_evs + [g.d_convw])
            dsl = bld.g("ACT", lambda e, jd=jd, li=li: e.activation(
                g.xb[:, jd * T:(jd + 1) * T], g.xb[:, jd * T:(jd + 1) * T],
                AF.Silu, bias=P[:, PA_CONVB + li * 8 + jd:
                                PA_CONVB + li * 8 + jd + 1]), deps=[dcv])
            xc_evs.append(dsl)
        proj_evs = []
        proj_mms = []
        pstage_war = {0: list(tailz), 1: list(tailz)}
        statr_war = {0: [], 1: []}
        for half in range(2):
            dwx = _wx_load(g, li, half,
                           deps=(xz1_evs + win_evs) if half == 0
                           else proj_mms)
            mlist = range(0, 4) if half == 0 else range(4, 9)
            w = 512 if half == 0 else 544
            for m in mlist:
                mo = (m - (0 if half == 0 else 4)) * 128
                mw = 128 if m < 8 else 32
                for k in range(8):
                    for ts in range(4):
                        bank = ts
                        dd = []
                        if k == 0:
                            dd = [dwx] + xc_evs + pk.dep(bank)
                        dmm = bld.g("TEN", lambda e, mo=mo, mw=mw, ts=ts,
                                    k=k, bank=bank, w=w: e.matmul(
                                        pk.t[bank][0:mw, 0:512],
                                        g.wbuf[:, k * w + mo:
                                               k * w + mo + mw],
                                        g.xb[:, k * T + ts * 512:
                                             k * T + ts * 512 + 512],
                                        start=(k == 0), stop=(k == 7)),
                                    deps=dd)
                        if k < 7:
                            continue
                        proj_mms.append(dmm)
                        if m < 8:
                            ps = ts % 2
                            sp0 = ps * 512
                            da1 = bld.g("ACT", lambda e, sp0=sp0, bank=bank:
                                        e.activation(
                                            g.statr[:, sp0:sp0 + 512],
                                            pk.t[bank][:, 0:512], AF.Abs),
                                        deps=[dmm] + statr_war[ps])
                            da2 = bld.g("ACT", lambda e, sp0=sp0:
                                        e.activation(
                                            g.statr[:, sp0:sp0 + 512],
                                            g.statr[:, sp0:sp0 + 512],
                                            AF.Exp, scale=-1.0), deps=[da1])
                            da3 = bld.g("ACT", lambda e, sp0=sp0:
                                        e.activation(
                                            g.statr[:, sp0:sp0 + 512],
                                            g.statr[:, sp0:sp0 + 512],
                                            AF.Ln, bias=1.0), deps=[da2])
                            dev = bld.g("VEC", lambda e, ps=ps, sp0=sp0,
                                        bank=bank: e.scalar_tensor_tensor(
                                            g.bufC[:, ZST + ps * 512:
                                                   ZST + ps * 512 + 512],
                                            pk.t[bank][:, 0:512], 0.0,
                                            g.statr[:, sp0:sp0 + 512],
                                            ALU.max, ALU.add),
                                        deps=[da3] + pstage_war[ps])
                            pk.consumed(bank, dev)
                            statr_war[ps] = [dev]
                            dst = bld.dmas(
                                "SP", lambda e, ds, m=m, ts=ts, ps=ps:
                                e.dma_start(
                                    out=g.proj_dram[m * 128:(m + 1) * 128,
                                                    ts * 512:(ts + 1)
                                                    * 512],
                                    in_=g.bufC[:, ZST + ps * 512:
                                               ZST + ps * 512 + 512]
                                    ).then_inc(ds, 16), 1, deps=[dev])
                            pstage_war[ps] = [dst]
                            proj_evs.append(dst)
                        else:
                            dev = bld.g("ACT", lambda e, ts=ts, bank=bank:
                                        e.activation(
                                            g.bufC[0:32,
                                                   BCR + ts * 512:
                                                   BCR + ts * 512 + 512],
                                            pk.t[bank][0:32, 0:512],
                                            AF.Copy), deps=[dmm])
                            pk.consumed(bank, dev)
                            proj_evs.append(dev)
        def scaninit(e):
            return e.memset(g.bufC[:, CARR:CARR + 128], 0.0)
        dsi = bld.g("VEC", scaninit, deps=xs + tailz + xz1_evs
                    + [d for d in pstage_war[0] + pstage_war[1]])

        NIT = NSCH * 8
        free_a = {0: [dsi], 1: [dsi]}
        free_u = {0: [dsi], 1: [dsi]}
        free_h = {0: [dsi], 1: [dsi]}
        wv_free = {0: [], 1: []}
        tm_war = {0: [], 1: []}
        dsrc_free = {0: [], 1: [], 2: [], 3: []}
        bc_evs = {0: [], 1: []}
        bc_readers = {0: [], 1: []}
        dsrc_ld = {}
        y_evs = []

        def emit_dsrc(i):
            sc, jd = divmod(i, 8)
            sl = i % 4
            dsrc_ld[i] = bld.dmas(
                "SP", lambda e, ds, jd=jd, sc=sc, sl=sl: e.dma_start(
                    out=g.bufC[:, DSRC + sl * 256:DSRC + sl * 256 + 256],
                    in_=g.proj_dram[jd * 128:(jd + 1) * 128,
                                    sc * 256:sc * 256 + 256]
                    ).then_inc(ds, 16), 1,
                deps=proj_evs + dsrc_free[sl])

        bc_rd_snap = {0: [], 1: []}
        bc_accum = {0: [], 1: []}

        def emit_bcast_part(sc, part):
            sl = sc % 2
            if part == 0:
                bc_rd_snap[sl] = bc_readers[sl]
                bc_readers[sl] = []
                bc_accum[sl] = []
            for (rbase, base, bank_) in ((0, (B0, B1)[sl], 0),
                                         (16, (C0, C1)[sl], 1)):
                dmms = []
                for k in range(2):
                    n = 2 * part + k
                    dd = proj_evs + [g.d_masks, dsi] + bc_rd_snap[sl] \
                        + pk.dep(bank_)
                    dmms.append(bld.g("TEN", lambda e, srow=rbase + n,
                                      sc=sc, k=k, bank_=bank_: e.matmul(
                                          pk.t[bank_][:, k * 256:
                                                      k * 256 + 256],
                                          C[0:32, ID_B + srow:
                                            ID_B + srow + 1]
                                          .to_broadcast([32, 128]),
                                          g.bufC[0:32, BCR + sc * 256:
                                                 BCR + sc * 256 + 256],
                                          start=True, stop=True), deps=dd))
                dev = bld.g("ACT", lambda e, part=part, base=base,
                            bank_=bank_: e.activation(
                                g.bufC[:, base + 2 * part * RL:
                                       base + 2 * part * RL + 512],
                                pk.t[bank_][:, 0:512], AF.Copy),
                            deps=dmms)
                pk.consumed(bank_, dev)
                bc_accum[sl].append(dev)
            if part == 7:
                bc_evs[sl] = list(bc_accum[sl])

        for _j in range(min(4, NIT)):
            emit_dsrc(_j)
        for _p in range(8):
            emit_bcast_part(0, _p)

        st1 = {}
        def front(i):
            sc, jd = divmod(i, 8)
            sl = i % 2
            par = sl * PAR1
            if sc + 1 < NSCH:
                emit_bcast_part(sc + 1, jd)
            if i + 4 < NIT:
                emit_dsrc(i + 4)
            sl4 = i % 4
            dlt = g.bufC[:, DSRC + sl4 * 256:DSRC + sl4 * 256 + 256]
            wv = g.bufC[:, WVS + sl * 256:WVS + sl * 256 + 256]
            tm = g.bufC[:, TM16 + sl * 16:TM16 + sl * 16 + 16]
            dw = bld.g("VEC", lambda e, jd=jd, sc=sc, wv=wv, dlt=dlt:
                       e.tensor_tensor(
                           wv, dlt,
                           g.xb[:, jd * T + sc * 256:jd * T + sc * 256
                                + 256], ALU.mult),
                       deps=[dsrc_ld[i]] + xc_evs + wv_free[sl])
            du = bld.g("VEC", lambda e, par=par, sc=sc, wv=wv:
                       e.tensor_tensor(
                           g.bufA[:, par + U0:par + U0 + NRUN]
                           .rearrange("p (n c) -> p n c", n=16),
                           wv.rearrange("p (o c) -> p o c", o=1)
                           .to_broadcast([128, 16, 256]),
                           g.bufC[:, (B0, B1)[sc % 2]:
                                  (B0, B1)[sc % 2] + NRUN]
                           .rearrange("p (n c) -> p n c", n=16),
                           ALU.mult),
                       deps=[dw] + bc_evs[sc % 2] + free_u[sl])
            dua = du
            bc_readers[sc % 2].append(du)
            dex = None
            exd = [dsrc_ld[i]] + [g.d_aneg] + free_a[sl]
            for n in range(NS):
                dex = bld.g("ACT", lambda e, li=li, jd=jd, n=n, par=par,
                            dlt=dlt: e.activation(
                                g.bufA[:, par + A0 + n * RL:
                                       par + A0 + n * RL + RL],
                                dlt, AF.Exp,
                                scale=P[:, PA_ANEG + (li * 8 + jd) * 16
                                        + n:PA_ANEG + (li * 8 + jd) * 16
                                        + n + 1]),
                            deps=exd if n == 0 else [])
            st1[i] = (sl, par, dex, [du, dua], dw)
            wv_free[sl] = [du, dua]
            dsrc_free[sl4] = [dw, dex]

        def carry_inject(i):
            sc, jd = divmod(i, 8)
            sl = i % 2
            par = sl * PAR1
            dex, du = st1[i][2], st1[i][3]
            tm = g.bufC[:, TM16 + sl * 16:TM16 + sl * 16 + 16]
            dc1 = bld.g("VEC", lambda e, par=par, jd=jd, tm=tm:
                        e.tensor_tensor(
                            tm,
                            bass.AP(g.bufA, par + A0,
                                    [[24832, 128], [RL, 16]]),
                            g.bufC[:, CARR + jd * 16:CARR + jd * 16 + 16],
                            ALU.mult),
                        deps=[dex] + du + carry_w[jd] + tm_war[sl])
            dc2 = bld.g("VEC", lambda e, par=par, tm=tm: e.tensor_tensor(
                bass.AP(g.bufA, par + U0, [[24832, 128], [RL, 16]]),
                bass.AP(g.bufA, par + U0, [[24832, 128], [RL, 16]]),
                tm, ALU.add), deps=[dc1] + du)
            dc3 = bld.g("VEC", lambda e, par=par: e.memset(
                bass.AP(g.bufA, par + A0, [[24832, 128], [RL, 16]]), 0.0),
                deps=[dc1])
            tm_war[sl] = [dc2]
            st2[i] = (dc2, dc3)

        def back_scan(i):
            sl, par = st1[i][0], st1[i][1]
            dc2, dc3 = st2[i]
            dscan = bld.g("VEC", lambda e, par=par: e.tensor_tensor_scan(
                g.bufA[:, par + H0:par + H0 + NRUN],
                g.bufA[:, par + A0:par + A0 + NRUN],
                g.bufA[:, par + U0:par + U0 + NRUN], 0.0,
                ALU.mult, ALU.add),
                deps=[dc2, dc3] + free_h[sl])
            st2[i] = dscan

        def back(i):
            sc, jd = divmod(i, 8)
            sl, par = st1[i][0], st1[i][1]
            dscan = st2[i]
            dco = bld.g("POOL", lambda e, jd=jd, par=par: e.tensor_copy(
                g.bufC[:, CARR + jd * 16:CARR + jd * 16 + 16],
                bass.AP(g.bufA, par + H0 + RL - 1,
                        [[24832, 128], [RL, 16]])),
                deps=[dscan])
            carry_w[jd] = [dco]
            dhc = bld.g("VEC", lambda e, par=par, sc=sc: e.tensor_tensor(
                g.bufA[:, par + U0:par + U0 + NRUN],
                g.bufA[:, par + H0:par + H0 + NRUN],
                g.bufC[:, (C0, C1)[sc % 2]:(C0, C1)[sc % 2] + NRUN],
                ALU.mult), deps=[dscan] + bc_evs[sc % 2])
            bc_readers[sc % 2].append(dhc)
            free_h[sl] = [dhc, dco]
            dl = dhc
            w = NRUN
            for _lv in range(4):
                w //= 2
                dl = bld.g("VEC", lambda e, par=par, w=w: e.tensor_add(
                    g.bufA[:, par + U0:par + U0 + w],
                    g.bufA[:, par + U0:par + U0 + w],
                    g.bufA[:, par + U0 + w:par + U0 + 2 * w]), deps=[dl])
            dy = bld.g("VEC", lambda e, jd=jd, sc=sc, li=li, par=par:
                       e.scalar_tensor_tensor(
                           g.bufB[:, jd * T + sc * 256:jd * T + sc * 256
                                  + 256],
                           g.xb[:, jd * T + sc * 256:jd * T + sc * 256
                                + 256],
                           P[:, PA_DP + li * 8 + jd:
                             PA_DP + li * 8 + jd + 1],
                           g.bufA[:, par + U0:par + U0 + RL],
                           ALU.mult, ALU.add), deps=[dl, g.d_params[4]])
            y_evs.append(dy)
            free_a[sl] = [dscan]
            free_u[sl] = [dy]
            return dl, dy, dhc

        carry_w = {jd: [dsi] for jd in range(8)}
        st2 = {}
        front(0)
        carry_inject(0)
        for i in range(NIT):
            if i + 1 < NIT:
                front(i + 1)
            back_scan(i)
            dl4, dy, dhc = back(i)
            if i + 1 < NIT:
                carry_inject(i + 1)
        tree_war = [y_evs[-1]]
        dwo = bld.dmas("POOL", lambda e, ds, li=li: e.dma_start(
            out=g.wbuf[:, 0:8192].rearrange("p (c m) -> p c m", c=8),
            in_=g.Wout[li].rearrange("(c p) m -> p c m", p=128)
            ).then_inc(ds, 16), 1, deps=proj_mms)
        yz_evs = []
        for jd in range(8):
            dzl = bld.dmas("SP", lambda e, ds, b=b, jd=jd: e.dma_start(
                out=g.bufC[:, (jd % 4) * 2048:(jd % 4) * 2048 + 2048],
                in_=g.z_dram[b, jd * 128:(jd + 1) * 128, :]
                ).then_inc(ds, 16), 1,
                deps=y_evs + [dl4, dy, dhc]
                + ([] if jd < 4 else [yz_evs[jd - 4]]))
            dzsl = bld.g("ACT", lambda e, jd=jd: e.activation(
                g.bufC[:, (jd % 4) * 2048:(jd % 4) * 2048 + 2048],
                g.bufC[:, (jd % 4) * 2048:(jd % 4) * 2048 + 2048],
                AF.Silu), deps=[dzl])
            dyz = bld.g("VEC", lambda e, jd=jd: e.tensor_tensor(
                g.bufB[:, jd * T:(jd + 1) * T],
                g.bufB[:, jd * T:(jd + 1) * T],
                g.bufC[:, (jd % 4) * 2048:(jd % 4) * 2048 + 2048],
                ALU.mult), deps=[dzsl] + y_evs)
            yz_evs.append(dyz)
        tail = []
        slot = 0
        for m in range(8):
            for k in range(8):
                for ts in range(4):
                    bank = ts
                    dd = []
                    if k == 0:
                        dd = [dwo] + yz_evs + pk.dep(bank)
                    dmm = bld.g("TEN", lambda e, m=m, ts=ts, k=k, bank=bank:
                                e.matmul(
                                    pk.t[bank][:, 0:512],
                                    g.wbuf[:, k * 1024 + m * 128:
                                           k * 1024 + m * 128 + 128],
                                    g.bufB[:, k * T + ts * 512:
                                           k * T + ts * 512 + 512],
                                    start=(k == 0), stop=(k == 7)),
                                deps=dd)
                    if k == 7:
                        dso, dst = res_evac(g, b, m, ts, bank, [], [dmm],
                                            slot)
                        slot = (slot + 1) % 4
                        tail.append(dso)
        prev = tail
        g.wbuf_free = tail
    g.mamba_tail = prev





def phaseT(g):
    bld, pk = g.bld, g.pk
    prev = g.mamba_tail
    g.sfin_deps = []
    for b in range(B):
        for tt in range(16):
            dl = bld.dmas("SP", lambda e, ds, b=b, tt=tt: e.dma_start(
                out=g.xtmp.rearrange("p (c t) -> p c t", c=8),
                in_=g.xs_dram[b].rearrange("(c p) t -> p c t", p=128)
                [:, :, tt * 128:(tt + 1) * 128]).then_inc(ds, 16), 1,
                deps=prev)
            evd = []
            for j in range(8):
                bank = j % 2
                dtr = bld.g("TEN", lambda e, j=j, bank=bank: e.transpose(
                    pk.t[bank][:, 0:128], g.xtmp[:, j * 128:(j + 1) * 128],
                    g.idf[:, :]), deps=[dl] + pk.dep(bank))
                dev = bld.g("VEC", lambda e, j=j, bank=bank: e.tensor_copy(
                    g.rowsc[:, j * 128:(j + 1) * 128],
                    pk.t[bank][:, 0:128]), deps=[dtr] + prev)
                pk.consumed(bank, dev)
                evd.append(dev)
            dst = bld.dmas("SP", lambda e, ds, b=b, tt=tt: e.dma_start(
                out=g.sfin_tm[b, tt * 128:(tt + 1) * 128, :],
                in_=g.rowsc[:, :]).then_inc(ds, 16), 1, deps=evd)
            prev = [dst] + evd
            g.sfin_deps.append(dst)


def moe_experts(g):
    bld, pk, C, P = g.bld, g.pk, g.consts, g.params
    DSPF, EOF_, EOTM, DSTM = 0, 5120, 10240, 15360
    H1 = 0
    prev = g.sfin_deps + g.d_scatter
    fc1_war, fc2_war = [], []
    bias_war = []
    for ex in range(EM):
        dbl = bld.dmas("SP", lambda e, ds, ex=ex: (
            e.dma_start(out=g.rowsc[0:32, 0:128],
                        in_=g.fc1b[ex].rearrange("(r p) -> r p", p=128)
                        ).then_inc(ds, 16),
            e.dma_start(out=g.rowsc[32:40, 0:128],
                        in_=g.fc2b[ex].rearrange("(r p) -> r p", p=128)
                        ).then_inc(ds, 16))[-1], 2,
            deps=prev + bias_war)
        dbt = bld.g("TEN", lambda e: e.transpose(
            pk.t[4][:, 0:64], g.rowsc[0:64, 0:128], g.idf[0:64, 0:64]),
            deps=[dbl] + pk.dep(4))
        dbe = bld.g("VEC", lambda e: e.tensor_copy(
            g.xtmp[:, 0:64], pk.t[4][:, 0:64]), deps=[dbt])
        pk.consumed(4, dbe)
        bias_war = [dbt, dbe]

        ddl = bld.dmas("SP", lambda e, ds, ex=ex: [
            e.dma_start(out=g.bufA[:, DSTM + s * 1024:DSTM + (s + 1) * 1024],
                        in_=g.disp_dram[ex * CLOC + s * 128:
                                        ex * CLOC + (s + 1) * 128, :]
                        ).then_inc(ds, 16) for s in range(5)][-1], 5,
            deps=prev + g.d_scatter)
        df_evs = []
        for s in range(5):
            for j in range(8):
                bank = 6 + ((s * 8 + j) % 2)
                dtr = bld.g("TEN", lambda e, s=s, j=j, bank=bank:
                            e.transpose(
                                pk.t[bank][:, 0:128],
                                g.bufA[:, DSTM + s * 1024 + j * 128:
                                       DSTM + s * 1024 + j * 128 + 128],
                                C[:, ID_B:ID_B + 128]),
                            deps=[ddl, g.d_masks] + pk.dep(bank))
                dev = bld.g("VEC", lambda e, s=s, j=j, bank=bank:
                            e.tensor_copy(
                                g.bufA[:, DSPF + j * 640 + s * 128:
                                       DSPF + j * 640 + s * 128 + 128],
                                pk.t[bank][:, 0:128]), deps=[dtr])
                pk.consumed(bank, dev)
                df_evs.append(dev)

        h1_evs = []
        for half in range(2):
            dw1 = bld.dmas("POOL", lambda e, ds, ex=ex, half=half:
                           e.dma_start(
                               out=g.xb.rearrange("p (c m) -> p c m", c=8),
                               in_=g.fc1T[ex].rearrange(
                                   "(c p) m -> p c m", p=128)
                               [:, :, half * 2048:half * 2048 + 2048]
                               ).then_inc(ds, 16), 1,
                           deps=(prev + fc1_war) if (ex == 0 and half == 0)
                           else fc1_war)
            myw = []
            for fb in range(16):
                fg = half * 16 + fb
                for ss in range(2):
                    bank = (fb * 2 + ss) % 2
                    dmm = None
                    for k in range(8):
                        dd = ([dw1] + df_evs + pk.dep(bank)) if k == 0 \
                            else []
                        dmm = bld.g("TEN", lambda e, fb=fb, ss=ss, k=k,
                                    bank=bank: e.matmul(
                                        pk.t[bank][:, 0:320],
                                        g.xb[:, k * 2048 + fb * 128:
                                             k * 2048 + fb * 128 + 128],
                                        g.bufA[:, DSPF + k * 640 + ss * 320:
                                               DSPF + k * 640 + ss * 320
                                               + 320],
                                        start=(k == 0), stop=(k == 7)),
                                    deps=dd)
                    myw.append(dmm)
                    dge = bld.g("ACT", lambda e, fg=fg, ss=ss, bank=bank:
                                e.activation(
                                    g.bufC[:, H1 + fg * 640 + ss * 320:
                                           H1 + fg * 640 + ss * 320 + 320],
                                    pk.t[bank][:, 0:320], AF.Gelu,
                                    bias=g.xtmp[:, fg:fg + 1]),
                                deps=[dmm, dbe])
                    pk.consumed(bank, dge)
                    h1_evs.append(dge)
            fc1_war = myw
        eo_evs = []
        for mh in range(2):
            dw2 = bld.dmas("POOL", lambda e, ds, ex=ex, mh=mh: e.dma_start(
                out=g.bufB.rearrange("p (c m) -> p c m", c=32),
                in_=g.fc2T[ex].rearrange("(c p) m -> p c m", p=128)
                [:, :, mh * 512:mh * 512 + 512]).then_inc(ds, 16), 1,
                deps=fc2_war if (ex or mh) else prev)
            myw = []
            for mi in range(4):
                m = mh * 4 + mi
                for ss in range(2):
                    bank = 2 + ((mi * 2 + ss) % 2)
                    dmm = None
                    for k in range(32):
                        dd = ([dw2] + h1_evs + pk.dep(bank)) if k == 0 \
                            else []
                        dmm = bld.g("TEN", lambda e, mi=mi, ss=ss, k=k,
                                    bank=bank: e.matmul(
                                        pk.t[bank][:, 0:320],
                                        g.bufB[:, k * 512 + mi * 128:
                                               k * 512 + mi * 128 + 128],
                                        g.bufC[:, H1 + k * 640 + ss * 320:
                                               H1 + k * 640 + ss * 320
                                               + 320],
                                        start=(k == 0), stop=(k == 31)),
                                    deps=dd)
                    myw.append(dmm)
                    dev = bld.g("VEC", lambda e, ex=ex, m=m, ss=ss,
                                bank=bank: e.tensor_scalar(
                                    g.bufA[:, EOF_ + m * 640 + ss * 320:
                                           EOF_ + m * 640 + ss * 320 + 320],
                                    pk.t[bank][:, 0:320],
                                    g.xtmp[:, 32 + m:33 + m], None,
                                    ALU.add), deps=[dmm, dbe])
                    pk.consumed(bank, dev)
                    eo_evs.append(dev)
            fc2_war = myw
        st_evs = []
        for s in range(5):
            tev = []
            for j in range(8):
                bank = 6 + ((s * 8 + j) % 2)
                dtr = bld.g("TEN", lambda e, s=s, j=j, bank=bank:
                            e.transpose(
                                pk.t[bank][:, 0:128],
                                g.bufA[:, EOF_ + j * 640 + s * 128:
                                       EOF_ + j * 640 + s * 128 + 128],
                                C[:, ID_B:ID_B + 128]),
                            deps=eo_evs + pk.dep(bank))
                dev = bld.g("VEC", lambda e, s=s, j=j, bank=bank:
                            e.tensor_copy(
                                g.bufA[:, EOTM + s * 1024 + j * 128:
                                       EOTM + s * 1024 + j * 128 + 128],
                                pk.t[bank][:, 0:128]), deps=[dtr])
                pk.consumed(bank, dev)
                tev.append(dev)
            dst = bld.dmas("SP", lambda e, ds, ex=ex, s=s: e.dma_start(
                out=g.rows_dram[ex * CLOC + s * 128:
                                ex * CLOC + (s + 1) * 128, :],
                in_=g.bufA[:, EOTM + s * 1024:EOTM + (s + 1) * 1024]
                ).then_inc(ds, 16), 1, deps=tev)
            st_evs.append(dst)
        prev = st_evs
    g.moe_tail = prev


def combine(g):
    bld = g.bld
    prev = g.moe_tail
    g1w, g2w = [], []
    for b in range(B):
        for tt in range(16):
            ti = b * 16 + tt
            dlx = bld.dmas("SP", lambda e, ds, b=b, tt=tt: (
                e.dma_start(out=g.rowsc[:, :],
                            in_=g.x_in[b, tt * 128:(tt + 1) * 128, :]
                            ).then_inc(ds, 16),
                e.dma_start(out=g.xtmp[:, :],
                            in_=g.sfin_tm[b, tt * 128:(tt + 1) * 128, :]
                            ).then_inc(ds, 16))[-1], 2,
                deps=prev + g1w)
            dg = bld.dmas("POOL", lambda e, ds, ti=ti: (
                e.indirect_dma_start(
                    out=g.bufA[:, 0:1024], out_offset=None,
                    in_=g.rows_dram,
                    in_offset=bass.IndirectOffsetOnAxis(
                        g.metai[:, ti * 2:ti * 2 + 1], 0)
                    ).then_inc(ds, 16),
                e.indirect_dma_start(
                    out=g.bufA[:, 1024:2048], out_offset=None,
                    in_=g.rows_dram,
                    in_offset=bass.IndirectOffsetOnAxis(
                        g.metai[:, ti * 2 + 1:ti * 2 + 2], 0)
                    ).then_inc(ds, 16))[-1], 2,
                deps=prev + g2w)
            dc = bld.g("VEC", lambda e, ti=ti: (
                e.tensor_add(g.rowsc[:, :], g.rowsc[:, :], g.xtmp[:, :]),
                e.scalar_tensor_tensor(
                    g.rowsc[:, :], g.bufA[:, 0:1024],
                    g.meta[:, 1024 + ti * 2:1024 + ti * 2 + 1],
                    g.rowsc[:, :], ALU.mult, ALU.add),
                e.scalar_tensor_tensor(
                    g.rowsc[:, :], g.bufA[:, 1024:2048],
                    g.meta[:, 1024 + ti * 2 + 1:1024 + ti * 2 + 2],
                    g.rowsc[:, :], ALU.mult, ALU.add))[-1],
                deps=[dlx, dg])
            dso = bld.dmas("SP", lambda e, ds, b=b, tt=tt: e.dma_start(
                out=g.out[b, tt * 128:(tt + 1) * 128, :],
                in_=g.rowsc[:, :]).then_inc(ds, 16), 1, deps=[dc])
            g1w = [dc, dso]
            g2w = [dc]
            prev = []
